# version 13
# baseline (speedup 1.0000x reference)
import contextlib
import numpy as np
import ml_dtypes
import concourse.bass as bass
import concourse.mybir as mybir
from concourse.bass_utils import run_bass_kernel_spmd

F32 = mybir.dt.float32
BF16 = mybir.dt.bfloat16
FP8 = mybir.dt.float8e4
ALU = mybir.AluOpType
AX = mybir.AxisListType
AF = mybir.ActivationFunctionType

D = 1024
HDIM = 4096
NT = 8
MMW = 258
FW = 256
NC = 32
NDC = 8
MAGIC = float(np.float32(1.5 * 2 ** 23))
EPS = 1e-5
S_TOK = 2049
DENOM = float(np.float32(2 * 16 * 8192 * 64))

DO_MINI = True
DO_ATTN = True

ENGS = ("t", "v", "s", "g", "y")


class Rec:
    def __init__(self):
        self.st = {e: [] for e in ENGS}
        self.cnt = {}
        self.lastw = {e: {} for e in ENGS}

    def do(self, eng, fn, waits, inc):
        flat = []
        for w in waits:
            if isinstance(w, list):
                flat.extend(w)
            else:
                flat.append(w)
        if eng in ("v", "s", "g"):
            flat.append((eng, self.cnt.get(eng, 0)))
            if inc is None:
                inc = (eng, 1)
        ws = []
        for sem, val in flat:
            if val > 0 and self.lastw[eng].get(sem, -1) < val:
                ws.append((sem, val))
                self.lastw[eng][sem] = val
        self.st[eng].append((fn, ws, inc))
        if inc is not None:
            sem, amt = inc
            self.cnt[sem] = self.cnt.get(sem, 0) + amt
            return self.cnt[sem]
        return None

    def n(self, sem):
        return self.cnt.get(sem, 0)


def build_program():
    nc = bass.Bass("TRN2", target_bir_lowering=False, debug=False)

    def din(name, shape, dt=F32):
        return nc.dram_tensor(name, shape, dt, kind="ExternalInput").ap()

    x_d = din("x_slice", [S_TOK, D])
    xl_d = din("x_left", [6, D])
    xr_d = din("x_right", [6, D])
    xh_d = din("x_head", [40, D])
    w1_d = din("w1", [HDIM, D])
    w2_d = din("w2", [D, HDIM])
    cwt_d = din("cwt", [128, 4 * NC])
    ob_d = din("ob", [D])
    idf_d = din("identf", [128, 128])
    idb_d = din("identb", [128, 128], BF16)
    ones_d = din("onesf", [1, 128])
    onesc_d = din("onescol", [128, 1])
    msk_d = din("maskv", [1, 8])
    aw_d = [din(n, [D, D]) for n in ("qw", "kw", "vw", "ow")]
    qbv_d = din("qbv", [128, 16])
    lbv_d = din("lbv", [128, 8])

    lab_d = din("lab", [128, 16])
    lw_d = [din(n, [64, 64]) for n in ("lqw", "lkw", "lvw")]

    out_d = nc.dram_tensor("out", [2048, D], F32, kind="ExternalOutput").ap()

    w1scr = nc.dram_tensor("w1scr", [HDIM, D], BF16, kind="Internal").ap()
    w2scr = nc.dram_tensor("w2scr", [D, HDIM], BF16, kind="Internal").ap()
    awscr = nc.dram_tensor("awscr", [4 * D, D], BF16, kind="Internal").ap()

    es = contextlib.ExitStack()
    sbt = lambda n, shp, dt=F32: es.enter_context(nc.sbuf_tensor(n, shp, dt))
    pst = lambda n, shp, dt=F32: es.enter_context(nc.psum_tensor(n, shp, dt))

    w1q = sbt("w1q", [128, NDC * HDIM], FP8)
    w2q = sbt("w2q", [128, NC * D], FP8)
    hbuf = sbt("hbuf", [128, NC * MMW])
    xqma = sbt("xqma", [128, NC * FW], BF16)
    xqmb = sbt("xqmb", [128, NC * FW], BF16)
    xA = [sbt(f"xA{r}", [128, D]) for r in range(3)]
    xB = [sbt(f"xB{r}", [128, D]) for r in range(3)]
    poolf = sbt("poolf", [128, D])
    dump = sbt("dump", [128, D])
    xq1 = sbt("xq1", [128, 2 * D], BF16)
    xq1e = sbt("xq1e", [128, D], BF16)
    xq1x = xq1e
    xq1T = sbt("xq1T", [128, NDC * MMW], BF16)
    tmpA = sbt("tmpA", [128, FW])
    tmpB = sbt("tmpB", [128, FW])
    absT = sbt("absT", [128, FW])
    Mt = sbt("Mt", [128, FW])
    bcast = sbt("bcast", [128, 352])
    obt = sbt("obt", [128, D])
    cwt = sbt("cwt_s", [128, 4 * NC])
    cols = sbt("cols", [128, 48])
    cols2 = sbt("cols2", [128, 24])
    colsh = sbt("colsh", [64, 48])
    idf = sbt("idf", [128, 128])
    idb = sbt("idb", [128, 128], BF16)
    onesf = sbt("ones_s", [1, 128])
    onescol = sbt("onescol_s", [128, 1])
    swc = sbt("swc", [128, 8])
    mskv = sbt("mskv", [1, 8])
    macc = sbt("macc", [128, 2])
    arow = sbt("arow", [1, 64])
    a2row = sbt("a2row", [1, MMW])
    wstg = sbt("wstg", [128, 2 * D], BF16)
    xh = sbt("xh", [64, D])
    scrA = sbt("scrA", [64, D])
    scrB = sbt("scrB", [64, D])
    xqh = sbt("xqh", [64, D], BF16)
    vib = xqmb[0:16, 3840:5888]
    at16 = xh
    xqhT = sbt("xqhT", [128, NDC * 40], BF16)
    xqmh_big = sbt("xqmh_big", [128, NC * 40 + 2 * 320], BF16)
    xqmh = xqmh_big[:, 0:NC * 40]
    fmb = xqmh_big[:, NC * 40:NC * 40 + 640]
    bdg = sbt("bdg", [128, 3 * 128], BF16)
    lbv = sbt("lbv_s", [128, 8])
    qbv = sbt("qbv_s", [128, 16])
    lab = sbt("lab_s", [128, 16])

    habc = sbt("habc", [64, 80])
    absh = sbt("absh", [128, 40])
    Mh = sbt("Mh", [128, 40])

    xqmTm = sbt("xqmTm", [128, NDC * 12], BF16)

    xqm_m = sbt("xqm_m", [128, NC * 2], BF16)
    PTb = None

    PTb = wstg[0:16, 0:512]
    xqmb_fv = xqmb[:, :].bitcast(F32)
    fmp = xqmb_fv[:, 0:1920]
    hh = xqmb_fv[:, 1920:3200]
    hm2 = xqmb_fv[:, 3200:3584]
    minq = xqh
    fmini = scrA
    minx = wstg[:, :].bitcast(F32)
    pm1 = [pst(f"pm1_{i}", [128, 512]) for i in range(3)]
    pu = pst("pu", [128, 1024], BF16)
    pm2 = [pst("pm2a", [128, 1024]), pst("pm2b", [128, 1024])]
    puf = pu[:, :].bitcast(F32)

    UF0 = 129
    SP0 = 385

    sems = {}
    DMA_K = {"dx": 4, "dw": 6, "da": 4, "do": 4, "dg": 4}
    dma_pool = {}
    dma_cnt = {}
    dma_rr = {}
    for cls, K in DMA_K.items():
        dma_pool[cls] = []
        for i in range(K):
            nm = f"{cls}{i}"
            sems[nm] = es.enter_context(nc.semaphore(nm))
            dma_pool[cls].append(nm)
            dma_cnt[nm] = 0
        dma_rr[cls] = 0
    for n in ("t", "v", "s", "g"):
        sems[n] = es.enter_context(nc.semaphore(n))

    R = Rec()
    ev = {}

    def E(eng, fn, w=(), inc=None, amt=1):
        return R.do(eng, fn, w, (inc, amt) if inc else None)

    def note(name, semname):
        ev[name] = (semname, R.n(semname))

    def wt(name):
        return ev[name]

    def gnote(name, parts):
        acc = []
        for p in parts:
            v = ev[p]
            if isinstance(v, list):
                acc.extend(v)
            else:
                acc.append(v)
        ev[name] = acc

    def dma(out, in_, w=(), sem_="dw", tp=False, eng="y", note_name=None):
        cls = sem_
        i = dma_rr[cls] % DMA_K[cls]
        dma_rr[cls] += 1
        nm = dma_pool[cls][i]
        ws = list(w)
        if dma_cnt[nm] > 0:
            ws.append((nm, 16 * dma_cnt[nm]))
        if tp:
            E(eng, lambda e, o=out, i2=in_: e.dma_start(out=o, in_=i2,
                                                        transpose=True), ws, nm, 16)
        else:
            E(eng, lambda e, o=out, i2=in_: e.dma_start(out=o, in_=i2), ws, nm, 16)
        dma_cnt[nm] += 1
        if note_name is not None:
            ev[note_name] = (nm, 16 * dma_cnt[nm])

    p64_user = [None]
    p0_user = [None]
    p32_user = [None]

    def emit_xload(i):
        r = i % 3
        w0 = 256 * i
        wl = [wt(f"xfree{i - 3}")] if i >= 3 else []
        if i >= 3:
            wl.append(wt(f"b{i - 2}_efill"))
        if i >= 4:
            wl.append(wt(f"b{i - 4}_xfill"))
        dma(xA[r][:, :], x_d[w0:w0 + 128, :], wl, "dx", note_name=f"xload{i}_a")
        dma(xB[r][:, :], x_d[w0 + 128:w0 + 256, :], wl, "dx", note_name=f"xload{i}_b")
        gnote(f"xload{i}", [f"xload{i}_a", f"xload{i}_b"])

    for i in range(3):
        emit_xload(i)
    dma(minx[0:6, :], xl_d[:, :], (), "dx", note_name="xs_l")
    dma(minx[32:38, :], xr_d[:, :], (), "dx", note_name="xs_r")
    dma(xh[0:40, :], xh_d[:, :], (), "dx", note_name="xs_h")
    gnote("xsmall", ["xs_l", "xs_r", "xs_h"])
    cparts = []
    for j, (t_, d_) in enumerate([(obt, None), (idf, idf_d), (idb, idb_d),
                                  (onesf, ones_d), (onescol, onesc_d),
                                  (mskv, msk_d), (cwt, cwt_d), (qbv, qbv_d),
                                  (lbv, lbv_d), (lab, lab_d)]):
        if d_ is None:
            dma(obt[:, :], ob_d.unsqueeze(0).partition_broadcast(128),
                note_name=f"cst{j}")
        else:
            dma(t_[:, :], d_[:, :], note_name=f"cst{j}")
        cparts.append(f"cst{j}")
    gnote("consts", cparts)
    E("v", lambda e: e.tensor_tensor(out=lbv[:, 5:6], in0=lbv[:, 3:4],
                                     in1=lbv[:, 4:5], op=ALU.mult),
      [wt("consts")], "v")
    note("gcol", "v")
    for kk, dst in ((2, 4), (0, 5), (1, 6)):
        E("t", lambda e, k=kk, d_=dst: e.matmul(
            puf[:, SP0 + d_:SP0 + d_ + 1], onesf[0:1, :], mskv[0:1, k:k + 1],
            start=True, stop=True), [wt("consts")], "t")
    note("mbc", "t")
    E("v", lambda e: e.tensor_copy(out=swc[:, 4:7], in_=puf[:, SP0 + 4:SP0 + 7]),
      [wt("mbc")], "v")
    note("mcols", "v")
    for t_ in (cols, cols2, colsh, habc, arow, a2row, Mh, Mt, absT, tmpA, tmpB):
        E("v", lambda e, t2=t_: e.memset(t2[:, :], 0.0), (), "v")
    note("zinit", "v")

    xqmb_f = xqmb[:, :].bitcast(F32)

    def emit_wquant(which):
        wd = w1_d if which == 0 else w2_d
        scr = w1scr if which == 0 else w2scr
        R_, C_ = (HDIM, D) if which == 0 else (D, HDIM)
        CH = D
        nch, nsub = R_ // 128, C_ // CH
        NKC = nch * nsub
        NS = 6 if which == 0 else 4
        if which == 0:
            stg = [hbuf[:, s * CH:(s + 1) * CH] for s in range(NS)]
        else:
            stg = [xqmb_f[:, s * CH:(s + 1) * CH] for s in range(NS)]
        bst = [xqma[:, s * CH:(s + 1) * CH] for s in range(4)]
        E("v", lambda e, w_=which: e.memset(macc[:, w_:w_ + 1], 0.0), (), "v")
        note(f"wm0_{which}", "v")
        for k in range(NKC):
            ci, si = divmod(k, nsub)
            sl = stg[k % NS]
            wl = [wt(f"wred{which}_{k - NS}")] if k >= NS else []
            dma(sl, wd[ci * 128:(ci + 1) * 128, si * CH:(si + 1) * CH], wl,
                "dw" if which == 0 else "dg",
                eng="y" if which == 0 else "g", note_name=f"wld{which}_{k}")
            E("v", lambda e, s=sl: e.reduce_sum(
                out=tmpA[:, 0:1], in_=s, axis=AX.X, apply_absolute_value=True),
              [wt(f"wld{which}_{k}"), wt(f"wm0_{which}")] +
              ([wt(f"wred{which}_{k - 1}")] if k else []), "v")
            note(f"wabs{which}_{k}", "v")
            E("v", lambda e, w_=which: e.tensor_tensor(
                out=macc[:, w_:w_ + 1], in0=macc[:, w_:w_ + 1],
                in1=tmpA[:, 0:1], op=ALU.add), [wt(f"wabs{which}_{k}")], "v")
            note(f"wred{which}_{k}", "v")
        E("t", lambda e, w_=which: e.matmul(
            puf[0:1, SP0 + 8 + w_:SP0 + 9 + w_], macc[:, w_:w_ + 1],
            onescol[:, :], start=True, stop=True),
          [wt(f"wred{which}_{NKC - 1}")], "t")
        note(f"wsum{which}", "t")
        E("v", lambda e, w_=which, rc=float(R_) * float(C_): e.tensor_scalar(
            out=arow[0:1, w_:w_ + 1], in0=puf[0:1, SP0 + 8 + w_:SP0 + 9 + w_],
            scalar1=1.0 / rc, scalar2=1e-5, op0=ALU.mult, op1=ALU.max),
          [wt(f"wsum{which}")], "v")
        note(f"wmean{which}", "v")
        E("v", lambda e, w_=which: e.reciprocal(
            out=arow[0:1, 4 + w_:5 + w_], in_=arow[0:1, w_:w_ + 1]),
          [wt(f"wmean{which}")], "v")
        note(f"wrec{which}", "v")
        E("t", lambda e, w_=which: e.matmul(
            puf[:, SP0 + 10:SP0 + 11], onesf[0:1, :], arow[0:1, 4 + w_:5 + w_],
            start=True, stop=True), [wt(f"wrec{which}")])
        E("t", lambda e, w_=which: e.matmul(
            puf[:, SP0 + 11:SP0 + 12], onesf[0:1, :], arow[0:1, w_:w_ + 1],
            start=True, stop=True), (), "t")
        note(f"wbc{which}", "t")
        E("v", lambda e, w_=which: e.tensor_copy(
            out=swc[:, w_:w_ + 1], in_=puf[:, SP0 + 10:SP0 + 11]), [wt(f"wbc{which}")])
        if which == 1:
            E("v", lambda e: e.tensor_scalar(
                out=swc[:, 2:3], in0=puf[:, SP0 + 11:SP0 + 12],
                scalar1=0.5 / 127.0, scalar2=None, op0=ALU.mult), (), "v")
        else:
            E("v", lambda e: e.tensor_copy(
                out=swc[:, 3:4], in_=puf[:, SP0 + 11:SP0 + 12]), (), "v")
        note(f"wsw{which}", "v")
        for k in range(NKC):
            ci, si = divmod(k, nsub)
            sl, b_ = stg[k % NS], bst[k % 4]
            wl = [wt(f"wq{which}_{k - NS}") if k >= NS else
                  wt(f"wred{which}_{NKC - 1}")]
            if k >= 4:
                wl.append(wt(f"wbst{which}_{k - 4}"))
            dma(sl, wd[ci * 128:(ci + 1) * 128, si * CH:(si + 1) * CH], wl,
                "dw" if which == 0 else "dg",
                eng="y" if which == 0 else "g", note_name=f"wld2{which}_{k}")
            E("v", lambda e, s=sl, w_=which: e.tensor_scalar(
                out=s, in0=s, scalar1=swc[:, w_:w_ + 1], scalar2=MAGIC,
                op0=ALU.mult, op1=ALU.add),
              [wt(f"wld2{which}_{k}"), wt(f"wsw{which}")], "v")
            note(f"wqa{which}_{k}", "v")
            E("v", lambda e, s=sl: e.tensor_scalar(
                out=s, in0=s, scalar1=MAGIC, scalar2=1.0,
                op0=ALU.subtract, op1=ALU.min), [wt(f"wqa{which}_{k}")], "v")
            note(f"wqb{which}_{k}", "v")
            E("v", lambda e, s=sl, b=b_: e.tensor_scalar(
                out=b, in0=s, scalar1=-1.0, scalar2=None, op0=ALU.max),
              [wt(f"wqb{which}_{k}")], "v")
            note(f"wq{which}_{k}", "v")
            dma(scr[ci * 128:(ci + 1) * 128, si * CH:(si + 1) * CH], b_,
                [wt(f"wq{which}_{k}")], "dw" if which == 0 else "dg",
                eng="y" if which == 0 else "g", note_name=f"wbst{which}_{k}")
        gnote(f"wscr{which}", [f"wbst{which}_{k}" for k in range(NKC)])
        tgt = w1q if which == 0 else w2q
        nslab = C_ // 128
        hb16 = hbuf[:, :].bitcast(BF16)
        if which == 0:
            ts_ = [hb16[:, s * R_:(s + 1) * R_] for s in range(4)]
        else:
            ts_ = [xqma[:, 4 * CH + s * R_:4 * CH + (s + 1) * R_]
                   for s in range(4)]
        for s_i in range(nslab):
            tstg = ts_[s_i % 4]
            wl = [wt(f"wscr{which}")]
            if which == 0:
                wl.append(wt("wld1_%d" % (4 * nsub)
                             if False else f"wred{which}_{NKC - 1}"))
            if s_i >= 4:
                wl.append(wt(f"wq8{which}_{s_i - 4}"))
            dma(tstg, scr[:, s_i * 128:(s_i + 1) * 128], wl,
                "dw" if s_i % 2 == 0 else "da", tp=True,
                note_name=f"wtp{which}_{s_i}")
            E("v", lambda e, t_=tstg, o=tgt[:, s_i * R_:(s_i + 1) * R_]:
              e.tensor_copy(out=o, in_=t_), [wt(f"wtp{which}_{s_i}")], "v")
            note(f"wq8{which}_{s_i}", "v")
        note(f"wdone{which}", "v")

    emit_wquant(0)

    if DO_ATTN:
        xqma_f = xqma[:, :].bitcast(F32)
        poolf_b = poolf[:, :].bitcast(BF16)
        for wi in range(4):
            for ci in range(8):
                k = wi * 8 + ci
                stg = xqma_f[:, (k % 2) * D:(k % 2) * D + D]
                bst = poolf_b[:, (k % 2) * D:(k % 2) * D + D]
                wl = [wt("wscr0")]
                if k >= 2:
                    wl.append(wt(f"awcast{k - 2}"))
                dma(stg, aw_d[wi][ci * 128:(ci + 1) * 128, :], wl, "da", note_name=f"awld{k}")
                E("v", lambda e, s=stg, b=bst: e.tensor_copy(out=b, in_=s),
                  [wt(f"awld{k}")] +
                  ([wt(f"awst{k - 2}")] if k >= 2 else [wt("wscr0")]), "v")
                note(f"awcast{k}", "v")
                dma(awscr[wi * D + ci * 128:wi * D + (ci + 1) * 128, :], bst,
                    [wt(f"awcast{k}")], "da", note_name=f"awst{k}")
        note("awq_done", "v")
        gnote("awscr_done", [f"awst{k}" for k in range(32)])
        gnote("awpf", ["awst30", "awst31"])
        p0_user[0] = "awpf"
        p32_user[0] = "awpf"
        p64_user[0] = "awpf"

    emit_wquant(1)

    CB = {"E": 0, "A": 12, "B": 24, "X": 36}

    def vch(fn, w=()):
        E("v", fn, list(w) + [("v", R.n("v"))], "v")

    def emit_qin(tag, sets, ct, srcdeps):
        srcdeps = list(srcdeps) + [wt("zinit")]
        prev_sq = None
        for (src, p0, n_, cb, dst, pstg, mcol) in sets:
            E("v", lambda e, s=src, c=ct[p0:p0 + n_, cb:cb + 1]:
              e.tensor_reduce(out=c, in_=s, axis=AX.X, op=ALU.add), srcdeps, "v")
            E("v", lambda e, s=src, c=ct[p0:p0 + n_, cb + 3:cb + 4]:
              e.tensor_reduce(out=c, in_=s, axis=AX.X, op=ALU.max), (), "v")
            E("v", lambda e, s=src, c=ct[p0:p0 + n_, cb + 4:cb + 5]:
              e.tensor_reduce(out=c, in_=s, axis=AX.X, op=ALU.min), (), "v")
            E("s", lambda e, s=src, c=ct[p0:p0 + n_, cb + 1:cb + 2], pg=pstg:
              e.activation(out=pg, in_=s, func=AF.Square, accum_out=c),
              list(srcdeps) + ([wt(prev_sq)] if prev_sq else []), "s")
            note(f"{tag}_sq_{cb}", "s")
            prev_sq = f"{tag}_sq_{cb}"
        note(f"{tag}_red", "v")
        note(f"{tag}_sq", "s")
        gdep = [wt(f"{tag}_red"), wt(f"{tag}_sq"), wt("zinit")]
        c = lambda k, _t=ct: _t[:, k::12] if _t.shape[0] == 128 else _t[0:_t.shape[0], k::12]
        cc = lambda k, _t=ct: _t[0:64, k::12] if _t is colsh else _t[:, k::12]
        c = cc
        vch(lambda e: e.tensor_scalar(out=c(0), in0=c(0), scalar1=1.0 / D,
                                      scalar2=None, op0=ALU.mult), gdep)
        vch(lambda e: e.tensor_scalar(out=c(1), in0=c(1), scalar1=1.0 / D,
                                      scalar2=None, op0=ALU.mult))
        vch(lambda e: e.tensor_tensor(out=c(11), in0=c(0), in1=c(0), op=ALU.mult))
        vch(lambda e: e.tensor_tensor(out=c(2), in0=c(1), in1=c(11), op=ALU.subtract))
        vch(lambda e: e.tensor_scalar(out=c(2), in0=c(2), scalar1=EPS,
                                      scalar2=None, op0=ALU.add))
        vch(lambda e: e.tensor_tensor(out=c(3), in0=c(3), in1=c(0), op=ALU.subtract))
        vch(lambda e: e.tensor_tensor(out=c(4), in0=c(0), in1=c(4), op=ALU.subtract))
        vch(lambda e: e.tensor_tensor(out=c(4), in0=c(3), in1=c(4), op=ALU.max))
        note(f"{tag}_pre", "v")
        E("s", lambda e: e.activation(out=c(5), in_=c(2), func=AF.Sqrt),
          [wt(f"{tag}_pre")], "s")
        note(f"{tag}_sqrt", "s")
        vch(lambda e: e.reciprocal(out=c(11), in_=c(5)), [wt(f"{tag}_sqrt")])
        for _ in range(2):
            vch(lambda e: e.tensor_tensor(out=c(5), in0=c(2), in1=c(11), op=ALU.mult))
            vch(lambda e: e.tensor_tensor(out=c(5), in0=c(5), in1=c(11), op=ALU.mult))
            vch(lambda e: e.tensor_scalar(out=c(5), in0=c(5), scalar1=-0.5,
                                          scalar2=1.5, op0=ALU.mult, op1=ALU.add))
            vch(lambda e: e.tensor_tensor(out=c(11), in0=c(11), in1=c(5), op=ALU.mult))
        vch(lambda e: e.tensor_tensor(out=c(6), in0=c(11), in1=c(4), op=ALU.mult))
        vch(lambda e: e.tensor_scalar(out=c(6), in0=c(6), scalar1=1e-5,
                                      scalar2=None, op0=ALU.max))
        vch(lambda e: e.tensor_scalar(out=c(9), in0=c(6), scalar1=1.0 / 127.0,
                                      scalar2=None, op0=ALU.mult))
        pn = 64 if ct is colsh else 128
        vch(lambda e, p_=pn: e.tensor_scalar(
            out=c(9), in0=c(9), scalar1=swc[0:p_, 3:4], scalar2=None,
            op0=ALU.mult))
        vch(lambda e: e.reciprocal(out=c(10), in_=c(6)))
        vch(lambda e: e.tensor_tensor(out=c(7), in0=c(11), in1=c(10), op=ALU.mult))
        vch(lambda e: e.tensor_scalar(out=c(7), in0=c(7), scalar1=127.0,
                                      scalar2=None, op0=ALU.mult))
        vch(lambda e: e.tensor_tensor(out=c(8), in0=c(0), in1=c(7), op=ALU.mult))
        vch(lambda e: e.tensor_scalar(out=c(8), in0=c(8), scalar1=-1.0,
                                      scalar2=None, op0=ALU.mult))
        note(f"{tag}_cols", "v")
        prev_rn = None
        for (src, p0, n_, cb, dst, pstg, mcol) in sets:
            E("s", lambda e, s=src, p=p0, n2=n_, b=cb, pg=pstg:
              e.activation(out=pg, in_=s, func=AF.Identity,
                           scale=ct[p:p + n2, b + 7:b + 8],
                           bias=ct[p:p + n2, b + 8:b + 9]),
              [wt(f"{tag}_cols")] + ([wt(prev_rn)] if prev_rn else []), "s")
            note(f"{tag}_p0_{cb}", "s")
            E("v", lambda e, d=dst, pg=pstg: e.tensor_scalar(
                out=d, in0=pg, scalar1=MAGIC, scalar2=MAGIC,
                op0=ALU.add, op1=ALU.subtract), [wt(f"{tag}_p0_{cb}")], "v")
            if mcol is not None:
                E("v", lambda e, d=dst, m=mcol: e.tensor_scalar(
                    out=d, in0=d, scalar1=m, scalar2=None, op0=ALU.mult), (), "v")
            note(f"{tag}_rn_{cb}", "v")
            prev_rn = f"{tag}_rn_{cb}"
        note(f"{tag}_done", "v")

    pu_read = [None]

    def pu_wait():
        return [wt(pu_read[0])] if pu_read[0] else []

    def pu_set(name):
        pu_read[0] = name

    def emit_bcast(tag, pieces, width, extradep=(), dst0=0):
        prev = None
        for (ct_, p0, n_, colk, dstc) in pieces:
            wl = list(extradep) + pu_wait()
            if prev is not None:
                wl.append(wt(prev))
            E("t", lambda e, _t=ct_, p=p0, n2=n_, k=colk: e.transpose(
                puf[0:1, UF0:UF0 + n2] if n2 == 1 else puf[0:n_, UF0:UF0 + n_]
                if False else puf[0:1, UF0:UF0 + n2],
                _t[p:p + n2, k:k + 1], idf[p:p + n2, p:p + n2]),
              wl, "t")
            note(f"{tag}_t_{dstc}", "t")
            E("s", lambda e, d=dstc, n2=n_: e.copy(
                out=a2row[0:1, d:d + n2], in_=puf[0:1, UF0:UF0 + n2]),
              [wt(f"{tag}_t_{dstc}")], "s")
            note(f"{tag}_c_{dstc}", "s")
            prev = f"{tag}_c_{dstc}"
        note(f"{tag}_row", "s")
        E("t", lambda e, w_=width: e.matmul(
            puf[:, UF0:UF0 + w_], onesf[0:1, :], a2row[0:1, 0:w_],
            start=True, stop=True), [wt(f"{tag}_row")], "t")
        note(f"{tag}_mm", "t")
        E("v", lambda e, w_=width, d0=dst0: e.tensor_copy(
            out=bcast[:, d0:d0 + w_], in_=puf[:, UF0:UF0 + w_]),
          [wt(f"{tag}_mm")], "v")
        note(f"{tag}_bc", "v")
        pu_set(f"{tag}_bc")

    prev_round = {}
    prev_ma = {"a": None, "b": None}
    pm2_read = [None, None]

    def hsl(c):
        return hbuf[:, c * MMW:(c + 1) * MMW]

    def ph_stage(st, i, do_round=True):
        tag = f"{st}{i}"
        xq1a = xq1[:, 0:D]
        xq1b = xq1[:, D:2 * D]
        er = 0 if st == "a" else 32
        cb2 = (i % 2) * 8 if st == "a" else 16
        xqm = xqma if st == "a" else xqmb
        if st == "a":
            wl = [wt(p0_user[0])] if p0_user[0] else []
            if i == 0:
                dma(poolf[0:1, :], xl_d[3:4, :], wl, "dx",
                    note_name=f"{tag}_efill")
            else:
                dma(poolf[0:1, :], x_d[256 * i - 1:256 * i, :], wl, "dx", note_name=f"{tag}_efill")
            p0_user[0] = f"{tag}_qdone"
            wl = [wt(p64_user[0])] if p64_user[0] else []
            dma(poolf[64:65, :], x_d[256 * i + 256:256 * i + 257, :], wl, "dx", note_name=f"{tag}_xfill")
            p64_user[0] = f"{tag}_qdone"
        else:
            if i == 0:
                wl = [wt(p32_user[0])] if p32_user[0] else []
                wl.append(wt("mini_done") if DO_MINI else wt(f"a0_res"))
                dma(poolf[32:33, :], xh[62:63, :], wl, "dg", eng="g",
                    note_name=f"{tag}_efill")
            p32_user[0] = f"{tag}_qdone"
            wl = [wt(p64_user[0])] if p64_user[0] else []
            if i == NT - 1:
                src = xh[63:64, :]
                wl.append(wt("mini_done") if DO_MINI else wt(f"a{i}_res"))
            else:
                src = xA[(i + 1) % 3][0:1, :]
                wl.append(wt(f"a{i + 1}_res"))
            dma(poolf[64:65, :], src, wl, "dg", eng="g",
                note_name=f"{tag}_xfill")
            p64_user[0] = f"{tag}_qdone"
        srcdeps = [wt(f"{tag}_efill"), wt(f"{tag}_xfill")]
        if st == "a":
            srcdeps.append(wt(f"xload{i}"))
        else:
            srcdeps.append(wt(f"a{i}_res"))
            if i == 0 and DO_ATTN:
                srcdeps.append(wt("attn_add"))
        emcol = swc[er:er + 1, 5:6] if (st == "b" and i == 0) else None
        xmcol = swc[64:65, 6:7] if (st == "b" and i == NT - 1) else None
        sets = [
            (poolf[er:er + 1, :], er, 1, CB["E"], xq1e[er:er + 1, :],
             dump[er:er + 1, :], emcol),
            (xA[i % 3][:, :], 0, 128, CB["A"], xq1a, dump[:, :], None),
            (xB[i % 3][:, :], 0, 128, CB["B"], xq1b, dump[:, :], None),
            (poolf[64:65, :], 64, 1, CB["X"], xq1x[64:65, :],
             dump[64:65, :], xmcol),
        ]
        sets[1] = (xA[i % 3][:, :], 0, 128, CB["A"], xq1a, dump[:, :], None)
        emit_qin(tag, sets, cols, srcdeps)
        note(f"{tag}_qdone", "v")
        emit_bcast(f"{tag}_al", [
            (cols, er, 1, CB["E"] + 9, 0),
            (cols, 0, 128, CB["A"] + 9, 1),
            (cols, 0, 128, CB["B"] + 9, 129),
            (cols, 64, 1, CB["X"] + 9, 257),
        ], MMW, extradep=[wt(f"{tag}_cols")])
        for dc in range(NDC):
            tdep = pu_wait()
            if dc >= 1:
                tdep = [wt(f"{tag}_tc_{dc - 1}")]
            E("t", lambda e, s=xq1a[:, dc * 128:(dc + 1) * 128]:
              e.transpose(pu[:, 0:128], s, idb[:, :]),
              tdep + [wt(f"{tag}_done")])
            E("t", lambda e, s=xq1b[:, dc * 128:(dc + 1) * 128]:
              e.transpose(pu[:, 128:256], s, idb[:, :]))
            E("t", lambda e, s=xq1e[er:er + 1, dc * 128:(dc + 1) * 128]:
              e.transpose(pu[:, 256:257], s, idb[er:er + 1, er:er + 1]))
            E("t", lambda e, s=xq1x[64:65, dc * 128:(dc + 1) * 128]:
              e.transpose(pu[:, 258:259], s, idb[64:65, 64:65]), (), "t")
            note(f"{tag}_tp_{dc}", "t")
            E("s", lambda e, d=dc: e.copy(
                out=xq1T[:, d * MMW + 1:d * MMW + 257], in_=pu[:, 0:256]),
              [wt(f"{tag}_tp_{dc}")], "s")
            E("s", lambda e, d=dc: e.copy(
                out=xq1T[:, d * MMW:d * MMW + 1], in_=pu[:, 256:257]))
            E("s", lambda e, d=dc: e.copy(
                out=xq1T[:, d * MMW + 257:d * MMW + 258], in_=pu[:, 258:259]),
              (), "s")
            note(f"{tag}_tc_{dc}", "s")
        pu_set(f"{tag}_tc_{NDC - 1}")
        note(f"{tag}_T", "s")
        pm1b = [pm1[0][:, 0:MMW], pm1[1][:, 0:MMW], pm1[2][:, 0:MMW],
                pm2[0][:, 0:MMW], pm2[0][:, 512:512 + MMW], pm2[1][:, 0:MMW]]
        tA = [tmpA[:, 0:FW], absT[:, 0:FW]]
        tB = [tmpB[:, 0:FW], dump[:, 0:FW]]
        for k in range(NC + 3):
            if k < NC:
                cch = k
                wl = [wt(f"{tag}_T"), wt(f"{tag}_al_bc")]
                if cch >= 6:
                    wl.append(wt(f"{tag}_ev_{cch - 6}"))
                if cch in (3, 4) and pm2_read[0]:
                    wl.append(wt(pm2_read[0]))
                if cch == 5 and pm2_read[1]:
                    wl.append(wt(pm2_read[1]))
                for dc in range(NDC):
                    E("t", lambda e, c2=cch, d=dc: e.matmul(
                        pm1b[c2 % 6],
                        w1q[:, d * HDIM + c2 * 128:d * HDIM + (c2 + 1) * 128],
                        xq1T[:, d * MMW:(d + 1) * MMW],
                        start=(d == 0), stop=(d == NDC - 1)),
                      wl if dc == 0 else (), "t" if dc == NDC - 1 else None)
                note(f"{tag}_mm1_{cch}", "t")
                wl = [wt(f"{tag}_mm1_{cch}")]
                if prev_round.get(cch):
                    wl.append(wt(prev_round[cch]))
                E("s", lambda e, c2=cch: e.copy(out=hsl(c2),
                                                in_=pm1b[c2 % 6]),
                  wl, "s")
                note(f"{tag}_ev_{cch}", "s")
            if 0 <= k - 1 < NC:
                cch = k - 1
                E("g", lambda e, c2=cch: e.tensor_tensor(
                    out=hsl(c2), in0=hsl(c2), in1=bcast[:, 0:MMW], op=ALU.mult),
                  [wt(f"{tag}_ev_{cch}"), wt(f"{tag}_al_bc")], "g")
                note(f"{tag}_hs_{cch}", "g")
                wl = [wt(f"{tag}_hs_{cch}")]
                if cch >= 2:
                    wl.append(wt(f"{tag}_B_{cch - 2}"))
                E("s", lambda e, c2=cch: e.activation(
                    out=tA[c2 % 2],
                    in_=hsl(c2)[:, 1:257], func=AF.Identity,
                    scale=cwt[:, NC + c2:NC + c2 + 1],
                    bias=cwt[:, 3 * NC + c2:3 * NC + c2 + 1]), wl, "s")
                note(f"{tag}_A_{cch}", "s")
            if 0 <= k - 2 < NC:
                cch = k - 2
                E("v", lambda e, c2=cch: e.scalar_tensor_tensor(
                    out=tB[c2 % 2], in0=hsl(c2)[:, 0:256],
                    scalar=cwt[:, c2:c2 + 1],
                    in1=tA[c2 % 2],
                    op0=ALU.mult, op1=ALU.add), [wt(f"{tag}_A_{cch}")], "v")
                note(f"{tag}_B_{cch}", "v")
                E("v", lambda e, c2=cch: e.scalar_tensor_tensor(
                    out=hsl(c2)[:, 0:256], in0=hsl(c2)[:, 2:258],
                    scalar=cwt[:, 2 * NC + c2:2 * NC + c2 + 1],
                    in1=tB[c2 % 2], op0=ALU.mult, op1=ALU.add), (), "v")
                note(f"{tag}_C_{cch}", "v")
            if 0 <= k - 3 < NC:
                cch = k - 3
                E("s", lambda e, c2=cch: e.activation(
                    out=hsl(c2)[:, 0:256], in_=hsl(c2)[:, 0:256], func=AF.Silu),
                  [wt(f"{tag}_C_{cch}")], "s")
                note(f"{tag}_si_{cch}", "s")
        E("v", lambda e: e.tensor_reduce(
            out=Mt[:, 0:FW],
            in_=hbuf[:, :].rearrange("p (c w) -> p w c", w=MMW)[:, 0:FW, :],
            axis=AX.X, op=ALU.max, apply_absolute_value=True),
          [wt(f"{tag}_si_{NC - 1}")], "v")
        note(f"{tag}_Mt", "v")
        for g in range(2):
            E("t", lambda e, g2=g: e.transpose(
                puf[:, UF0:UF0 + 128], Mt[:, g2 * 128:(g2 + 1) * 128], idf[:, :]),
              [wt(f"{tag}_Mt")] + pu_wait() +
              ([wt(f"{tag}_mxr_{g - 1}")] if g else []), "t")
            note(f"{tag}_mxt_{g}", "t")
            E("v", lambda e, g2=g: e.reduce_max(
                out=cols2[:, cb2 + g2 * 4:cb2 + g2 * 4 + 1],
                in_=puf[:, UF0:UF0 + 128],
                axis=AX.X), [wt(f"{tag}_mxt_{g}")], "v")
            note(f"{tag}_mxr_{g}", "v")
        pu_set(f"{tag}_mxr_1")
        c2s = lambda k: cols2[:, cb2 + k:cb2 + 8:4]
        vch(lambda e: e.tensor_scalar(out=c2s(0), in0=c2s(0), scalar1=1e-5,
                                      scalar2=None, op0=ALU.max),
            [wt(f"{tag}_mxr_1")])
        vch(lambda e: e.tensor_tensor(out=c2s(2), in0=c2s(0), in1=swc[:, 2:3]
                                      .broadcast_to([128, 2]), op=ALU.mult))
        vch(lambda e: e.reciprocal(out=c2s(1), in0=c2s(0)) if False else
            e.reciprocal(out=c2s(1), in_=c2s(0)))
        vch(lambda e: e.tensor_scalar(out=c2s(1), in0=c2s(1), scalar1=127.0,
                                      scalar2=None, op0=ALU.mult))
        note(f"{tag}_s2", "v")
        emit_bcast(f"{tag}_s2", [
            (cols2, 0, 128, cb2 + 1, 0),
            (cols2, 0, 128, cb2 + 5, 128),
        ], FW, extradep=[wt(f"{tag}_s2"), wt(f"{tag}_hs_{NC - 1}")])
        if do_round:
            ph_round(st, i)

    def ph_round(st, i):
        tag = f"{st}{i}"
        xqm = xqma if st == "a" else xqmb
        xqm_dep = [wt(prev_ma[st])] if prev_ma[st] else []
        h3 = hbuf[:, :].rearrange("p (c w) -> p c w", w=MMW)
        for cch in range(NC):
            E("g", lambda e, c2=cch: e.tensor_tensor(
                out=hsl(c2)[:, 0:256], in0=hsl(c2)[:, 0:256],
                in1=bcast[:, 0:FW], op=ALU.mult),
              [wt(f"{tag}_s2_bc")] +
              ([wt(f"{tag}_Mt")] if cch == 0 else []), "g")
            note(f"{tag}_sc_{cch}", "g")
        for j in range(4):
            E("v", lambda e, j2=j, xq=xqm: e.tensor_scalar(
                out=xq[:, j2 * 8 * FW:(j2 + 1) * 8 * FW]
                .rearrange("p (c w) -> p c w", w=FW),
                in0=h3[:, j2 * 8:(j2 + 1) * 8, 0:256],
                scalar1=MAGIC, scalar2=MAGIC, op0=ALU.add, op1=ALU.subtract),
              [wt(f"{tag}_sc_{j * 8 + 7}")] + xqm_dep, "v")
            note(f"{tag}_rn_{j}", "v")
            for c in range(j * 8, j * 8 + 8):
                prev_round[c] = f"{tag}_rn_{j}"
        note(f"{tag}_round", "v")

    def ph_mm2(st, i):
        tag = f"{st}{i}"
        cb2 = (i % 2) * 8 if st == "a" else 16
        xqm = xqma if st == "a" else xqmb
        for g in range(2):
            wl = [wt(f"{tag}_round")]
            if pm2_read[g]:
                wl.append(wt(pm2_read[g]))
            for jc in range(NC):
                for nh in range(2):
                    E("t", lambda e, g2=g, j=jc, n_=nh, xq=xqm: e.matmul(
                        pm2[g2][:, n_ * 512:(n_ + 1) * 512],
                        xq[:, j * FW + g2 * 128:j * FW + g2 * 128 + 128],
                        w2q[:, j * D + n_ * 512:j * D + (n_ + 1) * 512],
                        start=(j == 0), stop=(j == NC - 1)),
                      wl if (jc == 0 and nh == 0) else (),
                      "t" if (jc == NC - 1 and nh == 1) else None)
            note(f"{tag}_mm2_{g}", "t")
        prev_ma[st] = f"{tag}_mm2_1"
        if st == "b" and i + 1 < NT:
            dma(poolf[32:33, :], xB[i % 3][127:128, :],
                [wt(p32_user[0]), wt(f"a{i}_res")], "dg", eng="g",
                note_name=f"b{i + 1}_efill")
        for g in range(2):
            xg = xA[i % 3] if g == 0 else xB[i % 3]
            rwl = [wt(f"{tag}_mm2_{g}"), wt(f"{tag}_s2")]
            if st == "b" and g == 1 and i + 1 < NT:
                rwl.append(wt(f"b{i + 1}_efill"))
            E("v", lambda e, g2=g, x=xg: e.scalar_tensor_tensor(
                out=x[:, :], in0=pm2[g2][:, 0:D],
                scalar=cols2[:, cb2 + g2 * 4 + 2:cb2 + g2 * 4 + 3], in1=x[:, :],
                op0=ALU.mult, op1=ALU.add),
              rwl, "v")
            if st == "a":
                E("v", lambda e, x=xg: e.tensor_tensor(
                    out=x[:, :], in0=x[:, :], in1=obt[:, :], op=ALU.add), (), "v")
            note(f"{tag}_res_{g}", "v")
            pm2_read[g] = f"{tag}_res_{g}"
        note(f"{tag}_res", "v")
        if st == "b":
            dma(out_d[256 * i:256 * i + 128, :], xA[i % 3][:, :],
                [wt(f"{tag}_res_0")], "do", note_name=f"out{i}_a")
            dma(out_d[256 * i + 128:256 * i + 256, :], xB[i % 3][:, :],
                [wt(f"{tag}_res_1")], "do", note_name=f"out{i}_b")
            gnote(f"xfree{i}", [f"out{i}_a", f"out{i}_b"])

    def ph_mini(upto=99):
        tag = "mi"
        sets = [
            (minx[0:6, :], 0, 6, CB["A"], minq[0:6, :], scrB[0:6, :], None),
            (minx[32:38, :], 32, 6, CB["B"], minq[32:38, :], scrB[32:38, :], None),
        ]
        emit_qin(tag, sets, colsh, [wt("xsmall")])
        for dc in range(NDC):
            tdep = pu_wait() if dc == 0 else [wt(f"{tag}_tc_{dc - 1}")]
            E("t", lambda e, d=dc: e.transpose(
                pu[:, 0:6], minq[0:6, d * 128:(d + 1) * 128], idb[0:6, 0:6]),
              tdep + [wt(f"{tag}_done")])
            E("t", lambda e, d=dc: e.transpose(
                pu[:, 6:12], minq[32:38, d * 128:(d + 1) * 128],
                idb[32:38, 32:38]), (), "t")
            note(f"{tag}_tp_{dc}", "t")
            E("s", lambda e, d=dc: e.copy(
                out=xqmTm[:, d * 12:(d + 1) * 12], in_=pu[:, 0:12]),
              [wt(f"{tag}_tp_{dc}")], "s")
            note(f"{tag}_tc_{dc}", "s")
        pu_set(f"{tag}_tc_{NDC - 1}")
        for cch in range(NC):
            wl = [wt(f"{tag}_tc_{NDC - 1}")] + (pu_wait() if cch == 0 else
                                                [wt(f"{tag}_ev_{cch - 1}")])
            for dc in range(NDC):
                E("t", lambda e, c2=cch, d=dc: e.matmul(
                    puf[:, UF0:UF0 + 12],
                    w1q[:, d * HDIM + c2 * 128:d * HDIM + (c2 + 1) * 128],
                    xqmTm[:, d * 12:(d + 1) * 12],
                    start=(d == 0), stop=(d == NDC - 1)),
                  wl if dc == 0 else (), "t" if dc == NDC - 1 else None)
            note(f"{tag}_mm1_{cch}", "t")
            E("s", lambda e, c2=cch: e.copy(
                out=hm2[:, c2 * 12:(c2 + 1) * 12], in_=puf[:, UF0:UF0 + 12]),
              [wt(f"{tag}_mm1_{cch}")], "s")
            note(f"{tag}_ev_{cch}", "s")
        pu_set(f"{tag}_ev_{NC - 1}")
        emit_bcast(f"{tag}_al", [
            (colsh, 0, 6, CB["A"] + 9, 0),
            (colsh, 32, 6, CB["B"] + 9, 6),
        ], 12, extradep=[wt(f"{tag}_cols")], dst0=258)
        hm3 = hm2[:, :].rearrange("p (c w) -> p c w", w=12)
        E("g", lambda e: e.tensor_tensor(
            out=hm3[:, :, :], in0=hm3[:, :, :],
            in1=bcast[:, 0:12].unsqueeze(1).broadcast_to([128, NC, 12])
            if False else bcast[:, 0:12].broadcast_to([128, 12]),
            op=ALU.mult), ()) if False else None
        for cch in range(NC):
            E("g", lambda e, c2=cch: e.tensor_tensor(
                out=hm2[:, c2 * 12:(c2 + 1) * 12], in0=hm2[:, c2 * 12:(c2 + 1) * 12],
                in1=bcast[:, 258:270], op=ALU.mult),
              [wt(f"{tag}_al_bc"), wt(f"{tag}_ev_{cch}")], "g")
        note(f"{tag}_hs", "g")
        h3 = hm2[:, :].rearrange("p (c w) -> p c w", w=12)
        E("s", lambda e: e.activation(
            out=scrB[:, :].rearrange("p (c w) -> p c w", w=16)[:, 0:NC, 0:2],
            in_=h3[:, :, 3:9:5] if False else h3[:, :, 3:10:6],
            func=AF.Identity, scale=cwt[:, NC:2 * NC].transpose([0, 1])
            if False else 1.0, bias=0.0),
          [wt(f"{tag}_hs")], "s") if False else None
        for cch in range(NC):
            E("s", lambda e, c2=cch: e.activation(
                out=tmpB[:, 0:2], in_=hm2[:, c2 * 12 + 3:c2 * 12 + 10:6],
                func=AF.Identity, scale=cwt[:, NC + c2:NC + c2 + 1],
                bias=cwt[:, 3 * NC + c2:3 * NC + c2 + 1]),
              [wt(f"{tag}_hs")]
              + ([wt(f"{tag}_cC_{cch - 1}")] if cch else []), "s")
            note(f"{tag}_cA_{cch}", "s")
            E("v", lambda e, c2=cch: e.scalar_tensor_tensor(
                out=tmpB[:, 2:4], in0=hm2[:, c2 * 12 + 2:c2 * 12 + 9:6],
                scalar=cwt[:, c2:c2 + 1], in1=tmpB[:, 0:2],
                op0=ALU.mult, op1=ALU.add), [wt(f"{tag}_cA_{cch}")], "v")
            E("v", lambda e, c2=cch: e.scalar_tensor_tensor(
                out=hm2[:, c2 * 12 + 3:c2 * 12 + 10:6],
                in0=hm2[:, c2 * 12 + 4:c2 * 12 + 11:6],
                scalar=cwt[:, 2 * NC + c2:2 * NC + c2 + 1], in1=tmpB[:, 2:4],
                op0=ALU.mult, op1=ALU.add), (), "v")
            note(f"{tag}_cC_{cch}", "v")
            E("s", lambda e, c2=cch: e.activation(
                out=hm2[:, c2 * 12 + 3:c2 * 12 + 10:6],
                in_=hm2[:, c2 * 12 + 3:c2 * 12 + 10:6], func=AF.Silu),
              [wt(f"{tag}_cC_{cch}")], "s")
            note(f"{tag}_cs_{cch}", "s")
            E("s", lambda e, c2=cch: e.activation(
                out=absT[:, 0:2],
                in_=hm2[:, c2 * 12 + 3:c2 * 12 + 10:6], func=AF.Abs),
              [wt(f"{tag}_cs_{cch}")] +
              ([wt(f"{tag}_Mm_{cch - 1}")] if cch >= 1 else []), "s")
            note(f"{tag}_cb_{cch}", "s")
            if cch == 0:
                E("v", lambda e, c2=cch: e.tensor_copy(
                    out=Mh[:, 0:2], in_=absT[:, 0:2]),
                  [wt(f"{tag}_cb_{cch}")], "v")
            else:
                E("v", lambda e, c2=cch: e.tensor_tensor(
                    out=Mh[:, 0:2], in0=Mh[:, 0:2],
                    in1=absT[:, 0:2], op=ALU.max),
                  [wt(f"{tag}_cb_{cch}")], "v")
            note(f"{tag}_Mm_{cch}", "v")
        E("t", lambda e: e.transpose(puf[0:2, UF0:UF0 + 128], Mh[:, 0:2]
                                     .transpose([0, 1]) if False else Mh[:, 0:2],
                                     idf[:, :]),
          [wt(f"{tag}_Mm_{NC - 1}")] + pu_wait(), "t")
        note(f"{tag}_mxt", "t")
        E("v", lambda e: e.reduce_max(out=cols2[0:2, 18:19],
                                      in_=puf[0:2, UF0:UF0 + 128], axis=AX.X),
          [wt(f"{tag}_mxt")], "v")
        note(f"{tag}_mxr", "v")
        pu_set(f"{tag}_mxr")
        vch(lambda e: e.tensor_scalar(out=cols2[0:2, 18:19], in0=cols2[0:2, 18:19],
                                      scalar1=1e-5, scalar2=None, op0=ALU.max),
            [wt(f"{tag}_mxr")])
        vch(lambda e: e.tensor_tensor(out=cols2[0:2, 19:20], in0=cols2[0:2, 18:19],
                                      in1=swc[0:2, 2:3], op=ALU.mult))
        vch(lambda e: e.reciprocal(out=cols2[0:2, 17:18], in_=cols2[0:2, 18:19]))
        vch(lambda e: e.tensor_scalar(out=cols2[0:2, 17:18], in0=cols2[0:2, 17:18],
                                      scalar1=127.0, scalar2=None, op0=ALU.mult))
        note(f"{tag}_s2c", "v")
        emit_bcast(f"{tag}_s2", [(cols2, 0, 2, 17, 0)], 2,
                   extradep=[wt(f"{tag}_s2c")], dst0=270)
        for cch in range(NC):
            E("g", lambda e, c2=cch: e.tensor_tensor(
                out=hm2[:, c2 * 12 + 3:c2 * 12 + 10:6],
                in0=hm2[:, c2 * 12 + 3:c2 * 12 + 10:6],
                in1=bcast[:, 270:272], op=ALU.mult),
              [wt(f"{tag}_s2_bc"), wt(f"{tag}_Mm_{cch}")], "g")
            note(f"{tag}_ms_{cch}", "g")
            E("v", lambda e, c2=cch: e.tensor_scalar(
                out=xqm_m[:, c2 * 2:(c2 + 1) * 2],
                in0=hm2[:, c2 * 12 + 3:c2 * 12 + 10:6],
                scalar1=MAGIC, scalar2=MAGIC, op0=ALU.add, op1=ALU.subtract),
              [wt(f"{tag}_ms_{cch}")], "v")
            note(f"{tag}_rn_{cch}", "v")
        note(f"{tag}_round", "v")
        for dmc in range(NDC):
            wl = [wt(f"{tag}_round")] + (pu_wait() if dmc == 0 else
                                         [wt(f"{tag}_fc_{dmc - 1}"
                                             if upto >= 63 else
                                             f"{tag}_fv_{dmc - 1}")])
            for jc in range(NC):
                E("t", lambda e, d=dmc, j=jc: e.matmul(
                    puf[:, UF0:UF0 + 2],
                    w2q[:, j * D + d * 128:j * D + (d + 1) * 128],
                    xqm_m[:, j * 2:(j + 1) * 2],
                    start=(j == 0), stop=(j == NC - 1)),
                  wl if jc == 0 else (), "t" if jc == NC - 1 else None)
            note(f"{tag}_mm2_{dmc}", "t")
            E("t", lambda e: e.transpose(puf[0:2, UF0 + 4:UF0 + 132],
                                         puf[:, UF0:UF0 + 2], idf[:, :]),
              (), "t") if False else None
            if upto < 62:
                E("v", lambda e, d=dmc: e.tensor_copy(
                    out=tmpB[:, 0:2], in_=puf[:, UF0:UF0 + 2]),
                  [wt(f"{tag}_mm2_{dmc}")], "v")
                note(f"{tag}_fv_{dmc}", "v")
                continue
            E("v", lambda e, d=dmc: e.tensor_copy(
                out=tmpB[:, 0:2], in_=puf[:, UF0:UF0 + 2]),
              [wt(f"{tag}_mm2_{dmc}")], "v")
            note(f"{tag}_fv_{dmc}", "v")
            E("t", lambda e, d=dmc: e.transpose(
                puf[0:2, UF0 + 4:UF0 + 132], tmpB[:, 0:2], idf[:, :]),
              [wt(f"{tag}_fv_{dmc}")]
              + ([wt(f"{tag}_fc_{dmc - 1}")] if dmc > 0 else []), "t")
            note(f"{tag}_ft_{dmc}", "t")
            if upto < 63:
                note(f"{tag}_fc_{dmc}", "t")
                continue
            E("s", lambda e, d=dmc: e.copy(
                out=fmini[0:2, d * 128:(d + 1) * 128],
                in_=puf[0:2, UF0 + 4:UF0 + 132]),
              [wt(f"{tag}_ft_{dmc}")], "s")
            note(f"{tag}_fc_{dmc}", "s")
        if upto < 99:
            return
        pu_set(f"{tag}_fc_{NDC - 1}")
        note(f"{tag}_f", "s")
        dma(scrB[0:1, :], minx[3:4, :], [wt(f"{tag}_done")], "dg", eng="g",
            note_name=f"{tag}_xsrc0")
        dma(scrB[1:2, :], minx[36:37, :], [wt(f"{tag}_done")], "dg", eng="g",
            note_name=f"{tag}_xsrc")
        E("v", lambda e: e.scalar_tensor_tensor(
            out=scrB[0:2, :], in0=fmini[0:2, :], scalar=cols2[0:2, 19:20],
            in1=scrB[0:2, :], op0=ALU.mult, op1=ALU.add),
          [wt(f"{tag}_f"), wt(f"{tag}_xsrc"), wt(f"{tag}_xsrc0"),
           wt(f"{tag}_s2c")], "v")
        E("v", lambda e: e.tensor_tensor(
            out=scrB[0:2, :], in0=scrB[0:2, :], in1=obt[0:2, :], op=ALU.add),
          (), "v")
        dma(xh[62:64, :], scrB[0:2, :], [("v", R.n("v"))], "dg", eng="g",
            note_name="mini_done")

    def hqin(tag, src, p0, n_, dst, pstg, deps):
        emit_qin(tag, [(src, p0, n_, CB["A"], dst, pstg, None)], colsh, deps)

    qzb = xq1T[:, 0:640]

    def ph_attn_gen():
        tag = "at"
        hqin("ah", xh[0:40, :], 0, 40, xqh[0:40, :], scrB[0:40, :],
             [wt("xsmall"), wt("mini_done")])
        for dc in range(NDC):
            tdep = (pu_wait() if dc == 0 else [wt(f"ah_tc_{dc - 1}")])
            E("t", lambda e, d=dc: e.transpose(
                pu[:, 0:40], xqh[0:40, d * 128:(d + 1) * 128], idb[0:40, 0:40]),
              tdep + [wt("ah_done")], "t")
            note(f"ah_tp_{dc}", "t")
            E("s", lambda e, d=dc: e.copy(
                out=xqhT[:, d * 40:(d + 1) * 40], in_=pu[:, 0:40]),
              [wt(f"ah_tp_{dc}")], "s")
            note(f"ah_tc_{dc}", "s")
        pu_set(f"ah_tc_{NDC - 1}")
        yield
        emit_bcast("ah_al", [(colsh, 0, 40, CB["A"] + 9, 0)], 40,
                   extradep=[wt("ah_cols")], dst0=272)
        for grp in range(NC // 4):
            wl = [wt(f"ah_tc_{NDC - 1}")] + (pu_wait() if grp == 0 else
                                             [wt(f"ah_evg_{grp - 1}")])
            for ci in range(4):
                cch = grp * 4 + ci
                for dc in range(NDC):
                    E("t", lambda e, c2=cch, ci2=ci, d=dc: e.matmul(
                        puf[:, UF0 + ci2 * 40:UF0 + ci2 * 40 + 40],
                        w1q[:, d * HDIM + c2 * 128:d * HDIM + (c2 + 1) * 128],
                        xqhT[:, d * 40:(d + 1) * 40],
                        start=(d == 0), stop=(d == NDC - 1)),
                      wl if (ci == 0 and dc == 0) else (),
                      "t" if (ci == 3 and dc == NDC - 1) else None)
            note(f"ah_mm1g_{grp}", "t")
            E("v", lambda e, g2=grp: e.tensor_copy(
                out=hh[:, g2 * 160:(g2 + 1) * 160],
                in_=puf[:, UF0:UF0 + 160]),
              [wt(f"ah_mm1g_{grp}")], "v")
            note(f"ah_evg_{grp}", "v")
        for cch in range(NC):
            ev[f"ah_ev_{cch}"] = ev[f"ah_evg_{cch // 4}"]
            E("g", lambda e, c2=cch: e.tensor_tensor(
                out=hh[:, c2 * 40:(c2 + 1) * 40], in0=hh[:, c2 * 40:(c2 + 1) * 40],
                in1=bcast[:, 272:312], op=ALU.mult),
              [wt("ah_al_bc"), wt(f"ah_ev_{cch}")], "g")
            note(f"ah_hs_{cch}", "g")
            E("s", lambda e, c2=cch: e.activation(
                out=absh[:, 1:39], in_=hh[:, c2 * 40 + 1:c2 * 40 + 39],
                func=AF.Identity, scale=cwt[:, NC + c2:NC + c2 + 1],
                bias=cwt[:, 3 * NC + c2:3 * NC + c2 + 1]),
              [wt(f"ah_hs_{cch}")] + ([wt(f"ah_Mm_{cch - 1}")] if cch else []),
              "s")
            note(f"ah_cA_{cch}", "s")
            E("v", lambda e, c2=cch: e.scalar_tensor_tensor(
                out=absh[:, 1:39], in0=hh[:, c2 * 40 + 0:c2 * 40 + 38],
                scalar=cwt[:, c2:c2 + 1], in1=absh[:, 1:39],
                op0=ALU.mult, op1=ALU.add), [wt(f"ah_cA_{cch}")], "v")
            note(f"ah_cB_{cch}", "v")
            E("v", lambda e, c2=cch: e.scalar_tensor_tensor(
                out=hh[:, c2 * 40 + 1:c2 * 40 + 39],
                in0=hh[:, c2 * 40 + 2:c2 * 40 + 40],
                scalar=cwt[:, 2 * NC + c2:2 * NC + c2 + 1], in1=absh[:, 1:39],
                op0=ALU.mult, op1=ALU.add), (), "v")
            note(f"ah_cC_{cch}", "v")
            E("s", lambda e, c2=cch: e.activation(
                out=hh[:, c2 * 40 + 1:c2 * 40 + 39],
                in_=hh[:, c2 * 40 + 1:c2 * 40 + 39], func=AF.Silu),
              [wt(f"ah_cC_{cch}")], "s")
            note(f"ah_si_{cch}", "s")
            E("s", lambda e, c2=cch: e.activation(
                out=absh[:, 1:39], in_=hh[:, c2 * 40 + 1:c2 * 40 + 39],
                func=AF.Abs), [wt(f"ah_si_{cch}")] +
              ([wt(f"ah_Mm_{cch - 1}")] if cch else []), "s")
            note(f"ah_ab_{cch}", "s")
            if cch == 0:
                E("v", lambda e: e.tensor_copy(out=Mh[:, 1:39], in_=absh[:, 1:39]),
                  [wt(f"ah_ab_{cch}")], "v")
            else:
                E("v", lambda e: e.tensor_tensor(
                    out=Mh[:, 1:39], in0=Mh[:, 1:39], in1=absh[:, 1:39],
                    op=ALU.max), [wt(f"ah_ab_{cch}")], "v")
            note(f"ah_Mm_{cch}", "v")
        yield
        E("t", lambda e: e.transpose(puf[0:40, UF0:UF0 + 128], Mh[:, 0:40],
                                     idf[:, :]),
          [wt(f"ah_Mm_{NC - 1}")] + pu_wait(), "t")
        note("ah_mxt", "t")
        E("v", lambda e: e.reduce_max(out=colsh[0:40, 6:7],
                                      in_=puf[0:40, UF0:UF0 + 128], axis=AX.X),
          [wt("ah_mxt")], "v")
        note("ah_mxr", "v")
        pu_set("ah_mxr")
        vch(lambda e: e.tensor_scalar(out=colsh[0:40, 6:7], in0=colsh[0:40, 6:7],
                                      scalar1=1e-5, scalar2=None, op0=ALU.max),
            [wt("ah_mxr")])
        vch(lambda e: e.tensor_tensor(out=colsh[0:40, 10:11],
                                      in0=colsh[0:40, 6:7], in1=swc[0:40, 2:3],
                                      op=ALU.mult))
        vch(lambda e: e.reciprocal(out=colsh[0:40, 11:12], in_=colsh[0:40, 6:7]))
        vch(lambda e: e.tensor_scalar(out=colsh[0:40, 11:12],
                                      in0=colsh[0:40, 11:12], scalar1=127.0,
                                      scalar2=None, op0=ALU.mult))
        note("ah_s2c", "v")
        emit_bcast("ah_s2", [(colsh, 0, 40, 11, 0)], 40,
                   extradep=[wt("ah_s2c")], dst0=312)
        for cch in range(NC):
            E("g", lambda e, c2=cch: e.tensor_tensor(
                out=hh[:, c2 * 40 + 1:c2 * 40 + 39],
                in0=hh[:, c2 * 40 + 1:c2 * 40 + 39], in1=bcast[:, 313:351],
                op=ALU.mult), [wt("ah_s2_bc"), wt(f"ah_Mm_{cch}")], "g")
            note(f"ah_ms_{cch}", "g")
            E("v", lambda e, c2=cch: e.tensor_scalar(
                out=xqmh[:, c2 * 40:(c2 + 1) * 40], in0=hh[:, c2 * 40:(c2 + 1) * 40],
                scalar1=MAGIC, scalar2=MAGIC, op0=ALU.add, op1=ALU.subtract),
              [wt(f"ah_ms_{cch}")], "v")
            note(f"ah_rn_{cch}", "v")
        note("ah_round", "v")
        yield
        for nh in range(2):
            wl = [wt("ah_round")] + pu_wait()
            for jc in range(NC):
                E("t", lambda e, j=jc, n_=nh: e.matmul(
                    puf[0:40, 0:512],
                    xqmh[:, j * 40:(j + 1) * 40],
                    w2q[:, j * D + n_ * 512:j * D + (n_ + 1) * 512],
                    start=(j == 0), stop=(j == NC - 1)),
                  wl if jc == 0 else (), "t" if jc == NC - 1 else None)
            note(f"ah_mm2_{nh}", "t")
            E("s", lambda e, n_=nh: e.activation(
                out=scrA[0:40, n_ * 512:(n_ + 1) * 512], in_=puf[0:40, 0:512],
                func=AF.Identity, scale=colsh[0:40, 10:11]),
              [wt(f"ah_mm2_{nh}")], "s")
            note(f"ah_f_{nh}", "s")
            pu_set(f"ah_f_{nh}")
        E("v", lambda e: e.tensor_tensor(out=xh[0:40, :], in0=xh[0:40, :],
                                         in1=scrA[0:40, :], op=ALU.add),
          [wt("ah_f_0"), wt("ah_f_1")], "v")
        note("ah_x1h", "v")
        yield
        E("v", lambda e: e.tensor_reduce(out=colsh[0:40, 0:1], in_=xh[0:40, :],
                                         axis=AX.X, op=ALU.add),
          [wt("ah_x1h")], "v")
        note("zr", "v")
        E("s", lambda e: e.activation(out=scrB[0:40, :], in_=xh[0:40, :],
                                      func=AF.Square, accum_out=colsh[0:40, 1:2]),
          [wt("ah_x1h")], "s")
        note("zs", "s")
        zc = lambda k: colsh[0:40, k:k + 1]
        vch(lambda e: e.tensor_scalar(out=zc(0), in0=zc(0), scalar1=1.0 / D,
                                      scalar2=None, op0=ALU.mult),
            [wt("zr"), wt("zs")])
        vch(lambda e: e.tensor_scalar(out=zc(1), in0=zc(1), scalar1=1.0 / D,
                                      scalar2=None, op0=ALU.mult))
        vch(lambda e: e.tensor_tensor(out=zc(3), in0=zc(0), in1=zc(0),
                                      op=ALU.mult))
        vch(lambda e: e.tensor_tensor(out=zc(2), in0=zc(1), in1=zc(3),
                                      op=ALU.subtract))
        vch(lambda e: e.tensor_scalar(out=zc(2), in0=zc(2), scalar1=EPS,
                                      scalar2=None, op0=ALU.add))
        note("zpre", "v")
        E("s", lambda e: e.activation(out=zc(5), in_=zc(2), func=AF.Sqrt),
          [wt("zpre")], "s")
        note("zsqrt", "s")
        vch(lambda e: e.reciprocal(out=zc(11), in_=zc(5)), [wt("zsqrt")])
        for _ in range(2):
            vch(lambda e: e.tensor_tensor(out=zc(5), in0=zc(2), in1=zc(11),
                                          op=ALU.mult))
            vch(lambda e: e.tensor_tensor(out=zc(5), in0=zc(5), in1=zc(11),
                                          op=ALU.mult))
            vch(lambda e: e.tensor_scalar(out=zc(5), in0=zc(5), scalar1=-0.5,
                                          scalar2=1.5, op0=ALU.mult, op1=ALU.add))
            vch(lambda e: e.tensor_tensor(out=zc(11), in0=zc(11), in1=zc(5),
                                          op=ALU.mult))
        vch(lambda e: e.tensor_tensor(out=zc(4), in0=zc(0), in1=zc(11),
                                      op=ALU.mult))
        vch(lambda e: e.tensor_scalar(out=zc(4), in0=zc(4), scalar1=-1.0,
                                      scalar2=None, op0=ALU.mult))
        note("zcols", "v")
        E("s", lambda e: e.activation(out=scrA[0:40, :], in_=xh[0:40, :],
                                      func=AF.Identity, scale=zc(11),
                                      bias=zc(4)), [wt("zcols")], "s")
        note("z16", "s")
        zaT = fmb[:, 0:320]
        zbT = fmb[:, 320:640]
        for dc in range(NDC):
            E("t", lambda e, d=dc: e.transpose(
                puf[:, UF0:UF0 + 40], scrA[0:40, d * 128:(d + 1) * 128],
                idf[0:40, 0:40]),
              [wt("z16")] + (pu_wait() if dc == 0 else [wt(f"zb_{dc - 1}")]),
              "t")
            note(f"zt_{dc}", "t")
            E("s", lambda e, d=dc: e.activation(
                out=zaT[:, d * 40:(d + 1) * 40], in_=puf[:, UF0:UF0 + 40],
                func=AF.Identity, scale=lab[:, d:d + 1]), [wt(f"zt_{dc}")], "s")
            note(f"za_{dc}", "s")
            E("s", lambda e, d=dc: e.activation(
                out=zbT[:, d * 40:(d + 1) * 40], in_=puf[:, UF0:UF0 + 40],
                func=AF.Identity, scale=lab[:, 8 + d:9 + d]), (), "s")
            note(f"zb_{dc}", "s")
        pu_set(f"zb_{NDC - 1}")
        note("zT", "s")
        yield
        def pj_dst(i2):
            if i2 < 3:
                return pm1[i2][:, 0:40]
            if i2 == 3:
                return puf[:, UF0:UF0 + 40]
            g, hb = divmod(i2 - 4, 2)
            return pm2[g][:, hb * 512:hb * 512 + 40]

        for wi in range(3):
            rhs = zaT if wi == 0 else zbT
            for dc in range(NDC):
                wl = [wt("awscr_done"), wt("zT")]
                if wi or dc:
                    wl.append(wt(f"pj_ev_{wi - 1}_{NDC - 1}") if (dc == 0 and wi)
                              else wt(f"pj_mm_{wi}_{dc - 2}") if dc >= 2 else
                              wt("zT"))
                dma(wstg[:, (dc % 2) * D:(dc % 2) * D + D],
                    awscr[wi * D:(wi + 1) * D, dc * 128:(dc + 1) * 128],
                    wl, "da", tp=True, note_name=f"pj_sl_{wi}_{dc}")
                for ic in range(8):
                    E("t", lambda e, w_=wi, d=dc, i2=ic, r=rhs: e.matmul(
                        pj_dst(i2),
                        wstg[:, (d % 2) * D + i2 * 128:(d % 2) * D + (i2 + 1) * 128],
                        r[:, d * 40:(d + 1) * 40],
                        start=(d == 0), stop=(d == NDC - 1)),
                      ([wt(f"pj_sl_{wi}_{dc}")] +
                       ((pu_wait() + [wt(pm2_read[0]), wt(pm2_read[1])])
                        if (wi == 0 and dc == 0 and ic == 0) else []) +
                       ([wt(f"pj_ev_{wi - 1}_7")] if (dc == 0 and ic == 0 and wi)
                        else [])) if ic == 0 else (),
                      "t" if ic == 7 else None)
                note(f"pj_mm_{wi}_{dc}", "t")
            for ic in range(8):
                if wi == 0:
                    E("s", lambda e, w_=wi, i2=ic: e.activation(
                        out=fmp[:, w_ * 320 + i2 * 40:w_ * 320 + (i2 + 1) * 40],
                        in_=pj_dst(i2),
                        func=AF.Identity, bias=qbv[:, i2:i2 + 1]),
                      [wt(f"pj_mm_{wi}_{NDC - 1}")], "s")
                elif wi == 2:
                    E("s", lambda e, w_=wi, i2=ic: e.activation(
                        out=fmp[:, w_ * 320 + i2 * 40:w_ * 320 + (i2 + 1) * 40],
                        in_=pj_dst(i2),
                        func=AF.Identity, bias=qbv[:, 8 + i2:9 + i2]),
                      [wt(f"pj_mm_{wi}_{NDC - 1}")], "s")
                else:
                    E("s", lambda e, w_=wi, i2=ic: e.copy(
                        out=fmp[:, w_ * 320 + i2 * 40:w_ * 320 + (i2 + 1) * 40],
                        in_=pj_dst(i2)),
                      [wt(f"pj_mm_{wi}_{NDC - 1}")], "s")
                note(f"pj_ev_{wi}_{ic}", "s")
            note(f"pj_{wi}", "s")
        pu_set("pj_ev_2_7")
        pm2_read[0] = "pj_2"
        pm2_read[1] = "pj_2"
        yield
        E("v", lambda e: e.memset(bdg[:, :], 0.0), (), "v")
        note("bdg0", "v")
        for wi in range(3):
            dma(scrB[0:64, 0:64], lw_d[wi][:, :],
                [wt("bdg0")] + ([wt(f"lwt_{wi - 1}")] if wi else []),
                "da", note_name=f"lw_{wi}")
            E("t", lambda e, w_=wi: e.transpose(
                puf[0:64, UF0:UF0 + 64], scrB[0:64, 0:64], idf[0:64, 0:64]),
              [wt(f"lw_{wi}")] + pu_wait()
              + ([wt(f"lwc1_{wi - 1}")] if wi else []), "t")
            note(f"lwt_{wi}", "t")
            E("s", lambda e, w_=wi: e.copy(
                out=bdg[0:64, w_ * 128:w_ * 128 + 64],
                in_=puf[0:64, UF0:UF0 + 64]), [wt(f"lwt_{wi}"), wt("bdg0")], "s")
            note(f"lwc1_{wi}", "s")
            dma(bdg[64:128, wi * 128 + 64:wi * 128 + 128],
                bdg[0:64, wi * 128:wi * 128 + 64], [wt(f"lwc1_{wi}")], "dg",
                eng="g", note_name=f"lwc_{wi}")
        pu_set("lwc_2")
        gnote("bdg_done", ["lwc_0", "lwc_1", "lwc_2"])
        def transform(tg, src_slot, wi, dst_slot, lncol):
            E("v", lambda e, s=src_slot: e.tensor_copy(
                out=fmb[:, 320:640], in_=fmp[:, s * 320:(s + 1) * 320]),
              [wt(f"{tg}_srcdep")], "v")
            note(f"{tg}_cast", "v")
            for ic in range(8):
                E("t", lambda e, w_=wi, i2=ic: e.matmul(
                    puf[:, UF0 + i2 * 40:UF0 + (i2 + 1) * 40],
                    bdg[:, w_ * 128:(w_ + 1) * 128],
                    fmb[:, 320 + i2 * 40:320 + (i2 + 1) * 40],
                    start=True, stop=True),
                  [wt(f"{tg}_cast"), wt("bdg_done")] +
                  (pu_wait() if ic == 0 else []), "t" if ic == 7 else None)
            note(f"{tg}_mm", "t")
            for ic in range(8):
                E("s", lambda e, w_=wi, i2=ic: e.activation(
                    out=scrB[0:40, 0:0 + 128][0:40, :] if False else
                    scrB[0:40, i2 * 128:(i2 + 1) * 128],
                    in_=puf[:, UF0 + i2 * 40:UF0 + (i2 + 1) * 40],
                    func=AF.Identity, bias=lbv[:, w_:w_ + 1]),
                  [wt(f"{tg}_mm")], "s") if False else None
            for ic in range(8):
                E("s", lambda e, w_=wi, i2=ic: e.activation(
                    out=fmp[:, 3 * 320 + i2 * 40:3 * 320 + (i2 + 1) * 40],
                    in_=puf[:, UF0 + i2 * 40:UF0 + (i2 + 1) * 40],
                    func=AF.Identity, bias=lbv[:, w_:w_ + 1]),
                  [wt(f"{tg}_mm")], "s")
            note(f"{tg}_ev", "s")
            pu_set(f"{tg}_ev")
            for dc in range(NDC):
                E("t", lambda e, d=dc: e.transpose(
                    puf[0:40, UF0:UF0 + 128],
                    fmp[:, 3 * 320 + d * 40:3 * 320 + (d + 1) * 40],
                    idf[:, :]),
                  [wt(f"{tg}_ev")] + (pu_wait() if dc == 0 else
                                      [wt(f"{tg}_tmc_{dc - 1}")]), "t")
                note(f"{tg}_tm_{dc}", "t")
                E("s", lambda e, d=dc: e.copy(
                    out=scrB[0:40, d * 128:(d + 1) * 128],
                    in_=puf[0:40, UF0:UF0 + 128]), [wt(f"{tg}_tm_{dc}")], "s")
                note(f"{tg}_tmc_{dc}", "s")
            pu_set(f"{tg}_tmc_{NDC - 1}")
            note(f"{tg}_tm", "s")
            s3 = scrB[0:40, :].rearrange("p (h w) -> p h w", w=64)
            E("v", lambda e: e.tensor_reduce(
                out=habc[0:40, 0:16], in_=s3, axis=AX.X, op=ALU.add),
              [wt(f"{tg}_tm")], "v")
            note(f"{tg}_hr", "v")
            E("s", lambda e: e.activation(
                out=at16[0:40, :], in_=scrB[0:40, :], func=AF.Square),
              [wt(f"{tg}_tm")], "s")
            note(f"{tg}_hsq", "s")
            E("v", lambda e: e.tensor_reduce(
                out=habc[0:40, 16:32],
                in_=at16[0:40, :].rearrange("p (h w) -> p h w", w=64),
                axis=AX.X, op=ALU.add), [wt(f"{tg}_hsq")], "v")
            note(f"{tg}_hr2", "v")
            hc = lambda k: habc[0:40, k * 16:(k + 1) * 16]
            vch(lambda e: e.tensor_scalar(out=hc(0), in0=hc(0), scalar1=1.0 / 64,
                                          scalar2=None, op0=ALU.mult),
                [wt(f"{tg}_hr"), wt(f"{tg}_hr2")])
            vch(lambda e: e.tensor_scalar(out=hc(1), in0=hc(1), scalar1=1.0 / 64,
                                          scalar2=None, op0=ALU.mult))
            vch(lambda e: e.tensor_tensor(out=hc(4), in0=hc(0), in1=hc(0),
                                          op=ALU.mult))
            vch(lambda e: e.tensor_tensor(out=hc(2), in0=hc(1), in1=hc(4),
                                          op=ALU.subtract))
            vch(lambda e: e.tensor_scalar(out=hc(2), in0=hc(2), scalar1=EPS,
                                          scalar2=None, op0=ALU.add))
            note(f"{tg}_hpre", "v")
            E("s", lambda e: e.activation(out=hc(4), in_=hc(2), func=AF.Sqrt),
              [wt(f"{tg}_hpre")], "s")
            note(f"{tg}_hsqrt", "s")
            vch(lambda e: e.reciprocal(out=hc(3), in_=hc(4)), [wt(f"{tg}_hsqrt")])
            for _ in range(2):
                vch(lambda e: e.tensor_tensor(out=hc(4), in0=hc(2), in1=hc(3),
                                              op=ALU.mult))
                vch(lambda e: e.tensor_tensor(out=hc(4), in0=hc(4), in1=hc(3),
                                              op=ALU.mult))
                vch(lambda e: e.tensor_scalar(out=hc(4), in0=hc(4), scalar1=-0.5,
                                              scalar2=1.5, op0=ALU.mult,
                                              op1=ALU.add))
                vch(lambda e: e.tensor_tensor(out=hc(3), in0=hc(3), in1=hc(4),
                                              op=ALU.mult))
            note(f"{tg}_hrs", "v")
            vch(lambda e: e.tensor_tensor(out=habc[0:40, 64:80],
                                          in0=hc(0), in1=hc(3), op=ALU.mult),
                [wt(f"{tg}_hrs")])
            vch(lambda e: e.tensor_scalar(out=habc[0:40, 64:80],
                                          in0=habc[0:40, 64:80], scalar1=-1.0,
                                          scalar2=None, op0=ALU.mult))
            note(f"{tg}_mrs", "v")
            for h in range(16):
                E("s", lambda e, h2=h: e.activation(
                    out=scrB[0:40, h2 * 64:(h2 + 1) * 64],
                    in_=scrB[0:40, h2 * 64:(h2 + 1) * 64], func=AF.Identity,
                    scale=habc[0:40, 48 + h2:49 + h2],
                    bias=habc[0:40, 64 + h2:65 + h2]),
                  [wt(f"{tg}_mrs")] if h == 0 else (),
                  "s" if h == 15 else None)
            note(f"{tg}_ln", "s")
            for dc in range(NDC):
                E("t", lambda e, d=dc: e.transpose(
                    puf[:, UF0:UF0 + 40], scrB[0:40, d * 128:(d + 1) * 128],
                    idf[0:40, 0:40]),
                  [wt(f"{tg}_ln")] + (pu_wait() if dc == 0 else
                                      [wt(f"{tg}_fbc_{dc - 1}")]), "t")
                note(f"{tg}_fb_{dc}", "t")
                if lncol == 1:
                    E("s", lambda e, d=dc, ds=dst_slot: e.activation(
                        out=fmb[:, ds * 320 + d * 40:ds * 320 + (d + 1) * 40],
                        in_=puf[:, UF0:UF0 + 40], func=AF.Identity,
                        scale=lbv[:, 5:6]), [wt(f"{tg}_fb_{dc}"), wt("gcol")], "s")
                else:
                    E("s", lambda e, d=dc, ds=dst_slot: e.copy(
                        out=fmb[:, ds * 320 + d * 40:ds * 320 + (d + 1) * 40],
                        in_=puf[:, UF0:UF0 + 40]), [wt(f"{tg}_fb_{dc}")], "s")
                note(f"{tg}_fbc_{dc}", "s")
            pu_set(f"{tg}_fbc_{NDC - 1}")
            note(f"{tg}_out", "s")

        yield
        note("kn_srcdep", "s")
        ev["kn_srcdep"] = ev["pj_2"]
        transform("kn", 1, 1, 0, 1)
        yield
        E("v", lambda e: e.tensor_copy(out=fmb[:, 320:640],
                                       in_=fmp[:, 2 * 320:3 * 320]),
          [wt("pj_2"), wt("kn_mm")], "v")
        note("vi_cast", "v")
        for ic in range(8):
            E("t", lambda e, i2=ic: e.matmul(
                puf[:, UF0 + i2 * 40:UF0 + (i2 + 1) * 40],
                bdg[:, 2 * 128:3 * 128],
                fmb[:, 320 + i2 * 40:320 + (i2 + 1) * 40],
                start=True, stop=True),
              [wt("vi_cast"), wt("bdg_done")] + (pu_wait() if ic == 0 else []),
              "t" if ic == 7 else None)
        note("vi_mm", "t")
        for ic in range(8):
            E("s", lambda e, i2=ic: e.activation(
                out=fmp[:, 3 * 320 + i2 * 40:3 * 320 + (i2 + 1) * 40],
                in_=puf[:, UF0 + i2 * 40:UF0 + (i2 + 1) * 40],
                func=AF.Identity, bias=lbv[:, 2:3]), [wt("vi_mm")], "s")
        note("vi_ev", "s")
        pu_set("vi_ev")
        for dc in range(NDC):
            for b in range(2):
                E("t", lambda e, d=dc, b2=b: e.transpose(
                    puf[0:16, UF0:UF0 + 128],
                    fmp[:, 3 * 320 + d * 40 + b2 * 20 + 2:
                        3 * 320 + d * 40 + b2 * 20 + 18], idf[:, :]),
                  [wt("vi_ev")] + (pu_wait() if (dc == 0 and b == 0) else
                                   [wt(f"vi_tc_{dc}_{b - 1}") if b else
                                    wt(f"vi_tc_{dc - 1}_1")]), "t")
                note(f"vi_tm_{dc}_{b}", "t")
                E("s", lambda e, d=dc, b2=b: e.copy(
                    out=vib[0:16, b2 * 1024 + d * 128:b2 * 1024 + (d + 1) * 128],
                    in_=puf[0:16, UF0:UF0 + 128]),
                  [wt(f"vi_tm_{dc}_{b}"), wt("ah_round")], "s")
                note(f"vi_tc_{dc}_{b}", "s")
        pu_set(f"vi_tc_{NDC - 1}_1")
        note("vi_tm", "s")
        yield
        E("v", lambda e: e.memset(qzb[:, :], 0.0),
          [wt(f"a2_mm1_{NC - 1}")], "v")
        note("qzb0", "v")
        E("v", lambda e: e.tensor_copy(out=fmp[:, 5 * 320:6 * 320],
                                       in_=fmp[:, 0:320]), [wt("pj_0")], "v")
        note("q0c", "v")

        def iteration(n):
            tg = f"it{n}"
            ev[f"{tg}_srcdep"] = [
                ev["q0c" if n == 0 else f"qup{n}"],
                ev["vi_mm" if n == 0 else f"it{n - 1}_lg"]]
            transform(tg, 5, 0, 1, 0)
            E("v", lambda e: e.tensor_copy(
                out=qzb[0:64, 0:320], in_=fmb[0:64, 320:640]),
              [wt(f"{tg}_out"), wt("qzb0")], "v")
            E("v", lambda e: e.tensor_copy(
                out=qzb[64:128, 320:640], in_=fmb[64:128, 320:640]), (), "v")
            note(f"{tg}_qz", "v")
            for b in range(2):
                for h in range(16):
                    bh = b * 16 + h
                    E("t", lambda e, b2=b, h2=h, bh2=bh: e.matmul(
                        puf[0:16, bh2 * 16:(bh2 + 1) * 16],
                        qzb[:, (h2 % 2) * 320 + (h2 // 2) * 40 + b2 * 20 + 2:
                            (h2 % 2) * 320 + (h2 // 2) * 40 + b2 * 20 + 18],
                        fmb[:, (h2 // 2) * 40 + b2 * 20 + 2:
                            (h2 // 2) * 40 + b2 * 20 + 18],
                        start=True, stop=True),
                      ([wt(f"{tg}_out"), wt(f"{tg}_qz"), wt("kn_out")]
                       + pu_wait())
                      if bh == 0 else (), "t" if bh == 31 else None)
            note(f"{tg}_lg", "t")
            E("v", lambda e: e.tensor_reduce(
                out=habc[0:16, 0:32],
                in_=puf[0:16, 0:512].rearrange("p (t k) -> p t k", k=16),
                axis=AX.X, op=ALU.max), [wt(f"{tg}_lg")], "v")
            note(f"{tg}_sm1", "v")
            vch(lambda e: e.tensor_scalar(
                out=habc[0:16, 0:32], in0=habc[0:16, 0:32], scalar1=-0.125,
                scalar2=None, op0=ALU.mult), [wt(f"{tg}_sm1")])
            note(f"{tg}_nmx", "v")
            for t in range(32):
                E("s", lambda e, t2=t: e.activation(
                    out=scrA[0:16, t2 * 16:(t2 + 1) * 16],
                    in_=puf[0:16, t2 * 16:(t2 + 1) * 16], func=AF.Exp,
                    scale=0.125, bias=habc[0:16, t2:t2 + 1]),
                  [wt(f"{tg}_nmx")] if t == 0 else (),
                  "s" if t == 31 else None)
            note(f"{tg}_exp", "s")
            pu_set(f"{tg}_exp")
            E("v", lambda e: e.tensor_reduce(
                out=habc[0:16, 32:64],
                in_=scrA[0:16, 0:512].rearrange("p (t k) -> p t k", k=16),
                axis=AX.X, op=ALU.add), [wt(f"{tg}_exp")], "v")
            note(f"{tg}_sm2", "v")
            vch(lambda e: e.reciprocal(out=habc[0:16, 0:32],
                                       in_=habc[0:16, 32:64]),
                [wt(f"{tg}_sm2")])
            note(f"{tg}_rcp", "v")
            for t in range(32):
                E("s", lambda e, t2=t: e.activation(
                    out=scrA[0:16, t2 * 16:(t2 + 1) * 16],
                    in_=scrA[0:16, t2 * 16:(t2 + 1) * 16], func=AF.Identity,
                    scale=habc[0:16, t2:t2 + 1]),
                  [wt(f"{tg}_rcp")] if t == 0 else (),
                  "s" if t == 31 else None)
            note(f"{tg}_P", "s")
            def pt_tgt(bh2):
                if bh2 % 2 == 0:
                    return puf[0:16, 256 + (bh2 % 16) * 16:
                               256 + (bh2 % 16) * 16 + 16]
                return pm1[0][0:16, (bh2 % 16) * 16:(bh2 % 16) * 16 + 16]
            pt_order = [b for b in range(32) if b % 2 == 0] + \
                [b for b in range(32) if b % 2 == 1]
            for idx, bh in enumerate(pt_order):
                E("t", lambda e, bh2=bh: e.transpose(
                    pt_tgt(bh2),
                    scrA[0:16, bh2 * 16:(bh2 + 1) * 16], idf[0:16, 0:16]),
                  [wt(f"{tg}_P")] if idx == 0 else
                  ([wt(f"{tg}_PTc_{bh - 16}")] if bh >= 16 else ()), "t")
                note(f"{tg}_PT_{bh}", "t")
                E("v", lambda e, bh2=bh: e.tensor_copy(
                    out=PTb[0:16, bh2 * 16:(bh2 + 1) * 16],
                    in_=pt_tgt(bh2)),
                  [wt(f"{tg}_PT_{bh}")], "v")
                note(f"{tg}_PTc_{bh}", "v")
            note(f"{tg}_PTb", "s")
            pu_set(f"{tg}_PTb")
            E("v", lambda e: e.memset(puf[:, UF0:UF0 + 320], 0.0),
              pu_wait(), "v")
            note(f"{tg}_az", "v")
            for b in range(2):
                for h in range(16):
                    E("t", lambda e, b2=b, h2=h: e.matmul(
                        puf[(h2 % 2) * 64:(h2 % 2) * 64 + 64,
                            UF0 + (h2 // 2) * 40 + b2 * 20 + 2:
                            UF0 + (h2 // 2) * 40 + b2 * 20 + 18],
                        vib[0:16, b2 * 1024 + h2 * 64:b2 * 1024 + (h2 + 1) * 64],
                        PTb[0:16, (b2 * 16 + h2) * 16:(b2 * 16 + h2 + 1) * 16],
                        start=True, stop=True,
                        tile_position=(0, 64 * (h2 % 2))),
                      [wt(f"{tg}_PTb"), wt("vi_tm"), wt(f"{tg}_az")]
                      if (b == 0 and h == 0) else (),
                      "t" if (b == 1 and h == 15) else None)
            note(f"{tg}_av", "t")
            its = {0: 1, 1: 2, 2: 4}[n]
            E("v", lambda e, s2=its: e.tensor_copy(
                out=fmp[:, s2 * 320:(s2 + 1) * 320],
                in_=puf[:, UF0:UF0 + 320]), [wt(f"{tg}_av")], "v")
            note(f"{tg}_it", "v")
            pu_set(f"{tg}_it")

        iteration(0)
        E("v", lambda e: e.tensor_tensor(
            out=fmp[:, 5 * 320:6 * 320], in0=fmp[:, 5 * 320:6 * 320],
            in1=fmp[:, 1 * 320:2 * 320], op=ALU.add), [wt("it0_it")], "v")
        note("qup1", "v")
        yield
        iteration(1)
        E("v", lambda e: e.tensor_tensor(
            out=fmp[:, 3 * 320:4 * 320], in0=fmp[:, 2 * 320:3 * 320],
            in1=fmp[:, 1 * 320:2 * 320], op=ALU.subtract), [wt("it1_it")], "v")
        note("df_sub", "v")
        d3 = fmp[:, 3 * 320:4 * 320].rearrange("p (c w) -> p c w", w=40)
        for b in range(2):
            E("v", lambda e, b2=b: e.reduce_sum(
                out=habc[0:128, 64 + b2 * 8:64 + b2 * 8 + 8]
                if False else arow[0:1, 40:48],
                in_=d3[:, :, b2 * 20 + 2:b2 * 20 + 18], axis=AX.X,
                apply_absolute_value=True), [wt("df_sub")], "v") if False else None
        E("v", lambda e: e.reduce_sum(
            out=habc[0:128, 64:72] if False else Mh[:, 0:8],
            in_=d3[:, :, 2:18], axis=AX.X, apply_absolute_value=True),
          [wt("df_sub")], "v")
        note("df_r1a", "v")
        E("v", lambda e: e.reduce_sum(
            out=Mh[:, 8:16], in_=d3[:, :, 22:38], axis=AX.X,
            apply_absolute_value=True), (), "v")
        note("df_r1", "v")
        E("v", lambda e: e.reduce_sum(out=Mh[:, 16:17], in_=Mh[:, 0:16],
                                      axis=AX.X), [wt("df_r1")], "v")
        note("df_r2", "v")
        E("t", lambda e: e.matmul(
            puf[0:1, SP0:SP0 + 1], Mh[:, 16:17],
            onescol[:, :], start=True, stop=True),
          [wt("df_r2")], "t")
        note("df_mm", "t")
        vch(lambda e: e.tensor_scalar(
            out=arow[0:1, 16:17], in0=puf[0:1, SP0:SP0 + 1],
            scalar1=1.0 / DENOM, scalar2=None, op0=ALU.mult), [wt("df_mm")])
        vch(lambda e: e.tensor_scalar(
            out=arow[0:1, 17:18], in0=arow[0:1, 16:17], scalar1=0.1,
            scalar2=0.01, op0=ALU.mult, op1=ALU.add))
        vch(lambda e: e.tensor_tensor(
            out=arow[0:1, 18:19], in0=arow[0:1, 16:17], in1=arow[0:1, 17:18],
            op=ALU.is_lt))
        vch(lambda e: e.tensor_scalar(
            out=arow[0:1, 19:20], in0=arow[0:1, 18:19], scalar1=-1.0,
            scalar2=1.0, op0=ALU.mult, op1=ALU.add))
        note("df_s", "v")
        E("t", lambda e: e.matmul(
            puf[:, SP0 + 1:SP0 + 2], onesf[0:1, :], arow[0:1, 18:19],
            start=True, stop=True), [wt("df_s")])
        E("t", lambda e: e.matmul(
            puf[:, SP0 + 2:SP0 + 3], onesf[0:1, :], arow[0:1, 19:20],
            start=True, stop=True), (), "t")
        note("df_bc", "t")
        E("v", lambda e: e.tensor_copy(out=Mh[:, 20:22],
                                       in_=puf[:, SP0 + 1:SP0 + 3]),
          [wt("df_bc")], "v")
        note("df_cols", "v")
        E("v", lambda e: e.tensor_scalar(
            out=fmp[:, 3 * 320:4 * 320], in0=fmp[:, 2 * 320:3 * 320],
            scalar1=Mh[:, 21:22], scalar2=None, op0=ALU.mult),
          [wt("df_cols")], "v")
        E("v", lambda e: e.tensor_tensor(
            out=fmp[:, 5 * 320:6 * 320], in0=fmp[:, 5 * 320:6 * 320],
            in1=fmp[:, 3 * 320:4 * 320], op=ALU.add), (), "v")
        note("qup2", "v")
        yield
        iteration(2)
        yield
        E("v", lambda e: e.tensor_scalar(
            out=fmp[:, 3 * 320:4 * 320], in0=fmp[:, 2 * 320:3 * 320],
            scalar1=Mh[:, 20:21], scalar2=None, op0=ALU.mult),
          [wt("it2_it"), wt("df_cols")], "v")
        E("v", lambda e: e.scalar_tensor_tensor(
            out=fmp[:, 3 * 320:4 * 320], in0=fmp[:, 4 * 320:5 * 320],
            scalar=Mh[:, 21:22], in1=fmp[:, 3 * 320:4 * 320],
            op0=ALU.mult, op1=ALU.add), (), "v")
        note("at_fin", "v")
        E("v", lambda e: e.tensor_copy(out=fmb[:, 320:640],
                                       in_=fmp[:, 3 * 320:4 * 320]),
          [wt("at_fin"), wt("it2_lg")], "v")
        note("at_cast", "v")
        for nh in range(2):
            for dc in range(NDC):
                wl = [wt("awscr_done"), wt("at_cast")]
                if nh or dc:
                    wl.append(wt(f"oj_mm_{nh}_{dc - 2}") if dc >= 2 else
                              wt(f"oj_ev_{nh - 1}") if nh else wt("at_cast"))
                dma(wstg[:, (dc % 2) * D:(dc % 2) * D + D],
                    awscr[3 * D:4 * D, dc * 128:(dc + 1) * 128], wl, "da",
                    tp=True, note_name=f"oj_sl_{nh}_{dc}")
                E("t", lambda e, n_=nh, d=dc: e.matmul(
                    puf[0:16, 0:512],
                    fmb[:, 320 + d * 40 + 2:320 + d * 40 + 18],
                    wstg[:, (d % 2) * D + n_ * 512:(d % 2) * D + (n_ + 1) * 512],
                    start=(d == 0), stop=(d == NDC - 1)),
                  [wt(f"oj_sl_{nh}_{dc}")] + (pu_wait() if (nh == 0 and dc == 0)
                                              else []), "t")
                note(f"oj_mm_{nh}_{dc}", "t")
            E("s", lambda e, n_=nh: e.copy(
                out=at16[0:16, n_ * 512:(n_ + 1) * 512], in_=puf[0:16, 0:512]),
              [wt(f"oj_mm_{nh}_{NDC - 1}")], "s")
            note(f"oj_ev_{nh}", "s")
        pu_set("oj_ev_1")
        note("attn_o", "s")

        E("v", lambda e: e.tensor_scalar(
            out=at16[0:16, :], in0=at16[0:16, :], scalar1=swc[0:16, 4:5],
            scalar2=None, op0=ALU.mult),
          [wt("attn_o"), wt("a0_res"), wt("mcols")], "v")
        E("v", lambda e: e.tensor_tensor(
            out=xA[0][0:16, :], in0=xA[0][0:16, :], in1=at16[0:16, :],
            op=ALU.add), (), "v")
        note("attn_add", "v")
        yield

    attg = ph_attn_gen() if DO_ATTN else None
    order = []
    order += [("A", 0), ("MINI", 0), ("ATT", 1), ("Af", 1), ("ATT", 2),
              ("MA", 0), ("ATT", 3), ("Ar", 1), ("ATT", 4), ("Af", 2),
              ("ATT", 5), ("ATT", 6), ("MA", 1), ("ATT", 7), ("Ar", 2),
              ("MA", 2), ("ATT", 8), ("ATT", 9), ("ATT", 10), ("ATT", 11),
              ("ATT", 12), ("ATT", 13),
              ("B", 0), ("MB", 0), ("B", 1), ("A", 3),
              ("MA", 3), ("MB", 1), ("B", 2), ("A", 4), ("MA", 4), ("MB", 2),
              ("B", 3), ("A", 5), ("MA", 5), ("MB", 3), ("B", 4), ("A", 6),
              ("MA", 6), ("MB", 4), ("B", 5), ("A", 7), ("MA", 7), ("MB", 5),
              ("B", 6), ("MB", 6), ("B", 7), ("MB", 7)]
    for kind, i in order:
        if kind == "A":
            if i >= 3:
                emit_xload(i)
            ph_stage("a", i)
        elif kind == "Af":
            ph_stage("a", i, do_round=False)
        elif kind == "Ar":
            ph_round("a", i)
        elif kind == "B":
            ph_stage("b", i)
        elif kind == "MA":
            ph_mm2("a", i)
        elif kind == "MB":
            ph_mm2("b", i)
        elif kind == "MINI":
            if DO_MINI:
                ph_mini()
        elif kind == "ATT":
            if DO_ATTN:
                next(attg)

    E("t", lambda e: e.wait_ge(sems["do"], R.n("do")) if False else None) \
        if False else None
    R.st["t"].append((None, [(nm, 16 * dma_cnt[nm]) for nm in dma_pool["do"]],
                      None))

    with nc.Block() as block:
        def mk(eng_name):
            def body(eng):
                for fn, ws, inc in R.st[eng_name]:
                    for semn, val in ws:
                        eng.wait_ge(sems[semn], val)
                    if fn is None:
                        continue
                    inst = fn(eng)
                    if inc is not None:
                        semn, amt = inc
                        inst.then_inc(sems[semn], amt)
            return body

        block.tensor(mk("t"))
        block.vector(mk("v"))
        block.scalar(mk("s"))
        block.gpsimd(mk("g"))
        block.sync(mk("y"))

    es.close()
    return nc


_NC_CACHE = []


def kernel(**inputs):
    x = np.ascontiguousarray(np.asarray(inputs["L0_attn_x"], np.float32))
    w1 = np.ascontiguousarray(np.asarray(inputs["L0_ffn_w1"], np.float32))
    w2 = np.ascontiguousarray(np.asarray(inputs["L0_ffn_w2"], np.float32))
    cw = np.asarray(inputs["L0_ffn_conv_w"], np.float32)
    cb = np.asarray(inputs["L0_ffn_conv_b"], np.float32)
    ob = np.ascontiguousarray(np.asarray(inputs["L0_attn_ob"], np.float32))

    cwt = np.zeros((128, 4 * NC), np.float32)
    for k in range(3):
        cwt[:, k * NC:(k + 1) * NC] = cw[:, 0, k].reshape(NC, 128).T
    cwt[:, 3 * NC:] = cb.reshape(NC, 128).T
    qbv = np.zeros((128, 16), np.float32)
    qbv[:, 0:8] = np.asarray(inputs["L0_attn_qb"], np.float32).reshape(8, 128).T
    qbv[:, 8:16] = np.asarray(inputs["L0_attn_vb"], np.float32).reshape(8, 128).T
    lbv = np.zeros((128, 8), np.float32)
    for k, n in enumerate(("L0_attn_lqb", "L0_attn_lkb", "L0_attn_lvb",
                           "L0_attn_lnc_w", "L0_attn_lnd_w")):
        lbv[:, k] = np.tile(np.asarray(inputs[n], np.float32), 2)
    lab = np.zeros((128, 16), np.float32)
    lab[:, 0:8] = np.asarray(inputs["L0_attn_lna_w"], np.float32).reshape(8, 128).T
    lab[:, 8:16] = np.asarray(inputs["L0_attn_lnb_w"], np.float32).reshape(8, 128).T
    identf = np.eye(128, dtype=np.float32)
    identb = np.eye(128, dtype=np.float32).astype(ml_dtypes.bfloat16)
    onesf = np.ones((1, 128), np.float32)

    in_maps = []
    for core in range(8):
        b, q = core // 4, core % 4
        base = 2048 * q
        xs = np.zeros((S_TOK, D), np.float32)
        hi = min(base + S_TOK, 8192)
        xs[0:hi - base] = x[b, base:hi]
        xl = np.zeros((6, D), np.float32)
        lo2, hi2 = base - 4, base + 2
        s0, s1 = max(lo2, 0), min(hi2, 8192)
        xl[s0 - lo2:s1 - lo2] = x[b, s0:s1]
        xr = np.zeros((6, D), np.float32)
        lo2, hi2 = base + 2045, base + 2051
        s0, s1 = max(lo2, 0), min(hi2, 8192)
        xr[s0 - lo2:s1 - lo2] = x[b, s0:s1]
        xhh = np.zeros((40, D), np.float32)
        xhh[2:20] = x[b, 0:18]
        xhh[22:40] = x[1 - b, 0:18]
        maskv = np.zeros((1, 8), np.float32)
        maskv[0, 0] = 0.0 if base == 0 else 1.0
        maskv[0, 1] = 0.0 if base + 2048 >= 8192 else 1.0
        maskv[0, 2] = 1.0 if base == 0 else 0.0
        m = {
            "x_slice": xs, "x_left": xl, "x_right": xr, "x_head": xhh,
            "w1": w1, "w2": w2, "cwt": cwt, "ob": ob,
            "identf": identf, "identb": identb, "onesf": onesf,
            "onescol": np.ones((128, 1), np.float32),
            "maskv": maskv,
            "qw": np.ascontiguousarray(np.asarray(inputs["L0_attn_qw"], np.float32)),
            "kw": np.ascontiguousarray(np.asarray(inputs["L0_attn_kw"], np.float32)),
            "vw": np.ascontiguousarray(np.asarray(inputs["L0_attn_vw"], np.float32)),
            "ow": np.ascontiguousarray(np.asarray(inputs["L0_attn_ow"], np.float32)),
            "qbv": qbv, "lbv": lbv, "lab": lab,
            "lqw": np.ascontiguousarray(np.asarray(inputs["L0_attn_lqw"], np.float32)),
            "lkw": np.ascontiguousarray(np.asarray(inputs["L0_attn_lkw"], np.float32)),
            "lvw": np.ascontiguousarray(np.asarray(inputs["L0_attn_lvw"], np.float32)),
        }
        in_maps.append(m)

    nc = build_program()
    res = run_bass_kernel_spmd(nc, in_maps, core_ids=list(range(8)))
    out = np.zeros((2, 8192, D), np.float32)
    for core in range(8):
        b, q = core // 4, core % 4
        out[b, 2048 * q:2048 * (q + 1)] = res.results[core]["out"]
    return out


# version 14
# speedup vs baseline: 1.0569x; 1.0569x over previous
import contextlib
import numpy as np
import ml_dtypes
import concourse.bass as bass
import concourse.mybir as mybir
from concourse.bass_utils import run_bass_kernel_spmd

F32 = mybir.dt.float32
BF16 = mybir.dt.bfloat16
FP8 = mybir.dt.float8e4
ALU = mybir.AluOpType
AX = mybir.AxisListType
AF = mybir.ActivationFunctionType

D = 1024
HDIM = 4096
NT = 8
MMW = 258
FW = 256
NC = 32
NDC = 8
MAGIC = float(np.float32(1.5 * 2 ** 23))
EPS = 1e-5
S_TOK = 2049
DENOM = float(np.float32(2 * 16 * 8192 * 64))

DO_MINI = True
DO_ATTN = True

ENGS = ("t", "v", "s", "g", "y")


class Rec:
    def __init__(self):
        self.st = {e: [] for e in ENGS}
        self.cnt = {}
        self.lastw = {e: {} for e in ENGS}

    def do(self, eng, fn, waits, inc):
        flat = []
        for w in waits:
            if isinstance(w, list):
                flat.extend(w)
            else:
                flat.append(w)
        if eng in ("v", "s", "g"):
            flat.append((eng, self.cnt.get(eng, 0)))
            if inc is None:
                inc = (eng, 1)
        ws = []
        for sem, val in flat:
            if val > 0 and self.lastw[eng].get(sem, -1) < val:
                ws.append((sem, val))
                self.lastw[eng][sem] = val
        self.st[eng].append((fn, ws, inc))
        if inc is not None:
            sem, amt = inc
            self.cnt[sem] = self.cnt.get(sem, 0) + amt
            return self.cnt[sem]
        return None

    def n(self, sem):
        return self.cnt.get(sem, 0)


def build_program():
    nc = bass.Bass("TRN2", target_bir_lowering=False, debug=False)

    def din(name, shape, dt=F32):
        return nc.dram_tensor(name, shape, dt, kind="ExternalInput").ap()

    x_d = din("x_slice", [S_TOK, D])
    xl_d = din("x_left", [6, D])
    xr_d = din("x_right", [6, D])
    xh_d = din("x_head", [40, D])
    w1_d = din("w1", [HDIM, D])
    w2_d = din("w2", [D, HDIM])
    cwt_d = din("cwt", [128, 4 * NC])
    ob_d = din("ob", [D])
    idf_d = din("identf", [128, 128])
    idb_d = din("identb", [128, 128], BF16)
    ones_d = din("onesf", [1, 128])
    onesc_d = din("onescol", [128, 1])
    msk_d = din("maskv", [1, 8])
    aw_d = [din(n, [D, D]) for n in ("qw", "kw", "vw", "ow")]
    qbv_d = din("qbv", [128, 16])
    lbv_d = din("lbv", [128, 8])

    lab_d = din("lab", [128, 16])
    lw_d = [din(n, [64, 64]) for n in ("lqw", "lkw", "lvw")]

    out_d = nc.dram_tensor("out", [2048, D], F32, kind="ExternalOutput").ap()

    w1scr = nc.dram_tensor("w1scr", [HDIM, D], BF16, kind="Internal").ap()
    w2scr = nc.dram_tensor("w2scr", [D, HDIM], BF16, kind="Internal").ap()
    awscr = nc.dram_tensor("awscr", [4 * D, D], BF16, kind="Internal").ap()

    es = contextlib.ExitStack()
    sbt = lambda n, shp, dt=F32: es.enter_context(nc.sbuf_tensor(n, shp, dt))
    pst = lambda n, shp, dt=F32: es.enter_context(nc.psum_tensor(n, shp, dt))

    w1q = sbt("w1q", [128, NDC * HDIM], FP8)
    w2q = sbt("w2q", [128, NC * D], FP8)
    hbuf = sbt("hbuf", [128, NC * MMW])
    xqma = sbt("xqma", [128, NC * FW], BF16)
    xqmb = sbt("xqmb", [128, NC * FW], BF16)
    xA = [sbt(f"xA{r}", [128, D]) for r in range(3)]
    xB = [sbt(f"xB{r}", [128, D]) for r in range(3)]
    poolf = sbt("poolf", [128, D])
    dump = sbt("dump", [128, D])
    xq1 = sbt("xq1", [128, 2 * D], BF16)
    xq1e = sbt("xq1e", [128, D], BF16)
    xq1x = xq1e
    xq1T = sbt("xq1T", [128, NDC * MMW], BF16)
    tmpA = sbt("tmpA", [128, FW])
    tmpB = sbt("tmpB", [128, FW])
    absT = sbt("absT", [128, FW])
    Mt = sbt("Mt", [128, FW])
    bcast = sbt("bcast", [128, 352])
    obt = sbt("obt", [128, D])
    cwt = sbt("cwt_s", [128, 4 * NC])
    cols = sbt("cols", [128, 48])
    cols2 = sbt("cols2", [128, 24])
    colsh = sbt("colsh", [64, 48])
    idf = sbt("idf", [128, 128])
    idb = sbt("idb", [128, 128], BF16)
    onesf = sbt("ones_s", [1, 128])
    onescol = sbt("onescol_s", [128, 1])
    swc = sbt("swc", [128, 8])
    mskv = sbt("mskv", [1, 8])
    macc = sbt("macc", [128, 2])
    arow = sbt("arow", [1, 64])
    a2row = sbt("a2row", [1, MMW])
    wstg = sbt("wstg", [128, 2 * D], BF16)
    xh = sbt("xh", [64, D])
    scrA = sbt("scrA", [64, D])
    scrB = sbt("scrB", [64, D])
    xqh = sbt("xqh", [64, D], BF16)
    vib = xqmb[0:16, 3840:5888]
    at16 = xh
    xqhT = sbt("xqhT", [128, NDC * 40], BF16)
    xqmh_big = sbt("xqmh_big", [128, NC * 40 + 2 * 320], BF16)
    xqmh = xqmh_big[:, 0:NC * 40]
    fmb = xqmh_big[:, NC * 40:NC * 40 + 640]
    bdg = sbt("bdg", [128, 3 * 128], BF16)
    lbv = sbt("lbv_s", [128, 8])
    qbv = sbt("qbv_s", [128, 16])
    lab = sbt("lab_s", [128, 16])

    habc = sbt("habc", [64, 80])
    absh = sbt("absh", [128, 40])
    Mh = sbt("Mh", [128, 40])

    xqmTm = sbt("xqmTm", [128, NDC * 12], BF16)

    xqm_m = sbt("xqm_m", [128, NC * 2], BF16)
    PTb = None

    PTb = wstg[0:16, 0:512]
    xqmb_fv = xqmb[:, :].bitcast(F32)
    fmp = xqmb_fv[:, 0:1920]
    hh = xqmb_fv[:, 1920:3200]
    hm2 = xqmb_fv[:, 3200:3584]
    minq = xqh
    fmini = scrA
    minx = wstg[:, :].bitcast(F32)
    pm1 = [pst(f"pm1_{i}", [128, 512]) for i in range(3)]
    pu = pst("pu", [128, 1024], BF16)
    pm2 = [pst("pm2a", [128, 1024]), pst("pm2b", [128, 1024])]
    puf = pu[:, :].bitcast(F32)

    UF0 = 129
    SP0 = 385

    sems = {}
    DMA_K = {"dx": 4, "dw": 6, "da": 4, "do": 4, "dg": 4}
    dma_pool = {}
    dma_cnt = {}
    dma_rr = {}
    for cls, K in DMA_K.items():
        dma_pool[cls] = []
        for i in range(K):
            nm = f"{cls}{i}"
            sems[nm] = es.enter_context(nc.semaphore(nm))
            dma_pool[cls].append(nm)
            dma_cnt[nm] = 0
        dma_rr[cls] = 0
    for n in ("t", "v", "s", "g"):
        sems[n] = es.enter_context(nc.semaphore(n))

    R = Rec()
    ev = {}

    def E(eng, fn, w=(), inc=None, amt=1):
        return R.do(eng, fn, w, (inc, amt) if inc else None)

    def note(name, semname):
        ev[name] = (semname, R.n(semname))

    def wt(name):
        return ev[name]

    def gnote(name, parts):
        acc = []
        for p in parts:
            v = ev[p]
            if isinstance(v, list):
                acc.extend(v)
            else:
                acc.append(v)
        ev[name] = acc

    def dma(out, in_, w=(), sem_="dw", tp=False, eng="y", note_name=None):
        cls = sem_
        i = dma_rr[cls] % DMA_K[cls]
        dma_rr[cls] += 1
        nm = dma_pool[cls][i]
        ws = list(w)
        if dma_cnt[nm] > 0:
            ws.append((nm, 16 * dma_cnt[nm]))
        if tp:
            E(eng, lambda e, o=out, i2=in_: e.dma_start(out=o, in_=i2,
                                                        transpose=True), ws, nm, 16)
        else:
            E(eng, lambda e, o=out, i2=in_: e.dma_start(out=o, in_=i2), ws, nm, 16)
        dma_cnt[nm] += 1
        if note_name is not None:
            ev[note_name] = (nm, 16 * dma_cnt[nm])

    p64_user = [None]
    p0_user = [None]
    p32_user = [None]

    def emit_xload(i):
        r = i % 3
        w0 = 256 * i
        wl = [wt(f"xfree{i - 3}")] if i >= 3 else []
        if i >= 3:
            wl.append(wt(f"b{i - 2}_efill"))
        if i >= 4:
            wl.append(wt(f"b{i - 4}_xfill"))
        dma(xA[r][:, :], x_d[w0:w0 + 128, :], wl, "dx", note_name=f"xload{i}_a")
        dma(xB[r][:, :], x_d[w0 + 128:w0 + 256, :], wl, "dx", note_name=f"xload{i}_b")
        gnote(f"xload{i}", [f"xload{i}_a", f"xload{i}_b"])

    for i in range(3):
        emit_xload(i)
    dma(minx[0:6, :], xl_d[:, :], (), "dx", note_name="xs_l")
    dma(minx[32:38, :], xr_d[:, :], (), "dx", note_name="xs_r")
    dma(xh[0:40, :], xh_d[:, :], (), "dx", note_name="xs_h")
    gnote("xsmall", ["xs_l", "xs_r", "xs_h"])
    cparts = []
    for j, (t_, d_) in enumerate([(obt, None), (idf, idf_d), (idb, idb_d),
                                  (onesf, ones_d), (onescol, onesc_d),
                                  (mskv, msk_d), (cwt, cwt_d), (qbv, qbv_d),
                                  (lbv, lbv_d), (lab, lab_d)]):
        if d_ is None:
            dma(obt[:, :], ob_d.unsqueeze(0).partition_broadcast(128),
                note_name=f"cst{j}")
        else:
            dma(t_[:, :], d_[:, :], note_name=f"cst{j}")
        cparts.append(f"cst{j}")
    gnote("consts", cparts)
    E("v", lambda e: e.tensor_tensor(out=lbv[:, 5:6], in0=lbv[:, 3:4],
                                     in1=lbv[:, 4:5], op=ALU.mult),
      [wt("consts")], "v")
    note("gcol", "v")
    for kk, dst in ((2, 4), (0, 5), (1, 6)):
        E("t", lambda e, k=kk, d_=dst: e.matmul(
            puf[:, SP0 + d_:SP0 + d_ + 1], onesf[0:1, :], mskv[0:1, k:k + 1],
            start=True, stop=True), [wt("consts")], "t")
    note("mbc", "t")
    E("v", lambda e: e.tensor_copy(out=swc[:, 4:7], in_=puf[:, SP0 + 4:SP0 + 7]),
      [wt("mbc")], "v")
    note("mcols", "v")
    for t_ in (cols, cols2, colsh, habc, arow, a2row, Mh, Mt, absT, tmpA, tmpB):
        E("v", lambda e, t2=t_: e.memset(t2[:, :], 0.0), (), "v")
    note("zinit", "v")

    xqmb_f = xqmb[:, :].bitcast(F32)

    def emit_wquant_gen(which):
        wd = w1_d if which == 0 else w2_d
        scr = w1scr if which == 0 else w2scr
        R_, C_ = (HDIM, D) if which == 0 else (D, HDIM)
        CH = D
        nch, nsub = R_ // 128, C_ // CH
        NKC = nch * nsub
        NS = 6 if which == 0 else 4
        if which == 0:
            stg = [hbuf[:, s * CH:(s + 1) * CH] for s in range(NS)]
        else:
            stg = [xqmb_f[:, s * CH:(s + 1) * CH] for s in range(NS)]
        if which == 0:
            bst = [xqma[:, s * CH:(s + 1) * CH] for s in range(4)]
        else:
            bst = [xq1[:, 0:CH], xq1[:, CH:2 * CH]]
        NB_ = len(bst)
        E("v", lambda e, w_=which: e.memset(macc[:, w_:w_ + 1], 0.0), (), "v")
        note(f"wm0_{which}", "v")
        for k in range(NKC):
            ci, si = divmod(k, nsub)
            sl = stg[k % NS]
            wl = [wt(f"wred{which}_{k - NS}")] if k >= NS else []
            dma(sl, wd[ci * 128:(ci + 1) * 128, si * CH:(si + 1) * CH], wl,
                "dw" if which == 0 else "dg",
                eng="y" if which == 0 else "g", note_name=f"wld{which}_{k}")
            E("v", lambda e, s=sl: e.reduce_sum(
                out=tmpA[:, 0:1], in_=s, axis=AX.X, apply_absolute_value=True),
              [wt(f"wld{which}_{k}"), wt(f"wm0_{which}")] +
              ([wt(f"wred{which}_{k - 1}")] if k else []), "v")
            note(f"wabs{which}_{k}", "v")
            E("v", lambda e, w_=which: e.tensor_tensor(
                out=macc[:, w_:w_ + 1], in0=macc[:, w_:w_ + 1],
                in1=tmpA[:, 0:1], op=ALU.add), [wt(f"wabs{which}_{k}")], "v")
            note(f"wred{which}_{k}", "v")
            yield
        E("t", lambda e, w_=which: e.matmul(
            puf[0:1, SP0 + 8 + w_:SP0 + 9 + w_], macc[:, w_:w_ + 1],
            onescol[:, :], start=True, stop=True),
          [wt(f"wred{which}_{NKC - 1}")], "t")
        note(f"wsum{which}", "t")
        E("v", lambda e, w_=which, rc=float(R_) * float(C_): e.tensor_scalar(
            out=arow[0:1, w_:w_ + 1], in0=puf[0:1, SP0 + 8 + w_:SP0 + 9 + w_],
            scalar1=1.0 / rc, scalar2=1e-5, op0=ALU.mult, op1=ALU.max),
          [wt(f"wsum{which}")], "v")
        note(f"wmean{which}", "v")
        E("v", lambda e, w_=which: e.reciprocal(
            out=arow[0:1, 4 + w_:5 + w_], in_=arow[0:1, w_:w_ + 1]),
          [wt(f"wmean{which}")], "v")
        note(f"wrec{which}", "v")
        E("t", lambda e, w_=which: e.matmul(
            puf[:, SP0 + 10:SP0 + 11], onesf[0:1, :], arow[0:1, 4 + w_:5 + w_],
            start=True, stop=True), [wt(f"wrec{which}")])
        E("t", lambda e, w_=which: e.matmul(
            puf[:, SP0 + 11:SP0 + 12], onesf[0:1, :], arow[0:1, w_:w_ + 1],
            start=True, stop=True), (), "t")
        note(f"wbc{which}", "t")
        E("v", lambda e, w_=which: e.tensor_copy(
            out=swc[:, w_:w_ + 1], in_=puf[:, SP0 + 10:SP0 + 11]), [wt(f"wbc{which}")])
        if which == 1:
            E("v", lambda e: e.tensor_scalar(
                out=swc[:, 2:3], in0=puf[:, SP0 + 11:SP0 + 12],
                scalar1=0.5 / 127.0, scalar2=None, op0=ALU.mult), (), "v")
        else:
            E("v", lambda e: e.tensor_copy(
                out=swc[:, 3:4], in_=puf[:, SP0 + 11:SP0 + 12]), (), "v")
        note(f"wsw{which}", "v")
        for k in range(NKC):
            ci, si = divmod(k, nsub)
            sl, b_ = stg[k % NS], bst[k % NB_]
            wl = [wt(f"wq{which}_{k - NS}") if k >= NS else
                  wt(f"wred{which}_{NKC - 1}")]
            if k >= NB_:
                wl.append(wt(f"wbst{which}_{k - NB_}"))
            dma(sl, wd[ci * 128:(ci + 1) * 128, si * CH:(si + 1) * CH], wl,
                "dw" if which == 0 else "dg",
                eng="y" if which == 0 else "g", note_name=f"wld2{which}_{k}")
            E("v", lambda e, s=sl, w_=which: e.tensor_scalar(
                out=s, in0=s, scalar1=swc[:, w_:w_ + 1], scalar2=MAGIC,
                op0=ALU.mult, op1=ALU.add),
              [wt(f"wld2{which}_{k}"), wt(f"wsw{which}")], "v")
            note(f"wqa{which}_{k}", "v")
            E("v", lambda e, s=sl: e.tensor_scalar(
                out=s, in0=s, scalar1=MAGIC, scalar2=1.0,
                op0=ALU.subtract, op1=ALU.min), [wt(f"wqa{which}_{k}")], "v")
            note(f"wqb{which}_{k}", "v")
            E("v", lambda e, s=sl, b=b_: e.tensor_scalar(
                out=b, in0=s, scalar1=-1.0, scalar2=None, op0=ALU.max),
              [wt(f"wqb{which}_{k}")], "v")
            note(f"wq{which}_{k}", "v")
            dma(scr[ci * 128:(ci + 1) * 128, si * CH:(si + 1) * CH], b_,
                [wt(f"wq{which}_{k}")], "dw" if which == 0 else "dg",
                eng="y" if which == 0 else "g", note_name=f"wbst{which}_{k}")
            yield
        gnote(f"wscr{which}", [f"wbst{which}_{k}" for k in range(NKC)])
        tgt = w1q if which == 0 else w2q
        nslab = C_ // 128
        hb16 = hbuf[:, :].bitcast(BF16)
        if which == 0:
            ts_ = [hb16[:, s * R_:(s + 1) * R_] for s in range(4)]
        else:
            ts_ = [xqma[:, 4 * CH + s * R_:4 * CH + (s + 1) * R_]
                   for s in range(4)]
        for s_i in range(nslab):
            tstg = ts_[s_i % 4]
            wl = [wt(f"wscr{which}")]
            if which == 0:
                wl.append(wt("wld1_%d" % (4 * nsub)
                             if False else f"wred{which}_{NKC - 1}"))
            if s_i >= 4:
                wl.append(wt(f"wq8{which}_{s_i - 4}"))
            dma(tstg, scr[:, s_i * 128:(s_i + 1) * 128], wl,
                "dw" if s_i % 2 == 0 else "da", tp=True,
                note_name=f"wtp{which}_{s_i}")
            E("v", lambda e, t_=tstg, o=tgt[:, s_i * R_:(s_i + 1) * R_]:
              e.tensor_copy(out=o, in_=t_), [wt(f"wtp{which}_{s_i}")], "v")
            note(f"wq8{which}_{s_i}", "v")
            yield
        note(f"wdone{which}", "v")

    _wg = [emit_wquant_gen(0), emit_wquant_gen(1)]
    while _wg:
        for g_ in list(_wg):
            try:
                next(g_)
            except StopIteration:
                _wg.remove(g_)

    if DO_ATTN:
        xqma_f = xqma[:, :].bitcast(F32)
        poolf_b = poolf[:, :].bitcast(BF16)
        for wi in range(4):
            for ci in range(8):
                k = wi * 8 + ci
                stg = xqma_f[:, (k % 2) * D:(k % 2) * D + D]
                bst = poolf_b[:, (k % 2) * D:(k % 2) * D + D]
                wl = [wt("wscr0")]
                if k >= 2:
                    wl.append(wt(f"awcast{k - 2}"))
                dma(stg, aw_d[wi][ci * 128:(ci + 1) * 128, :], wl, "da", note_name=f"awld{k}")
                E("v", lambda e, s=stg, b=bst: e.tensor_copy(out=b, in_=s),
                  [wt(f"awld{k}")] +
                  ([wt(f"awst{k - 2}")] if k >= 2 else [wt("wscr0")]), "v")
                note(f"awcast{k}", "v")
                dma(awscr[wi * D + ci * 128:wi * D + (ci + 1) * 128, :], bst,
                    [wt(f"awcast{k}")], "da", note_name=f"awst{k}")
        note("awq_done", "v")
        gnote("awscr_done", [f"awst{k}" for k in range(32)])
        gnote("awpf", ["awst30", "awst31"])
        p0_user[0] = "awpf"
        p32_user[0] = "awpf"
        p64_user[0] = "awpf"

    CB = {"E": 0, "A": 12, "B": 24, "X": 36}

    def vch(fn, w=()):
        E("v", fn, list(w) + [("v", R.n("v"))], "v")

    def emit_qin(tag, sets, ct, srcdeps):
        srcdeps = list(srcdeps) + [wt("zinit")]
        prev_sq = None
        for (src, p0, n_, cb, dst, pstg, mcol) in sets:
            E("v", lambda e, s=src, c=ct[p0:p0 + n_, cb:cb + 1]:
              e.tensor_reduce(out=c, in_=s, axis=AX.X, op=ALU.add), srcdeps, "v")
            E("v", lambda e, s=src, c=ct[p0:p0 + n_, cb + 3:cb + 4]:
              e.tensor_reduce(out=c, in_=s, axis=AX.X, op=ALU.max), (), "v")
            E("v", lambda e, s=src, c=ct[p0:p0 + n_, cb + 4:cb + 5]:
              e.tensor_reduce(out=c, in_=s, axis=AX.X, op=ALU.min), (), "v")
            E("s", lambda e, s=src, c=ct[p0:p0 + n_, cb + 1:cb + 2], pg=pstg:
              e.activation(out=pg, in_=s, func=AF.Square, accum_out=c),
              list(srcdeps) + ([wt(prev_sq)] if prev_sq else []), "s")
            note(f"{tag}_sq_{cb}", "s")
            prev_sq = f"{tag}_sq_{cb}"
        note(f"{tag}_red", "v")
        note(f"{tag}_sq", "s")
        gdep = [wt(f"{tag}_red"), wt(f"{tag}_sq"), wt("zinit")]
        c = lambda k, _t=ct: _t[:, k::12] if _t.shape[0] == 128 else _t[0:_t.shape[0], k::12]
        cc = lambda k, _t=ct: _t[0:64, k::12] if _t is colsh else _t[:, k::12]
        c = cc
        vch(lambda e: e.tensor_scalar(out=c(0), in0=c(0), scalar1=1.0 / D,
                                      scalar2=None, op0=ALU.mult), gdep)
        vch(lambda e: e.tensor_scalar(out=c(1), in0=c(1), scalar1=1.0 / D,
                                      scalar2=None, op0=ALU.mult))
        vch(lambda e: e.tensor_tensor(out=c(11), in0=c(0), in1=c(0), op=ALU.mult))
        vch(lambda e: e.tensor_tensor(out=c(2), in0=c(1), in1=c(11), op=ALU.subtract))
        vch(lambda e: e.tensor_scalar(out=c(2), in0=c(2), scalar1=EPS,
                                      scalar2=None, op0=ALU.add))
        vch(lambda e: e.tensor_tensor(out=c(3), in0=c(3), in1=c(0), op=ALU.subtract))
        vch(lambda e: e.tensor_tensor(out=c(4), in0=c(0), in1=c(4), op=ALU.subtract))
        vch(lambda e: e.tensor_tensor(out=c(4), in0=c(3), in1=c(4), op=ALU.max))
        note(f"{tag}_pre", "v")
        E("s", lambda e: e.activation(out=c(5), in_=c(2), func=AF.Sqrt),
          [wt(f"{tag}_pre")], "s")
        note(f"{tag}_sqrt", "s")
        vch(lambda e: e.reciprocal(out=c(11), in_=c(5)), [wt(f"{tag}_sqrt")])
        for _ in range(2):
            vch(lambda e: e.tensor_tensor(out=c(5), in0=c(2), in1=c(11), op=ALU.mult))
            vch(lambda e: e.tensor_tensor(out=c(5), in0=c(5), in1=c(11), op=ALU.mult))
            vch(lambda e: e.tensor_scalar(out=c(5), in0=c(5), scalar1=-0.5,
                                          scalar2=1.5, op0=ALU.mult, op1=ALU.add))
            vch(lambda e: e.tensor_tensor(out=c(11), in0=c(11), in1=c(5), op=ALU.mult))
        vch(lambda e: e.tensor_tensor(out=c(6), in0=c(11), in1=c(4), op=ALU.mult))
        vch(lambda e: e.tensor_scalar(out=c(6), in0=c(6), scalar1=1e-5,
                                      scalar2=None, op0=ALU.max))
        vch(lambda e: e.tensor_scalar(out=c(9), in0=c(6), scalar1=1.0 / 127.0,
                                      scalar2=None, op0=ALU.mult))
        pn = 64 if ct is colsh else 128
        vch(lambda e, p_=pn: e.tensor_scalar(
            out=c(9), in0=c(9), scalar1=swc[0:p_, 3:4], scalar2=None,
            op0=ALU.mult))
        vch(lambda e: e.reciprocal(out=c(10), in_=c(6)))
        vch(lambda e: e.tensor_tensor(out=c(7), in0=c(11), in1=c(10), op=ALU.mult))
        vch(lambda e: e.tensor_scalar(out=c(7), in0=c(7), scalar1=127.0,
                                      scalar2=None, op0=ALU.mult))
        vch(lambda e: e.tensor_tensor(out=c(8), in0=c(0), in1=c(7), op=ALU.mult))
        vch(lambda e: e.tensor_scalar(out=c(8), in0=c(8), scalar1=-1.0,
                                      scalar2=None, op0=ALU.mult))
        note(f"{tag}_cols", "v")
        prev_rn = None
        for (src, p0, n_, cb, dst, pstg, mcol) in sets:
            E("s", lambda e, s=src, p=p0, n2=n_, b=cb, pg=pstg:
              e.activation(out=pg, in_=s, func=AF.Identity,
                           scale=ct[p:p + n2, b + 7:b + 8],
                           bias=ct[p:p + n2, b + 8:b + 9]),
              [wt(f"{tag}_cols")] + ([wt(prev_rn)] if prev_rn else []), "s")
            note(f"{tag}_p0_{cb}", "s")
            E("v", lambda e, d=dst, pg=pstg: e.tensor_scalar(
                out=d, in0=pg, scalar1=MAGIC, scalar2=MAGIC,
                op0=ALU.add, op1=ALU.subtract), [wt(f"{tag}_p0_{cb}")], "v")
            if mcol is not None:
                E("v", lambda e, d=dst, m=mcol: e.tensor_scalar(
                    out=d, in0=d, scalar1=m, scalar2=None, op0=ALU.mult), (), "v")
            note(f"{tag}_rn_{cb}", "v")
            prev_rn = f"{tag}_rn_{cb}"
        note(f"{tag}_done", "v")

    pu_read = [None]

    def pu_wait():
        return [wt(pu_read[0])] if pu_read[0] else []

    def pu_set(name):
        pu_read[0] = name

    def emit_bcast(tag, pieces, width, extradep=(), dst0=0):
        prev = None
        for (ct_, p0, n_, colk, dstc) in pieces:
            wl = list(extradep) + pu_wait()
            if prev is not None:
                wl.append(wt(prev))
            E("t", lambda e, _t=ct_, p=p0, n2=n_, k=colk: e.transpose(
                puf[0:1, UF0:UF0 + n2] if n2 == 1 else puf[0:n_, UF0:UF0 + n_]
                if False else puf[0:1, UF0:UF0 + n2],
                _t[p:p + n2, k:k + 1], idf[p:p + n2, p:p + n2]),
              wl, "t")
            note(f"{tag}_t_{dstc}", "t")
            E("s", lambda e, d=dstc, n2=n_: e.copy(
                out=a2row[0:1, d:d + n2], in_=puf[0:1, UF0:UF0 + n2]),
              [wt(f"{tag}_t_{dstc}")], "s")
            note(f"{tag}_c_{dstc}", "s")
            prev = f"{tag}_c_{dstc}"
        note(f"{tag}_row", "s")
        E("t", lambda e, w_=width: e.matmul(
            puf[:, UF0:UF0 + w_], onesf[0:1, :], a2row[0:1, 0:w_],
            start=True, stop=True), [wt(f"{tag}_row")], "t")
        note(f"{tag}_mm", "t")
        E("v", lambda e, w_=width, d0=dst0: e.tensor_copy(
            out=bcast[:, d0:d0 + w_], in_=puf[:, UF0:UF0 + w_]),
          [wt(f"{tag}_mm")], "v")
        note(f"{tag}_bc", "v")
        pu_set(f"{tag}_bc")

    prev_round = {}
    prev_ma = {"a": None, "b": None}
    pm2_read = [None, None]

    def hsl(c):
        return hbuf[:, c * MMW:(c + 1) * MMW]

    def ph_stage(st, i, do_round=True):
        tag = f"{st}{i}"
        xq1a = xq1[:, 0:D]
        xq1b = xq1[:, D:2 * D]
        er = 0 if st == "a" else 32
        cb2 = (i % 2) * 8 if st == "a" else 16
        xqm = xqma if st == "a" else xqmb
        if st == "a":
            wl = [wt(p0_user[0])] if p0_user[0] else []
            if i == 0:
                dma(poolf[0:1, :], xl_d[3:4, :], wl, "dx",
                    note_name=f"{tag}_efill")
            else:
                dma(poolf[0:1, :], x_d[256 * i - 1:256 * i, :], wl, "dx", note_name=f"{tag}_efill")
            p0_user[0] = f"{tag}_qdone"
            wl = [wt(p64_user[0])] if p64_user[0] else []
            dma(poolf[64:65, :], x_d[256 * i + 256:256 * i + 257, :], wl, "dx", note_name=f"{tag}_xfill")
            p64_user[0] = f"{tag}_qdone"
        else:
            if i == 0:
                wl = [wt(p32_user[0])] if p32_user[0] else []
                wl.append(wt("mini_done") if DO_MINI else wt(f"a0_res"))
                dma(poolf[32:33, :], xh[62:63, :], wl, "dg", eng="g",
                    note_name=f"{tag}_efill")
            p32_user[0] = f"{tag}_qdone"
            wl = [wt(p64_user[0])] if p64_user[0] else []
            if i == NT - 1:
                src = xh[63:64, :]
                wl.append(wt("mini_done") if DO_MINI else wt(f"a{i}_res"))
            else:
                src = xA[(i + 1) % 3][0:1, :]
                wl.append(wt(f"a{i + 1}_res"))
            dma(poolf[64:65, :], src, wl, "dg", eng="g",
                note_name=f"{tag}_xfill")
            p64_user[0] = f"{tag}_qdone"
        srcdeps = [wt(f"{tag}_efill"), wt(f"{tag}_xfill")]
        if st == "a":
            srcdeps.append(wt(f"xload{i}"))
        else:
            srcdeps.append(wt(f"a{i}_res"))
            if i == 0 and DO_ATTN:
                srcdeps.append(wt("attn_add"))
        emcol = swc[er:er + 1, 5:6] if (st == "b" and i == 0) else None
        xmcol = swc[64:65, 6:7] if (st == "b" and i == NT - 1) else None
        sets = [
            (poolf[er:er + 1, :], er, 1, CB["E"], xq1e[er:er + 1, :],
             dump[er:er + 1, :], emcol),
            (xA[i % 3][:, :], 0, 128, CB["A"], xq1a, dump[:, :], None),
            (xB[i % 3][:, :], 0, 128, CB["B"], xq1b, dump[:, :], None),
            (poolf[64:65, :], 64, 1, CB["X"], xq1x[64:65, :],
             dump[64:65, :], xmcol),
        ]
        sets[1] = (xA[i % 3][:, :], 0, 128, CB["A"], xq1a, dump[:, :], None)
        emit_qin(tag, sets, cols, srcdeps)
        note(f"{tag}_qdone", "v")
        emit_bcast(f"{tag}_al", [
            (cols, er, 1, CB["E"] + 9, 0),
            (cols, 0, 128, CB["A"] + 9, 1),
            (cols, 0, 128, CB["B"] + 9, 129),
            (cols, 64, 1, CB["X"] + 9, 257),
        ], MMW, extradep=[wt(f"{tag}_cols")])
        for dc in range(NDC):
            tdep = pu_wait()
            if dc >= 1:
                tdep = [wt(f"{tag}_tc_{dc - 1}")]
            E("t", lambda e, s=xq1a[:, dc * 128:(dc + 1) * 128]:
              e.transpose(pu[:, 0:128], s, idb[:, :]),
              tdep + [wt(f"{tag}_done")])
            E("t", lambda e, s=xq1b[:, dc * 128:(dc + 1) * 128]:
              e.transpose(pu[:, 128:256], s, idb[:, :]))
            E("t", lambda e, s=xq1e[er:er + 1, dc * 128:(dc + 1) * 128]:
              e.transpose(pu[:, 256:257], s, idb[er:er + 1, er:er + 1]))
            E("t", lambda e, s=xq1x[64:65, dc * 128:(dc + 1) * 128]:
              e.transpose(pu[:, 258:259], s, idb[64:65, 64:65]), (), "t")
            note(f"{tag}_tp_{dc}", "t")
            E("s", lambda e, d=dc: e.copy(
                out=xq1T[:, d * MMW + 1:d * MMW + 257], in_=pu[:, 0:256]),
              [wt(f"{tag}_tp_{dc}")], "s")
            E("s", lambda e, d=dc: e.copy(
                out=xq1T[:, d * MMW:d * MMW + 1], in_=pu[:, 256:257]))
            E("s", lambda e, d=dc: e.copy(
                out=xq1T[:, d * MMW + 257:d * MMW + 258], in_=pu[:, 258:259]),
              (), "s")
            note(f"{tag}_tc_{dc}", "s")
        pu_set(f"{tag}_tc_{NDC - 1}")
        note(f"{tag}_T", "s")
        pm1b = [pm1[0][:, 0:MMW], pm1[1][:, 0:MMW], pm1[2][:, 0:MMW],
                pm2[0][:, 0:MMW], pm2[0][:, 512:512 + MMW], pm2[1][:, 0:MMW]]
        tA = [tmpA[:, 0:FW], absT[:, 0:FW]]
        tB = [tmpB[:, 0:FW], dump[:, 0:FW]]
        for k in range(NC + 3):
            if k < NC:
                cch = k
                wl = [wt(f"{tag}_T"), wt(f"{tag}_al_bc")]
                if cch >= 6:
                    wl.append(wt(f"{tag}_ev_{cch - 6}"))
                if cch in (3, 4) and pm2_read[0]:
                    wl.append(wt(pm2_read[0]))
                if cch == 5 and pm2_read[1]:
                    wl.append(wt(pm2_read[1]))
                for dc in range(NDC):
                    E("t", lambda e, c2=cch, d=dc: e.matmul(
                        pm1b[c2 % 6],
                        w1q[:, d * HDIM + c2 * 128:d * HDIM + (c2 + 1) * 128],
                        xq1T[:, d * MMW:(d + 1) * MMW],
                        start=(d == 0), stop=(d == NDC - 1)),
                      wl if dc == 0 else (), "t" if dc == NDC - 1 else None)
                note(f"{tag}_mm1_{cch}", "t")
                wl = [wt(f"{tag}_mm1_{cch}")]
                if prev_round.get(cch):
                    wl.append(wt(prev_round[cch]))
                E("s", lambda e, c2=cch: e.copy(out=hsl(c2),
                                                in_=pm1b[c2 % 6]),
                  wl, "s")
                note(f"{tag}_ev_{cch}", "s")
            if 0 <= k - 1 < NC:
                cch = k - 1
                E("g", lambda e, c2=cch: e.tensor_tensor(
                    out=hsl(c2), in0=hsl(c2), in1=bcast[:, 0:MMW], op=ALU.mult),
                  [wt(f"{tag}_ev_{cch}"), wt(f"{tag}_al_bc")], "g")
                note(f"{tag}_hs_{cch}", "g")
                wl = [wt(f"{tag}_hs_{cch}")]
                if cch >= 2:
                    wl.append(wt(f"{tag}_B_{cch - 2}"))
                E("s", lambda e, c2=cch: e.activation(
                    out=tA[c2 % 2],
                    in_=hsl(c2)[:, 1:257], func=AF.Identity,
                    scale=cwt[:, NC + c2:NC + c2 + 1],
                    bias=cwt[:, 3 * NC + c2:3 * NC + c2 + 1]), wl, "s")
                note(f"{tag}_A_{cch}", "s")
            if 0 <= k - 2 < NC:
                cch = k - 2
                E("v", lambda e, c2=cch: e.scalar_tensor_tensor(
                    out=tB[c2 % 2], in0=hsl(c2)[:, 0:256],
                    scalar=cwt[:, c2:c2 + 1],
                    in1=tA[c2 % 2],
                    op0=ALU.mult, op1=ALU.add), [wt(f"{tag}_A_{cch}")], "v")
                note(f"{tag}_B_{cch}", "v")
                E("v", lambda e, c2=cch: e.scalar_tensor_tensor(
                    out=hsl(c2)[:, 0:256], in0=hsl(c2)[:, 2:258],
                    scalar=cwt[:, 2 * NC + c2:2 * NC + c2 + 1],
                    in1=tB[c2 % 2], op0=ALU.mult, op1=ALU.add), (), "v")
                note(f"{tag}_C_{cch}", "v")
            if 0 <= k - 3 < NC:
                cch = k - 3
                E("s", lambda e, c2=cch: e.activation(
                    out=hsl(c2)[:, 0:256], in_=hsl(c2)[:, 0:256], func=AF.Silu),
                  [wt(f"{tag}_C_{cch}")], "s")
                note(f"{tag}_si_{cch}", "s")
        E("v", lambda e: e.tensor_reduce(
            out=Mt[:, 0:FW],
            in_=hbuf[:, :].rearrange("p (c w) -> p w c", w=MMW)[:, 0:FW, :],
            axis=AX.X, op=ALU.max, apply_absolute_value=True),
          [wt(f"{tag}_si_{NC - 1}")], "v")
        note(f"{tag}_Mt", "v")
        for g in range(2):
            E("t", lambda e, g2=g: e.transpose(
                puf[:, UF0:UF0 + 128], Mt[:, g2 * 128:(g2 + 1) * 128], idf[:, :]),
              [wt(f"{tag}_Mt")] + pu_wait() +
              ([wt(f"{tag}_mxr_{g - 1}")] if g else []), "t")
            note(f"{tag}_mxt_{g}", "t")
            E("v", lambda e, g2=g: e.reduce_max(
                out=cols2[:, cb2 + g2 * 4:cb2 + g2 * 4 + 1],
                in_=puf[:, UF0:UF0 + 128],
                axis=AX.X), [wt(f"{tag}_mxt_{g}")], "v")
            note(f"{tag}_mxr_{g}", "v")
        pu_set(f"{tag}_mxr_1")
        c2s = lambda k: cols2[:, cb2 + k:cb2 + 8:4]
        vch(lambda e: e.tensor_scalar(out=c2s(0), in0=c2s(0), scalar1=1e-5,
                                      scalar2=None, op0=ALU.max),
            [wt(f"{tag}_mxr_1")])
        vch(lambda e: e.tensor_tensor(out=c2s(2), in0=c2s(0), in1=swc[:, 2:3]
                                      .broadcast_to([128, 2]), op=ALU.mult))
        vch(lambda e: e.reciprocal(out=c2s(1), in0=c2s(0)) if False else
            e.reciprocal(out=c2s(1), in_=c2s(0)))
        vch(lambda e: e.tensor_scalar(out=c2s(1), in0=c2s(1), scalar1=127.0,
                                      scalar2=None, op0=ALU.mult))
        note(f"{tag}_s2", "v")
        emit_bcast(f"{tag}_s2", [
            (cols2, 0, 128, cb2 + 1, 0),
            (cols2, 0, 128, cb2 + 5, 128),
        ], FW, extradep=[wt(f"{tag}_s2"), wt(f"{tag}_hs_{NC - 1}")])
        if do_round:
            ph_round(st, i)

    def ph_round(st, i):
        tag = f"{st}{i}"
        xqm = xqma if st == "a" else xqmb
        xqm_dep = [wt(prev_ma[st])] if prev_ma[st] else []
        h3 = hbuf[:, :].rearrange("p (c w) -> p c w", w=MMW)
        for cch in range(NC):
            E("g", lambda e, c2=cch: e.tensor_tensor(
                out=hsl(c2)[:, 0:256], in0=hsl(c2)[:, 0:256],
                in1=bcast[:, 0:FW], op=ALU.mult),
              [wt(f"{tag}_s2_bc")] +
              ([wt(f"{tag}_Mt")] if cch == 0 else []), "g")
            note(f"{tag}_sc_{cch}", "g")
        for j in range(4):
            E("v", lambda e, j2=j, xq=xqm: e.tensor_scalar(
                out=xq[:, j2 * 8 * FW:(j2 + 1) * 8 * FW]
                .rearrange("p (c w) -> p c w", w=FW),
                in0=h3[:, j2 * 8:(j2 + 1) * 8, 0:256],
                scalar1=MAGIC, scalar2=MAGIC, op0=ALU.add, op1=ALU.subtract),
              [wt(f"{tag}_sc_{j * 8 + 7}")] + xqm_dep, "v")
            note(f"{tag}_rn_{j}", "v")
            for c in range(j * 8, j * 8 + 8):
                prev_round[c] = f"{tag}_rn_{j}"
        note(f"{tag}_round", "v")

    def ph_mm2(st, i):
        tag = f"{st}{i}"
        cb2 = (i % 2) * 8 if st == "a" else 16
        xqm = xqma if st == "a" else xqmb
        for g in range(2):
            wl = [wt(f"{tag}_round")]
            if pm2_read[g]:
                wl.append(wt(pm2_read[g]))
            for jc in range(NC):
                for nh in range(2):
                    E("t", lambda e, g2=g, j=jc, n_=nh, xq=xqm: e.matmul(
                        pm2[g2][:, n_ * 512:(n_ + 1) * 512],
                        xq[:, j * FW + g2 * 128:j * FW + g2 * 128 + 128],
                        w2q[:, j * D + n_ * 512:j * D + (n_ + 1) * 512],
                        start=(j == 0), stop=(j == NC - 1)),
                      wl if (jc == 0 and nh == 0) else (),
                      "t" if (jc == NC - 1 and nh == 1) else None)
            note(f"{tag}_mm2_{g}", "t")
        prev_ma[st] = f"{tag}_mm2_1"
        if st == "b" and i + 1 < NT:
            dma(poolf[32:33, :], xB[i % 3][127:128, :],
                [wt(p32_user[0]), wt(f"a{i}_res")], "dg", eng="g",
                note_name=f"b{i + 1}_efill")
        for g in range(2):
            xg = xA[i % 3] if g == 0 else xB[i % 3]
            rwl = [wt(f"{tag}_mm2_{g}"), wt(f"{tag}_s2")]
            if st == "b" and g == 1 and i + 1 < NT:
                rwl.append(wt(f"b{i + 1}_efill"))
            E("v", lambda e, g2=g, x=xg: e.scalar_tensor_tensor(
                out=x[:, :], in0=pm2[g2][:, 0:D],
                scalar=cols2[:, cb2 + g2 * 4 + 2:cb2 + g2 * 4 + 3], in1=x[:, :],
                op0=ALU.mult, op1=ALU.add),
              rwl, "v")
            if st == "a":
                E("v", lambda e, x=xg: e.tensor_tensor(
                    out=x[:, :], in0=x[:, :], in1=obt[:, :], op=ALU.add), (), "v")
            note(f"{tag}_res_{g}", "v")
            pm2_read[g] = f"{tag}_res_{g}"
        note(f"{tag}_res", "v")
        if st == "b":
            dma(out_d[256 * i:256 * i + 128, :], xA[i % 3][:, :],
                [wt(f"{tag}_res_0")], "do", note_name=f"out{i}_a")
            dma(out_d[256 * i + 128:256 * i + 256, :], xB[i % 3][:, :],
                [wt(f"{tag}_res_1")], "do", note_name=f"out{i}_b")
            gnote(f"xfree{i}", [f"out{i}_a", f"out{i}_b"])

    def ph_mini(upto=99):
        tag = "mi"
        sets = [
            (minx[0:6, :], 0, 6, CB["A"], minq[0:6, :], scrB[0:6, :], None),
            (minx[32:38, :], 32, 6, CB["B"], minq[32:38, :], scrB[32:38, :], None),
        ]
        emit_qin(tag, sets, colsh, [wt("xsmall")])
        for dc in range(NDC):
            tdep = pu_wait() if dc == 0 else [wt(f"{tag}_tc_{dc - 1}")]
            E("t", lambda e, d=dc: e.transpose(
                pu[:, 0:6], minq[0:6, d * 128:(d + 1) * 128], idb[0:6, 0:6]),
              tdep + [wt(f"{tag}_done")])
            E("t", lambda e, d=dc: e.transpose(
                pu[:, 6:12], minq[32:38, d * 128:(d + 1) * 128],
                idb[32:38, 32:38]), (), "t")
            note(f"{tag}_tp_{dc}", "t")
            E("s", lambda e, d=dc: e.copy(
                out=xqmTm[:, d * 12:(d + 1) * 12], in_=pu[:, 0:12]),
              [wt(f"{tag}_tp_{dc}")], "s")
            note(f"{tag}_tc_{dc}", "s")
        pu_set(f"{tag}_tc_{NDC - 1}")
        for cch in range(NC):
            wl = [wt(f"{tag}_tc_{NDC - 1}")] + (pu_wait() if cch == 0 else
                                                [wt(f"{tag}_ev_{cch - 1}")])
            for dc in range(NDC):
                E("t", lambda e, c2=cch, d=dc: e.matmul(
                    puf[:, UF0:UF0 + 12],
                    w1q[:, d * HDIM + c2 * 128:d * HDIM + (c2 + 1) * 128],
                    xqmTm[:, d * 12:(d + 1) * 12],
                    start=(d == 0), stop=(d == NDC - 1)),
                  wl if dc == 0 else (), "t" if dc == NDC - 1 else None)
            note(f"{tag}_mm1_{cch}", "t")
            E("s", lambda e, c2=cch: e.copy(
                out=hm2[:, c2 * 12:(c2 + 1) * 12], in_=puf[:, UF0:UF0 + 12]),
              [wt(f"{tag}_mm1_{cch}")], "s")
            note(f"{tag}_ev_{cch}", "s")
        pu_set(f"{tag}_ev_{NC - 1}")
        emit_bcast(f"{tag}_al", [
            (colsh, 0, 6, CB["A"] + 9, 0),
            (colsh, 32, 6, CB["B"] + 9, 6),
        ], 12, extradep=[wt(f"{tag}_cols")], dst0=258)
        hm3 = hm2[:, :].rearrange("p (c w) -> p c w", w=12)
        E("g", lambda e: e.tensor_tensor(
            out=hm3[:, :, :], in0=hm3[:, :, :],
            in1=bcast[:, 0:12].unsqueeze(1).broadcast_to([128, NC, 12])
            if False else bcast[:, 0:12].broadcast_to([128, 12]),
            op=ALU.mult), ()) if False else None
        for cch in range(NC):
            E("g", lambda e, c2=cch: e.tensor_tensor(
                out=hm2[:, c2 * 12:(c2 + 1) * 12], in0=hm2[:, c2 * 12:(c2 + 1) * 12],
                in1=bcast[:, 258:270], op=ALU.mult),
              [wt(f"{tag}_al_bc"), wt(f"{tag}_ev_{cch}")], "g")
        note(f"{tag}_hs", "g")
        h3 = hm2[:, :].rearrange("p (c w) -> p c w", w=12)
        E("s", lambda e: e.activation(
            out=scrB[:, :].rearrange("p (c w) -> p c w", w=16)[:, 0:NC, 0:2],
            in_=h3[:, :, 3:9:5] if False else h3[:, :, 3:10:6],
            func=AF.Identity, scale=cwt[:, NC:2 * NC].transpose([0, 1])
            if False else 1.0, bias=0.0),
          [wt(f"{tag}_hs")], "s") if False else None
        for cch in range(NC):
            E("s", lambda e, c2=cch: e.activation(
                out=tmpB[:, 0:2], in_=hm2[:, c2 * 12 + 3:c2 * 12 + 10:6],
                func=AF.Identity, scale=cwt[:, NC + c2:NC + c2 + 1],
                bias=cwt[:, 3 * NC + c2:3 * NC + c2 + 1]),
              [wt(f"{tag}_hs")]
              + ([wt(f"{tag}_cC_{cch - 1}")] if cch else []), "s")
            note(f"{tag}_cA_{cch}", "s")
            E("v", lambda e, c2=cch: e.scalar_tensor_tensor(
                out=tmpB[:, 2:4], in0=hm2[:, c2 * 12 + 2:c2 * 12 + 9:6],
                scalar=cwt[:, c2:c2 + 1], in1=tmpB[:, 0:2],
                op0=ALU.mult, op1=ALU.add), [wt(f"{tag}_cA_{cch}")], "v")
            E("v", lambda e, c2=cch: e.scalar_tensor_tensor(
                out=hm2[:, c2 * 12 + 3:c2 * 12 + 10:6],
                in0=hm2[:, c2 * 12 + 4:c2 * 12 + 11:6],
                scalar=cwt[:, 2 * NC + c2:2 * NC + c2 + 1], in1=tmpB[:, 2:4],
                op0=ALU.mult, op1=ALU.add), (), "v")
            note(f"{tag}_cC_{cch}", "v")
            E("s", lambda e, c2=cch: e.activation(
                out=hm2[:, c2 * 12 + 3:c2 * 12 + 10:6],
                in_=hm2[:, c2 * 12 + 3:c2 * 12 + 10:6], func=AF.Silu),
              [wt(f"{tag}_cC_{cch}")], "s")
            note(f"{tag}_cs_{cch}", "s")
            E("s", lambda e, c2=cch: e.activation(
                out=absT[:, 0:2],
                in_=hm2[:, c2 * 12 + 3:c2 * 12 + 10:6], func=AF.Abs),
              [wt(f"{tag}_cs_{cch}")] +
              ([wt(f"{tag}_Mm_{cch - 1}")] if cch >= 1 else []), "s")
            note(f"{tag}_cb_{cch}", "s")
            if cch == 0:
                E("v", lambda e, c2=cch: e.tensor_copy(
                    out=Mh[:, 0:2], in_=absT[:, 0:2]),
                  [wt(f"{tag}_cb_{cch}")], "v")
            else:
                E("v", lambda e, c2=cch: e.tensor_tensor(
                    out=Mh[:, 0:2], in0=Mh[:, 0:2],
                    in1=absT[:, 0:2], op=ALU.max),
                  [wt(f"{tag}_cb_{cch}")], "v")
            note(f"{tag}_Mm_{cch}", "v")
        E("t", lambda e: e.transpose(puf[0:2, UF0:UF0 + 128], Mh[:, 0:2]
                                     .transpose([0, 1]) if False else Mh[:, 0:2],
                                     idf[:, :]),
          [wt(f"{tag}_Mm_{NC - 1}")] + pu_wait(), "t")
        note(f"{tag}_mxt", "t")
        E("v", lambda e: e.reduce_max(out=cols2[0:2, 18:19],
                                      in_=puf[0:2, UF0:UF0 + 128], axis=AX.X),
          [wt(f"{tag}_mxt")], "v")
        note(f"{tag}_mxr", "v")
        pu_set(f"{tag}_mxr")
        vch(lambda e: e.tensor_scalar(out=cols2[0:2, 18:19], in0=cols2[0:2, 18:19],
                                      scalar1=1e-5, scalar2=None, op0=ALU.max),
            [wt(f"{tag}_mxr")])
        vch(lambda e: e.tensor_tensor(out=cols2[0:2, 19:20], in0=cols2[0:2, 18:19],
                                      in1=swc[0:2, 2:3], op=ALU.mult))
        vch(lambda e: e.reciprocal(out=cols2[0:2, 17:18], in_=cols2[0:2, 18:19]))
        vch(lambda e: e.tensor_scalar(out=cols2[0:2, 17:18], in0=cols2[0:2, 17:18],
                                      scalar1=127.0, scalar2=None, op0=ALU.mult))
        note(f"{tag}_s2c", "v")
        emit_bcast(f"{tag}_s2", [(cols2, 0, 2, 17, 0)], 2,
                   extradep=[wt(f"{tag}_s2c")], dst0=270)
        for cch in range(NC):
            E("g", lambda e, c2=cch: e.tensor_tensor(
                out=hm2[:, c2 * 12 + 3:c2 * 12 + 10:6],
                in0=hm2[:, c2 * 12 + 3:c2 * 12 + 10:6],
                in1=bcast[:, 270:272], op=ALU.mult),
              [wt(f"{tag}_s2_bc"), wt(f"{tag}_Mm_{cch}")], "g")
            note(f"{tag}_ms_{cch}", "g")
            E("v", lambda e, c2=cch: e.tensor_scalar(
                out=xqm_m[:, c2 * 2:(c2 + 1) * 2],
                in0=hm2[:, c2 * 12 + 3:c2 * 12 + 10:6],
                scalar1=MAGIC, scalar2=MAGIC, op0=ALU.add, op1=ALU.subtract),
              [wt(f"{tag}_ms_{cch}")], "v")
            note(f"{tag}_rn_{cch}", "v")
        note(f"{tag}_round", "v")
        for dmc in range(NDC):
            wl = [wt(f"{tag}_round")] + (pu_wait() if dmc == 0 else
                                         [wt(f"{tag}_fc_{dmc - 1}"
                                             if upto >= 63 else
                                             f"{tag}_fv_{dmc - 1}")])
            for jc in range(NC):
                E("t", lambda e, d=dmc, j=jc: e.matmul(
                    puf[:, UF0:UF0 + 2],
                    w2q[:, j * D + d * 128:j * D + (d + 1) * 128],
                    xqm_m[:, j * 2:(j + 1) * 2],
                    start=(j == 0), stop=(j == NC - 1)),
                  wl if jc == 0 else (), "t" if jc == NC - 1 else None)
            note(f"{tag}_mm2_{dmc}", "t")
            E("t", lambda e: e.transpose(puf[0:2, UF0 + 4:UF0 + 132],
                                         puf[:, UF0:UF0 + 2], idf[:, :]),
              (), "t") if False else None
            if upto < 62:
                E("v", lambda e, d=dmc: e.tensor_copy(
                    out=tmpB[:, 0:2], in_=puf[:, UF0:UF0 + 2]),
                  [wt(f"{tag}_mm2_{dmc}")], "v")
                note(f"{tag}_fv_{dmc}", "v")
                continue
            E("v", lambda e, d=dmc: e.tensor_copy(
                out=tmpB[:, 0:2], in_=puf[:, UF0:UF0 + 2]),
              [wt(f"{tag}_mm2_{dmc}")], "v")
            note(f"{tag}_fv_{dmc}", "v")
            E("t", lambda e, d=dmc: e.transpose(
                puf[0:2, UF0 + 4:UF0 + 132], tmpB[:, 0:2], idf[:, :]),
              [wt(f"{tag}_fv_{dmc}")]
              + ([wt(f"{tag}_fc_{dmc - 1}")] if dmc > 0 else []), "t")
            note(f"{tag}_ft_{dmc}", "t")
            if upto < 63:
                note(f"{tag}_fc_{dmc}", "t")
                continue
            E("s", lambda e, d=dmc: e.copy(
                out=fmini[0:2, d * 128:(d + 1) * 128],
                in_=puf[0:2, UF0 + 4:UF0 + 132]),
              [wt(f"{tag}_ft_{dmc}")], "s")
            note(f"{tag}_fc_{dmc}", "s")
        if upto < 99:
            return
        pu_set(f"{tag}_fc_{NDC - 1}")
        note(f"{tag}_f", "s")
        dma(scrB[0:1, :], minx[3:4, :], [wt(f"{tag}_done")], "dg", eng="g",
            note_name=f"{tag}_xsrc0")
        dma(scrB[1:2, :], minx[36:37, :], [wt(f"{tag}_done")], "dg", eng="g",
            note_name=f"{tag}_xsrc")
        E("v", lambda e: e.scalar_tensor_tensor(
            out=scrB[0:2, :], in0=fmini[0:2, :], scalar=cols2[0:2, 19:20],
            in1=scrB[0:2, :], op0=ALU.mult, op1=ALU.add),
          [wt(f"{tag}_f"), wt(f"{tag}_xsrc"), wt(f"{tag}_xsrc0"),
           wt(f"{tag}_s2c")], "v")
        E("v", lambda e: e.tensor_tensor(
            out=scrB[0:2, :], in0=scrB[0:2, :], in1=obt[0:2, :], op=ALU.add),
          (), "v")
        dma(xh[62:64, :], scrB[0:2, :], [("v", R.n("v"))], "dg", eng="g",
            note_name="mini_done")

    def hqin(tag, src, p0, n_, dst, pstg, deps):
        emit_qin(tag, [(src, p0, n_, CB["A"], dst, pstg, None)], colsh, deps)

    qzb = xq1T[:, 0:640]

    def ph_attn_gen():
        tag = "at"
        hqin("ah", xh[0:40, :], 0, 40, xqh[0:40, :], scrB[0:40, :],
             [wt("xsmall"), wt("mini_done")])
        for dc in range(NDC):
            tdep = (pu_wait() if dc == 0 else [wt(f"ah_tc_{dc - 1}")])
            E("t", lambda e, d=dc: e.transpose(
                pu[:, 0:40], xqh[0:40, d * 128:(d + 1) * 128], idb[0:40, 0:40]),
              tdep + [wt("ah_done")], "t")
            note(f"ah_tp_{dc}", "t")
            E("s", lambda e, d=dc: e.copy(
                out=xqhT[:, d * 40:(d + 1) * 40], in_=pu[:, 0:40]),
              [wt(f"ah_tp_{dc}")], "s")
            note(f"ah_tc_{dc}", "s")
        pu_set(f"ah_tc_{NDC - 1}")
        yield
        emit_bcast("ah_al", [(colsh, 0, 40, CB["A"] + 9, 0)], 40,
                   extradep=[wt("ah_cols")], dst0=272)
        for grp in range(NC // 4):
            wl = [wt(f"ah_tc_{NDC - 1}")] + (pu_wait() if grp == 0 else
                                             [wt(f"ah_evg_{grp - 1}")])
            for ci in range(4):
                cch = grp * 4 + ci
                for dc in range(NDC):
                    E("t", lambda e, c2=cch, ci2=ci, d=dc: e.matmul(
                        puf[:, UF0 + ci2 * 40:UF0 + ci2 * 40 + 40],
                        w1q[:, d * HDIM + c2 * 128:d * HDIM + (c2 + 1) * 128],
                        xqhT[:, d * 40:(d + 1) * 40],
                        start=(d == 0), stop=(d == NDC - 1)),
                      wl if (ci == 0 and dc == 0) else (),
                      "t" if (ci == 3 and dc == NDC - 1) else None)
            note(f"ah_mm1g_{grp}", "t")
            E("v", lambda e, g2=grp: e.tensor_copy(
                out=hh[:, g2 * 160:(g2 + 1) * 160],
                in_=puf[:, UF0:UF0 + 160]),
              [wt(f"ah_mm1g_{grp}")], "v")
            note(f"ah_evg_{grp}", "v")
        for cch in range(NC):
            ev[f"ah_ev_{cch}"] = ev[f"ah_evg_{cch // 4}"]
            E("g", lambda e, c2=cch: e.tensor_tensor(
                out=hh[:, c2 * 40:(c2 + 1) * 40], in0=hh[:, c2 * 40:(c2 + 1) * 40],
                in1=bcast[:, 272:312], op=ALU.mult),
              [wt("ah_al_bc"), wt(f"ah_ev_{cch}")], "g")
            note(f"ah_hs_{cch}", "g")
            E("s", lambda e, c2=cch: e.activation(
                out=absh[:, 1:39], in_=hh[:, c2 * 40 + 1:c2 * 40 + 39],
                func=AF.Identity, scale=cwt[:, NC + c2:NC + c2 + 1],
                bias=cwt[:, 3 * NC + c2:3 * NC + c2 + 1]),
              [wt(f"ah_hs_{cch}")] + ([wt(f"ah_Mm_{cch - 1}")] if cch else []),
              "s")
            note(f"ah_cA_{cch}", "s")
            E("v", lambda e, c2=cch: e.scalar_tensor_tensor(
                out=absh[:, 1:39], in0=hh[:, c2 * 40 + 0:c2 * 40 + 38],
                scalar=cwt[:, c2:c2 + 1], in1=absh[:, 1:39],
                op0=ALU.mult, op1=ALU.add), [wt(f"ah_cA_{cch}")], "v")
            note(f"ah_cB_{cch}", "v")
            E("v", lambda e, c2=cch: e.scalar_tensor_tensor(
                out=hh[:, c2 * 40 + 1:c2 * 40 + 39],
                in0=hh[:, c2 * 40 + 2:c2 * 40 + 40],
                scalar=cwt[:, 2 * NC + c2:2 * NC + c2 + 1], in1=absh[:, 1:39],
                op0=ALU.mult, op1=ALU.add), (), "v")
            note(f"ah_cC_{cch}", "v")
            E("s", lambda e, c2=cch: e.activation(
                out=hh[:, c2 * 40 + 1:c2 * 40 + 39],
                in_=hh[:, c2 * 40 + 1:c2 * 40 + 39], func=AF.Silu),
              [wt(f"ah_cC_{cch}")], "s")
            note(f"ah_si_{cch}", "s")
            E("s", lambda e, c2=cch: e.activation(
                out=absh[:, 1:39], in_=hh[:, c2 * 40 + 1:c2 * 40 + 39],
                func=AF.Abs), [wt(f"ah_si_{cch}")] +
              ([wt(f"ah_Mm_{cch - 1}")] if cch else []), "s")
            note(f"ah_ab_{cch}", "s")
            if cch == 0:
                E("v", lambda e: e.tensor_copy(out=Mh[:, 1:39], in_=absh[:, 1:39]),
                  [wt(f"ah_ab_{cch}")], "v")
            else:
                E("v", lambda e: e.tensor_tensor(
                    out=Mh[:, 1:39], in0=Mh[:, 1:39], in1=absh[:, 1:39],
                    op=ALU.max), [wt(f"ah_ab_{cch}")], "v")
            note(f"ah_Mm_{cch}", "v")
        yield
        E("t", lambda e: e.transpose(puf[0:40, UF0:UF0 + 128], Mh[:, 0:40],
                                     idf[:, :]),
          [wt(f"ah_Mm_{NC - 1}")] + pu_wait(), "t")
        note("ah_mxt", "t")
        E("v", lambda e: e.reduce_max(out=colsh[0:40, 6:7],
                                      in_=puf[0:40, UF0:UF0 + 128], axis=AX.X),
          [wt("ah_mxt")], "v")
        note("ah_mxr", "v")
        pu_set("ah_mxr")
        vch(lambda e: e.tensor_scalar(out=colsh[0:40, 6:7], in0=colsh[0:40, 6:7],
                                      scalar1=1e-5, scalar2=None, op0=ALU.max),
            [wt("ah_mxr")])
        vch(lambda e: e.tensor_tensor(out=colsh[0:40, 10:11],
                                      in0=colsh[0:40, 6:7], in1=swc[0:40, 2:3],
                                      op=ALU.mult))
        vch(lambda e: e.reciprocal(out=colsh[0:40, 11:12], in_=colsh[0:40, 6:7]))
        vch(lambda e: e.tensor_scalar(out=colsh[0:40, 11:12],
                                      in0=colsh[0:40, 11:12], scalar1=127.0,
                                      scalar2=None, op0=ALU.mult))
        note("ah_s2c", "v")
        emit_bcast("ah_s2", [(colsh, 0, 40, 11, 0)], 40,
                   extradep=[wt("ah_s2c")], dst0=312)
        for cch in range(NC):
            E("g", lambda e, c2=cch: e.tensor_tensor(
                out=hh[:, c2 * 40 + 1:c2 * 40 + 39],
                in0=hh[:, c2 * 40 + 1:c2 * 40 + 39], in1=bcast[:, 313:351],
                op=ALU.mult), [wt("ah_s2_bc"), wt(f"ah_Mm_{cch}")], "g")
            note(f"ah_ms_{cch}", "g")
            E("v", lambda e, c2=cch: e.tensor_scalar(
                out=xqmh[:, c2 * 40:(c2 + 1) * 40], in0=hh[:, c2 * 40:(c2 + 1) * 40],
                scalar1=MAGIC, scalar2=MAGIC, op0=ALU.add, op1=ALU.subtract),
              [wt(f"ah_ms_{cch}")], "v")
            note(f"ah_rn_{cch}", "v")
        note("ah_round", "v")
        yield
        for nh in range(2):
            wl = [wt("ah_round")] + pu_wait()
            for jc in range(NC):
                E("t", lambda e, j=jc, n_=nh: e.matmul(
                    puf[0:40, 0:512],
                    xqmh[:, j * 40:(j + 1) * 40],
                    w2q[:, j * D + n_ * 512:j * D + (n_ + 1) * 512],
                    start=(j == 0), stop=(j == NC - 1)),
                  wl if jc == 0 else (), "t" if jc == NC - 1 else None)
            note(f"ah_mm2_{nh}", "t")
            E("s", lambda e, n_=nh: e.activation(
                out=scrA[0:40, n_ * 512:(n_ + 1) * 512], in_=puf[0:40, 0:512],
                func=AF.Identity, scale=colsh[0:40, 10:11]),
              [wt(f"ah_mm2_{nh}")], "s")
            note(f"ah_f_{nh}", "s")
            pu_set(f"ah_f_{nh}")
        E("v", lambda e: e.tensor_tensor(out=xh[0:40, :], in0=xh[0:40, :],
                                         in1=scrA[0:40, :], op=ALU.add),
          [wt("ah_f_0"), wt("ah_f_1")], "v")
        note("ah_x1h", "v")
        yield
        E("v", lambda e: e.tensor_reduce(out=colsh[0:40, 0:1], in_=xh[0:40, :],
                                         axis=AX.X, op=ALU.add),
          [wt("ah_x1h")], "v")
        note("zr", "v")
        E("s", lambda e: e.activation(out=scrB[0:40, :], in_=xh[0:40, :],
                                      func=AF.Square, accum_out=colsh[0:40, 1:2]),
          [wt("ah_x1h")], "s")
        note("zs", "s")
        zc = lambda k: colsh[0:40, k:k + 1]
        vch(lambda e: e.tensor_scalar(out=zc(0), in0=zc(0), scalar1=1.0 / D,
                                      scalar2=None, op0=ALU.mult),
            [wt("zr"), wt("zs")])
        vch(lambda e: e.tensor_scalar(out=zc(1), in0=zc(1), scalar1=1.0 / D,
                                      scalar2=None, op0=ALU.mult))
        vch(lambda e: e.tensor_tensor(out=zc(3), in0=zc(0), in1=zc(0),
                                      op=ALU.mult))
        vch(lambda e: e.tensor_tensor(out=zc(2), in0=zc(1), in1=zc(3),
                                      op=ALU.subtract))
        vch(lambda e: e.tensor_scalar(out=zc(2), in0=zc(2), scalar1=EPS,
                                      scalar2=None, op0=ALU.add))
        note("zpre", "v")
        E("s", lambda e: e.activation(out=zc(5), in_=zc(2), func=AF.Sqrt),
          [wt("zpre")], "s")
        note("zsqrt", "s")
        vch(lambda e: e.reciprocal(out=zc(11), in_=zc(5)), [wt("zsqrt")])
        for _ in range(2):
            vch(lambda e: e.tensor_tensor(out=zc(5), in0=zc(2), in1=zc(11),
                                          op=ALU.mult))
            vch(lambda e: e.tensor_tensor(out=zc(5), in0=zc(5), in1=zc(11),
                                          op=ALU.mult))
            vch(lambda e: e.tensor_scalar(out=zc(5), in0=zc(5), scalar1=-0.5,
                                          scalar2=1.5, op0=ALU.mult, op1=ALU.add))
            vch(lambda e: e.tensor_tensor(out=zc(11), in0=zc(11), in1=zc(5),
                                          op=ALU.mult))
        vch(lambda e: e.tensor_tensor(out=zc(4), in0=zc(0), in1=zc(11),
                                      op=ALU.mult))
        vch(lambda e: e.tensor_scalar(out=zc(4), in0=zc(4), scalar1=-1.0,
                                      scalar2=None, op0=ALU.mult))
        note("zcols", "v")
        E("s", lambda e: e.activation(out=scrA[0:40, :], in_=xh[0:40, :],
                                      func=AF.Identity, scale=zc(11),
                                      bias=zc(4)), [wt("zcols")], "s")
        note("z16", "s")
        zaT = fmb[:, 0:320]
        zbT = fmb[:, 320:640]
        for dc in range(NDC):
            E("t", lambda e, d=dc: e.transpose(
                puf[:, UF0:UF0 + 40], scrA[0:40, d * 128:(d + 1) * 128],
                idf[0:40, 0:40]),
              [wt("z16")] + (pu_wait() if dc == 0 else [wt(f"zb_{dc - 1}")]),
              "t")
            note(f"zt_{dc}", "t")
            E("s", lambda e, d=dc: e.activation(
                out=zaT[:, d * 40:(d + 1) * 40], in_=puf[:, UF0:UF0 + 40],
                func=AF.Identity, scale=lab[:, d:d + 1]), [wt(f"zt_{dc}")], "s")
            note(f"za_{dc}", "s")
            E("s", lambda e, d=dc: e.activation(
                out=zbT[:, d * 40:(d + 1) * 40], in_=puf[:, UF0:UF0 + 40],
                func=AF.Identity, scale=lab[:, 8 + d:9 + d]), (), "s")
            note(f"zb_{dc}", "s")
        pu_set(f"zb_{NDC - 1}")
        note("zT", "s")
        yield
        def pj_dst(i2):
            if i2 < 3:
                return pm1[i2][:, 0:40]
            if i2 == 3:
                return puf[:, UF0:UF0 + 40]
            g, hb = divmod(i2 - 4, 2)
            return pm2[g][:, hb * 512:hb * 512 + 40]

        for wi in range(3):
            rhs = zaT if wi == 0 else zbT
            for dc in range(NDC):
                wl = [wt("awscr_done"), wt("zT")]
                if wi or dc:
                    wl.append(wt(f"pj_ev_{wi - 1}_{NDC - 1}") if (dc == 0 and wi)
                              else wt(f"pj_mm_{wi}_{dc - 2}") if dc >= 2 else
                              wt("zT"))
                dma(wstg[:, (dc % 2) * D:(dc % 2) * D + D],
                    awscr[wi * D:(wi + 1) * D, dc * 128:(dc + 1) * 128],
                    wl, "da", tp=True, note_name=f"pj_sl_{wi}_{dc}")
                for ic in range(8):
                    E("t", lambda e, w_=wi, d=dc, i2=ic, r=rhs: e.matmul(
                        pj_dst(i2),
                        wstg[:, (d % 2) * D + i2 * 128:(d % 2) * D + (i2 + 1) * 128],
                        r[:, d * 40:(d + 1) * 40],
                        start=(d == 0), stop=(d == NDC - 1)),
                      ([wt(f"pj_sl_{wi}_{dc}")] +
                       ((pu_wait() + [wt(pm2_read[0]), wt(pm2_read[1])])
                        if (wi == 0 and dc == 0 and ic == 0) else []) +
                       ([wt(f"pj_ev_{wi - 1}_7")] if (dc == 0 and ic == 0 and wi)
                        else [])) if ic == 0 else (),
                      "t" if ic == 7 else None)
                note(f"pj_mm_{wi}_{dc}", "t")
            for ic in range(8):
                if wi == 0:
                    E("s", lambda e, w_=wi, i2=ic: e.activation(
                        out=fmp[:, w_ * 320 + i2 * 40:w_ * 320 + (i2 + 1) * 40],
                        in_=pj_dst(i2),
                        func=AF.Identity, bias=qbv[:, i2:i2 + 1]),
                      [wt(f"pj_mm_{wi}_{NDC - 1}")], "s")
                elif wi == 2:
                    E("s", lambda e, w_=wi, i2=ic: e.activation(
                        out=fmp[:, w_ * 320 + i2 * 40:w_ * 320 + (i2 + 1) * 40],
                        in_=pj_dst(i2),
                        func=AF.Identity, bias=qbv[:, 8 + i2:9 + i2]),
                      [wt(f"pj_mm_{wi}_{NDC - 1}")], "s")
                else:
                    E("s", lambda e, w_=wi, i2=ic: e.copy(
                        out=fmp[:, w_ * 320 + i2 * 40:w_ * 320 + (i2 + 1) * 40],
                        in_=pj_dst(i2)),
                      [wt(f"pj_mm_{wi}_{NDC - 1}")], "s")
                note(f"pj_ev_{wi}_{ic}", "s")
            note(f"pj_{wi}", "s")
        pu_set("pj_ev_2_7")
        pm2_read[0] = "pj_2"
        pm2_read[1] = "pj_2"
        yield
        E("v", lambda e: e.memset(bdg[:, :], 0.0), (), "v")
        note("bdg0", "v")
        for wi in range(3):
            dma(scrB[0:64, 0:64], lw_d[wi][:, :],
                [wt("bdg0")] + ([wt(f"lwt_{wi - 1}")] if wi else []),
                "da", note_name=f"lw_{wi}")
            E("t", lambda e, w_=wi: e.transpose(
                puf[0:64, UF0:UF0 + 64], scrB[0:64, 0:64], idf[0:64, 0:64]),
              [wt(f"lw_{wi}")] + pu_wait()
              + ([wt(f"lwc1_{wi - 1}")] if wi else []), "t")
            note(f"lwt_{wi}", "t")
            E("s", lambda e, w_=wi: e.copy(
                out=bdg[0:64, w_ * 128:w_ * 128 + 64],
                in_=puf[0:64, UF0:UF0 + 64]), [wt(f"lwt_{wi}"), wt("bdg0")], "s")
            note(f"lwc1_{wi}", "s")
            dma(bdg[64:128, wi * 128 + 64:wi * 128 + 128],
                bdg[0:64, wi * 128:wi * 128 + 64], [wt(f"lwc1_{wi}")], "dg",
                eng="g", note_name=f"lwc_{wi}")
        pu_set("lwc_2")
        gnote("bdg_done", ["lwc_0", "lwc_1", "lwc_2"])
        def transform(tg, src_slot, wi, dst_slot, lncol):
            E("v", lambda e, s=src_slot: e.tensor_copy(
                out=fmb[:, 320:640], in_=fmp[:, s * 320:(s + 1) * 320]),
              [wt(f"{tg}_srcdep")], "v")
            note(f"{tg}_cast", "v")
            for ic in range(8):
                E("t", lambda e, w_=wi, i2=ic: e.matmul(
                    puf[:, UF0 + i2 * 40:UF0 + (i2 + 1) * 40],
                    bdg[:, w_ * 128:(w_ + 1) * 128],
                    fmb[:, 320 + i2 * 40:320 + (i2 + 1) * 40],
                    start=True, stop=True),
                  [wt(f"{tg}_cast"), wt("bdg_done")] +
                  (pu_wait() if ic == 0 else []), "t" if ic == 7 else None)
            note(f"{tg}_mm", "t")
            for ic in range(8):
                E("s", lambda e, w_=wi, i2=ic: e.activation(
                    out=scrB[0:40, 0:0 + 128][0:40, :] if False else
                    scrB[0:40, i2 * 128:(i2 + 1) * 128],
                    in_=puf[:, UF0 + i2 * 40:UF0 + (i2 + 1) * 40],
                    func=AF.Identity, bias=lbv[:, w_:w_ + 1]),
                  [wt(f"{tg}_mm")], "s") if False else None
            for ic in range(8):
                E("s", lambda e, w_=wi, i2=ic: e.activation(
                    out=fmp[:, 3 * 320 + i2 * 40:3 * 320 + (i2 + 1) * 40],
                    in_=puf[:, UF0 + i2 * 40:UF0 + (i2 + 1) * 40],
                    func=AF.Identity, bias=lbv[:, w_:w_ + 1]),
                  [wt(f"{tg}_mm")], "s")
            note(f"{tg}_ev", "s")
            pu_set(f"{tg}_ev")
            for dc in range(NDC):
                E("t", lambda e, d=dc: e.transpose(
                    puf[0:40, UF0:UF0 + 128],
                    fmp[:, 3 * 320 + d * 40:3 * 320 + (d + 1) * 40],
                    idf[:, :]),
                  [wt(f"{tg}_ev")] + (pu_wait() if dc == 0 else
                                      [wt(f"{tg}_tmc_{dc - 1}")]), "t")
                note(f"{tg}_tm_{dc}", "t")
                E("s", lambda e, d=dc: e.copy(
                    out=scrB[0:40, d * 128:(d + 1) * 128],
                    in_=puf[0:40, UF0:UF0 + 128]), [wt(f"{tg}_tm_{dc}")], "s")
                note(f"{tg}_tmc_{dc}", "s")
            pu_set(f"{tg}_tmc_{NDC - 1}")
            note(f"{tg}_tm", "s")
            s3 = scrB[0:40, :].rearrange("p (h w) -> p h w", w=64)
            E("v", lambda e: e.tensor_reduce(
                out=habc[0:40, 0:16], in_=s3, axis=AX.X, op=ALU.add),
              [wt(f"{tg}_tm")], "v")
            note(f"{tg}_hr", "v")
            E("s", lambda e: e.activation(
                out=at16[0:40, :], in_=scrB[0:40, :], func=AF.Square),
              [wt(f"{tg}_tm")], "s")
            note(f"{tg}_hsq", "s")
            E("v", lambda e: e.tensor_reduce(
                out=habc[0:40, 16:32],
                in_=at16[0:40, :].rearrange("p (h w) -> p h w", w=64),
                axis=AX.X, op=ALU.add), [wt(f"{tg}_hsq")], "v")
            note(f"{tg}_hr2", "v")
            hc = lambda k: habc[0:40, k * 16:(k + 1) * 16]
            vch(lambda e: e.tensor_scalar(out=hc(0), in0=hc(0), scalar1=1.0 / 64,
                                          scalar2=None, op0=ALU.mult),
                [wt(f"{tg}_hr"), wt(f"{tg}_hr2")])
            vch(lambda e: e.tensor_scalar(out=hc(1), in0=hc(1), scalar1=1.0 / 64,
                                          scalar2=None, op0=ALU.mult))
            vch(lambda e: e.tensor_tensor(out=hc(4), in0=hc(0), in1=hc(0),
                                          op=ALU.mult))
            vch(lambda e: e.tensor_tensor(out=hc(2), in0=hc(1), in1=hc(4),
                                          op=ALU.subtract))
            vch(lambda e: e.tensor_scalar(out=hc(2), in0=hc(2), scalar1=EPS,
                                          scalar2=None, op0=ALU.add))
            note(f"{tg}_hpre", "v")
            E("s", lambda e: e.activation(out=hc(4), in_=hc(2), func=AF.Sqrt),
              [wt(f"{tg}_hpre")], "s")
            note(f"{tg}_hsqrt", "s")
            vch(lambda e: e.reciprocal(out=hc(3), in_=hc(4)), [wt(f"{tg}_hsqrt")])
            for _ in range(2):
                vch(lambda e: e.tensor_tensor(out=hc(4), in0=hc(2), in1=hc(3),
                                              op=ALU.mult))
                vch(lambda e: e.tensor_tensor(out=hc(4), in0=hc(4), in1=hc(3),
                                              op=ALU.mult))
                vch(lambda e: e.tensor_scalar(out=hc(4), in0=hc(4), scalar1=-0.5,
                                              scalar2=1.5, op0=ALU.mult,
                                              op1=ALU.add))
                vch(lambda e: e.tensor_tensor(out=hc(3), in0=hc(3), in1=hc(4),
                                              op=ALU.mult))
            note(f"{tg}_hrs", "v")
            vch(lambda e: e.tensor_tensor(out=habc[0:40, 64:80],
                                          in0=hc(0), in1=hc(3), op=ALU.mult),
                [wt(f"{tg}_hrs")])
            vch(lambda e: e.tensor_scalar(out=habc[0:40, 64:80],
                                          in0=habc[0:40, 64:80], scalar1=-1.0,
                                          scalar2=None, op0=ALU.mult))
            note(f"{tg}_mrs", "v")
            for h in range(16):
                E("s", lambda e, h2=h: e.activation(
                    out=scrB[0:40, h2 * 64:(h2 + 1) * 64],
                    in_=scrB[0:40, h2 * 64:(h2 + 1) * 64], func=AF.Identity,
                    scale=habc[0:40, 48 + h2:49 + h2],
                    bias=habc[0:40, 64 + h2:65 + h2]),
                  [wt(f"{tg}_mrs")] if h == 0 else (),
                  "s" if h == 15 else None)
            note(f"{tg}_ln", "s")
            for dc in range(NDC):
                E("t", lambda e, d=dc: e.transpose(
                    puf[:, UF0:UF0 + 40], scrB[0:40, d * 128:(d + 1) * 128],
                    idf[0:40, 0:40]),
                  [wt(f"{tg}_ln")] + (pu_wait() if dc == 0 else
                                      [wt(f"{tg}_fbc_{dc - 1}")]), "t")
                note(f"{tg}_fb_{dc}", "t")
                if lncol == 1:
                    E("s", lambda e, d=dc, ds=dst_slot: e.activation(
                        out=fmb[:, ds * 320 + d * 40:ds * 320 + (d + 1) * 40],
                        in_=puf[:, UF0:UF0 + 40], func=AF.Identity,
                        scale=lbv[:, 5:6]), [wt(f"{tg}_fb_{dc}"), wt("gcol")], "s")
                else:
                    E("s", lambda e, d=dc, ds=dst_slot: e.copy(
                        out=fmb[:, ds * 320 + d * 40:ds * 320 + (d + 1) * 40],
                        in_=puf[:, UF0:UF0 + 40]), [wt(f"{tg}_fb_{dc}")], "s")
                note(f"{tg}_fbc_{dc}", "s")
            pu_set(f"{tg}_fbc_{NDC - 1}")
            note(f"{tg}_out", "s")

        yield
        note("kn_srcdep", "s")
        ev["kn_srcdep"] = ev["pj_2"]
        transform("kn", 1, 1, 0, 1)
        yield
        E("v", lambda e: e.tensor_copy(out=fmb[:, 320:640],
                                       in_=fmp[:, 2 * 320:3 * 320]),
          [wt("pj_2"), wt("kn_mm")], "v")
        note("vi_cast", "v")
        for ic in range(8):
            E("t", lambda e, i2=ic: e.matmul(
                puf[:, UF0 + i2 * 40:UF0 + (i2 + 1) * 40],
                bdg[:, 2 * 128:3 * 128],
                fmb[:, 320 + i2 * 40:320 + (i2 + 1) * 40],
                start=True, stop=True),
              [wt("vi_cast"), wt("bdg_done")] + (pu_wait() if ic == 0 else []),
              "t" if ic == 7 else None)
        note("vi_mm", "t")
        for ic in range(8):
            E("s", lambda e, i2=ic: e.activation(
                out=fmp[:, 3 * 320 + i2 * 40:3 * 320 + (i2 + 1) * 40],
                in_=puf[:, UF0 + i2 * 40:UF0 + (i2 + 1) * 40],
                func=AF.Identity, bias=lbv[:, 2:3]), [wt("vi_mm")], "s")
        note("vi_ev", "s")
        pu_set("vi_ev")
        for dc in range(NDC):
            for b in range(2):
                E("t", lambda e, d=dc, b2=b: e.transpose(
                    puf[0:16, UF0:UF0 + 128],
                    fmp[:, 3 * 320 + d * 40 + b2 * 20 + 2:
                        3 * 320 + d * 40 + b2 * 20 + 18], idf[:, :]),
                  [wt("vi_ev")] + (pu_wait() if (dc == 0 and b == 0) else
                                   [wt(f"vi_tc_{dc}_{b - 1}") if b else
                                    wt(f"vi_tc_{dc - 1}_1")]), "t")
                note(f"vi_tm_{dc}_{b}", "t")
                E("s", lambda e, d=dc, b2=b: e.copy(
                    out=vib[0:16, b2 * 1024 + d * 128:b2 * 1024 + (d + 1) * 128],
                    in_=puf[0:16, UF0:UF0 + 128]),
                  [wt(f"vi_tm_{dc}_{b}"), wt("ah_round")], "s")
                note(f"vi_tc_{dc}_{b}", "s")
        pu_set(f"vi_tc_{NDC - 1}_1")
        note("vi_tm", "s")
        yield
        E("v", lambda e: e.memset(qzb[:, :], 0.0),
          [wt(f"a2_mm1_{NC - 1}")], "v")
        note("qzb0", "v")
        E("v", lambda e: e.tensor_copy(out=fmp[:, 5 * 320:6 * 320],
                                       in_=fmp[:, 0:320]), [wt("pj_0")], "v")
        note("q0c", "v")

        def iteration(n):
            tg = f"it{n}"
            ev[f"{tg}_srcdep"] = [
                ev["q0c" if n == 0 else f"qup{n}"],
                ev["vi_mm" if n == 0 else f"it{n - 1}_lg"]]
            transform(tg, 5, 0, 1, 0)
            E("v", lambda e: e.tensor_copy(
                out=qzb[0:64, 0:320], in_=fmb[0:64, 320:640]),
              [wt(f"{tg}_out"), wt("qzb0")], "v")
            E("v", lambda e: e.tensor_copy(
                out=qzb[64:128, 320:640], in_=fmb[64:128, 320:640]), (), "v")
            note(f"{tg}_qz", "v")
            for b in range(2):
                for h in range(16):
                    bh = b * 16 + h
                    E("t", lambda e, b2=b, h2=h, bh2=bh: e.matmul(
                        puf[0:16, bh2 * 16:(bh2 + 1) * 16],
                        qzb[:, (h2 % 2) * 320 + (h2 // 2) * 40 + b2 * 20 + 2:
                            (h2 % 2) * 320 + (h2 // 2) * 40 + b2 * 20 + 18],
                        fmb[:, (h2 // 2) * 40 + b2 * 20 + 2:
                            (h2 // 2) * 40 + b2 * 20 + 18],
                        start=True, stop=True),
                      ([wt(f"{tg}_out"), wt(f"{tg}_qz"), wt("kn_out")]
                       + pu_wait())
                      if bh == 0 else (), "t" if bh == 31 else None)
            note(f"{tg}_lg", "t")
            E("v", lambda e: e.tensor_reduce(
                out=habc[0:16, 0:32],
                in_=puf[0:16, 0:512].rearrange("p (t k) -> p t k", k=16),
                axis=AX.X, op=ALU.max), [wt(f"{tg}_lg")], "v")
            note(f"{tg}_sm1", "v")
            vch(lambda e: e.tensor_scalar(
                out=habc[0:16, 0:32], in0=habc[0:16, 0:32], scalar1=-0.125,
                scalar2=None, op0=ALU.mult), [wt(f"{tg}_sm1")])
            note(f"{tg}_nmx", "v")
            for t in range(32):
                E("s", lambda e, t2=t: e.activation(
                    out=scrA[0:16, t2 * 16:(t2 + 1) * 16],
                    in_=puf[0:16, t2 * 16:(t2 + 1) * 16], func=AF.Exp,
                    scale=0.125, bias=habc[0:16, t2:t2 + 1]),
                  [wt(f"{tg}_nmx")] if t == 0 else (),
                  "s" if t == 31 else None)
            note(f"{tg}_exp", "s")
            pu_set(f"{tg}_exp")
            E("v", lambda e: e.tensor_reduce(
                out=habc[0:16, 32:64],
                in_=scrA[0:16, 0:512].rearrange("p (t k) -> p t k", k=16),
                axis=AX.X, op=ALU.add), [wt(f"{tg}_exp")], "v")
            note(f"{tg}_sm2", "v")
            vch(lambda e: e.reciprocal(out=habc[0:16, 0:32],
                                       in_=habc[0:16, 32:64]),
                [wt(f"{tg}_sm2")])
            note(f"{tg}_rcp", "v")
            for t in range(32):
                E("s", lambda e, t2=t: e.activation(
                    out=scrA[0:16, t2 * 16:(t2 + 1) * 16],
                    in_=scrA[0:16, t2 * 16:(t2 + 1) * 16], func=AF.Identity,
                    scale=habc[0:16, t2:t2 + 1]),
                  [wt(f"{tg}_rcp")] if t == 0 else (),
                  "s" if t == 31 else None)
            note(f"{tg}_P", "s")
            def pt_tgt(bh2):
                if bh2 % 2 == 0:
                    return puf[0:16, 256 + (bh2 % 16) * 16:
                               256 + (bh2 % 16) * 16 + 16]
                return pm1[0][0:16, (bh2 % 16) * 16:(bh2 % 16) * 16 + 16]
            pt_order = [b for b in range(32) if b % 2 == 0] + \
                [b for b in range(32) if b % 2 == 1]
            for idx, bh in enumerate(pt_order):
                E("t", lambda e, bh2=bh: e.transpose(
                    pt_tgt(bh2),
                    scrA[0:16, bh2 * 16:(bh2 + 1) * 16], idf[0:16, 0:16]),
                  [wt(f"{tg}_P")] if idx == 0 else
                  ([wt(f"{tg}_PTc_{bh - 16}")] if bh >= 16 else ()), "t")
                note(f"{tg}_PT_{bh}", "t")
                E("v", lambda e, bh2=bh: e.tensor_copy(
                    out=PTb[0:16, bh2 * 16:(bh2 + 1) * 16],
                    in_=pt_tgt(bh2)),
                  [wt(f"{tg}_PT_{bh}")], "v")
                note(f"{tg}_PTc_{bh}", "v")
            note(f"{tg}_PTb", "s")
            pu_set(f"{tg}_PTb")
            E("v", lambda e: e.memset(puf[:, UF0:UF0 + 320], 0.0),
              pu_wait(), "v")
            note(f"{tg}_az", "v")
            for b in range(2):
                for h in range(16):
                    E("t", lambda e, b2=b, h2=h: e.matmul(
                        puf[(h2 % 2) * 64:(h2 % 2) * 64 + 64,
                            UF0 + (h2 // 2) * 40 + b2 * 20 + 2:
                            UF0 + (h2 // 2) * 40 + b2 * 20 + 18],
                        vib[0:16, b2 * 1024 + h2 * 64:b2 * 1024 + (h2 + 1) * 64],
                        PTb[0:16, (b2 * 16 + h2) * 16:(b2 * 16 + h2 + 1) * 16],
                        start=True, stop=True,
                        tile_position=(0, 64 * (h2 % 2))),
                      [wt(f"{tg}_PTb"), wt("vi_tm"), wt(f"{tg}_az")]
                      if (b == 0 and h == 0) else (),
                      "t" if (b == 1 and h == 15) else None)
            note(f"{tg}_av", "t")
            its = {0: 1, 1: 2, 2: 4}[n]
            E("v", lambda e, s2=its: e.tensor_copy(
                out=fmp[:, s2 * 320:(s2 + 1) * 320],
                in_=puf[:, UF0:UF0 + 320]), [wt(f"{tg}_av")], "v")
            note(f"{tg}_it", "v")
            pu_set(f"{tg}_it")

        iteration(0)
        E("v", lambda e: e.tensor_tensor(
            out=fmp[:, 5 * 320:6 * 320], in0=fmp[:, 5 * 320:6 * 320],
            in1=fmp[:, 1 * 320:2 * 320], op=ALU.add), [wt("it0_it")], "v")
        note("qup1", "v")
        yield
        iteration(1)
        E("v", lambda e: e.tensor_tensor(
            out=fmp[:, 3 * 320:4 * 320], in0=fmp[:, 2 * 320:3 * 320],
            in1=fmp[:, 1 * 320:2 * 320], op=ALU.subtract), [wt("it1_it")], "v")
        note("df_sub", "v")
        d3 = fmp[:, 3 * 320:4 * 320].rearrange("p (c w) -> p c w", w=40)
        for b in range(2):
            E("v", lambda e, b2=b: e.reduce_sum(
                out=habc[0:128, 64 + b2 * 8:64 + b2 * 8 + 8]
                if False else arow[0:1, 40:48],
                in_=d3[:, :, b2 * 20 + 2:b2 * 20 + 18], axis=AX.X,
                apply_absolute_value=True), [wt("df_sub")], "v") if False else None
        E("v", lambda e: e.reduce_sum(
            out=habc[0:128, 64:72] if False else Mh[:, 0:8],
            in_=d3[:, :, 2:18], axis=AX.X, apply_absolute_value=True),
          [wt("df_sub")], "v")
        note("df_r1a", "v")
        E("v", lambda e: e.reduce_sum(
            out=Mh[:, 8:16], in_=d3[:, :, 22:38], axis=AX.X,
            apply_absolute_value=True), (), "v")
        note("df_r1", "v")
        E("v", lambda e: e.reduce_sum(out=Mh[:, 16:17], in_=Mh[:, 0:16],
                                      axis=AX.X), [wt("df_r1")], "v")
        note("df_r2", "v")
        E("t", lambda e: e.matmul(
            puf[0:1, SP0:SP0 + 1], Mh[:, 16:17],
            onescol[:, :], start=True, stop=True),
          [wt("df_r2")], "t")
        note("df_mm", "t")
        vch(lambda e: e.tensor_scalar(
            out=arow[0:1, 16:17], in0=puf[0:1, SP0:SP0 + 1],
            scalar1=1.0 / DENOM, scalar2=None, op0=ALU.mult), [wt("df_mm")])
        vch(lambda e: e.tensor_scalar(
            out=arow[0:1, 17:18], in0=arow[0:1, 16:17], scalar1=0.1,
            scalar2=0.01, op0=ALU.mult, op1=ALU.add))
        vch(lambda e: e.tensor_tensor(
            out=arow[0:1, 18:19], in0=arow[0:1, 16:17], in1=arow[0:1, 17:18],
            op=ALU.is_lt))
        vch(lambda e: e.tensor_scalar(
            out=arow[0:1, 19:20], in0=arow[0:1, 18:19], scalar1=-1.0,
            scalar2=1.0, op0=ALU.mult, op1=ALU.add))
        note("df_s", "v")
        E("t", lambda e: e.matmul(
            puf[:, SP0 + 1:SP0 + 2], onesf[0:1, :], arow[0:1, 18:19],
            start=True, stop=True), [wt("df_s")])
        E("t", lambda e: e.matmul(
            puf[:, SP0 + 2:SP0 + 3], onesf[0:1, :], arow[0:1, 19:20],
            start=True, stop=True), (), "t")
        note("df_bc", "t")
        E("v", lambda e: e.tensor_copy(out=Mh[:, 20:22],
                                       in_=puf[:, SP0 + 1:SP0 + 3]),
          [wt("df_bc")], "v")
        note("df_cols", "v")
        E("v", lambda e: e.tensor_scalar(
            out=fmp[:, 3 * 320:4 * 320], in0=fmp[:, 2 * 320:3 * 320],
            scalar1=Mh[:, 21:22], scalar2=None, op0=ALU.mult),
          [wt("df_cols")], "v")
        E("v", lambda e: e.tensor_tensor(
            out=fmp[:, 5 * 320:6 * 320], in0=fmp[:, 5 * 320:6 * 320],
            in1=fmp[:, 3 * 320:4 * 320], op=ALU.add), (), "v")
        note("qup2", "v")
        yield
        iteration(2)
        yield
        E("v", lambda e: e.tensor_scalar(
            out=fmp[:, 3 * 320:4 * 320], in0=fmp[:, 2 * 320:3 * 320],
            scalar1=Mh[:, 20:21], scalar2=None, op0=ALU.mult),
          [wt("it2_it"), wt("df_cols")], "v")
        E("v", lambda e: e.scalar_tensor_tensor(
            out=fmp[:, 3 * 320:4 * 320], in0=fmp[:, 4 * 320:5 * 320],
            scalar=Mh[:, 21:22], in1=fmp[:, 3 * 320:4 * 320],
            op0=ALU.mult, op1=ALU.add), (), "v")
        note("at_fin", "v")
        E("v", lambda e: e.tensor_copy(out=fmb[:, 320:640],
                                       in_=fmp[:, 3 * 320:4 * 320]),
          [wt("at_fin"), wt("it2_lg")], "v")
        note("at_cast", "v")
        for nh in range(2):
            for dc in range(NDC):
                wl = [wt("awscr_done"), wt("at_cast")]
                if nh or dc:
                    wl.append(wt(f"oj_mm_{nh}_{dc - 2}") if dc >= 2 else
                              wt(f"oj_ev_{nh - 1}") if nh else wt("at_cast"))
                dma(wstg[:, (dc % 2) * D:(dc % 2) * D + D],
                    awscr[3 * D:4 * D, dc * 128:(dc + 1) * 128], wl, "da",
                    tp=True, note_name=f"oj_sl_{nh}_{dc}")
                E("t", lambda e, n_=nh, d=dc: e.matmul(
                    puf[0:16, 0:512],
                    fmb[:, 320 + d * 40 + 2:320 + d * 40 + 18],
                    wstg[:, (d % 2) * D + n_ * 512:(d % 2) * D + (n_ + 1) * 512],
                    start=(d == 0), stop=(d == NDC - 1)),
                  [wt(f"oj_sl_{nh}_{dc}")] + (pu_wait() if (nh == 0 and dc == 0)
                                              else []), "t")
                note(f"oj_mm_{nh}_{dc}", "t")
            E("s", lambda e, n_=nh: e.copy(
                out=at16[0:16, n_ * 512:(n_ + 1) * 512], in_=puf[0:16, 0:512]),
              [wt(f"oj_mm_{nh}_{NDC - 1}")], "s")
            note(f"oj_ev_{nh}", "s")
        pu_set("oj_ev_1")
        note("attn_o", "s")

        E("v", lambda e: e.tensor_scalar(
            out=at16[0:16, :], in0=at16[0:16, :], scalar1=swc[0:16, 4:5],
            scalar2=None, op0=ALU.mult),
          [wt("attn_o"), wt("a0_res"), wt("mcols")], "v")
        E("v", lambda e: e.tensor_tensor(
            out=xA[0][0:16, :], in0=xA[0][0:16, :], in1=at16[0:16, :],
            op=ALU.add), (), "v")
        note("attn_add", "v")
        yield

    attg = ph_attn_gen() if DO_ATTN else None
    order = []
    order += [("A", 0), ("MINI", 0), ("ATT", 1), ("Af", 1), ("ATT", 2),
              ("MA", 0), ("ATT", 3), ("Ar", 1), ("ATT", 4), ("Af", 2),
              ("ATT", 5), ("ATT", 6), ("MA", 1), ("ATT", 7), ("Ar", 2),
              ("MA", 2), ("ATT", 8), ("ATT", 9), ("ATT", 10), ("ATT", 11),
              ("ATT", 12), ("ATT", 13),
              ("B", 0), ("MB", 0), ("B", 1), ("A", 3),
              ("MA", 3), ("MB", 1), ("B", 2), ("A", 4), ("MA", 4), ("MB", 2),
              ("B", 3), ("A", 5), ("MA", 5), ("MB", 3), ("B", 4), ("A", 6),
              ("MA", 6), ("MB", 4), ("B", 5), ("A", 7), ("MA", 7), ("MB", 5),
              ("B", 6), ("MB", 6), ("B", 7), ("MB", 7)]
    for kind, i in order:
        if kind == "A":
            if i >= 3:
                emit_xload(i)
            ph_stage("a", i)
        elif kind == "Af":
            ph_stage("a", i, do_round=False)
        elif kind == "Ar":
            ph_round("a", i)
        elif kind == "B":
            ph_stage("b", i)
        elif kind == "MA":
            ph_mm2("a", i)
        elif kind == "MB":
            ph_mm2("b", i)
        elif kind == "MINI":
            if DO_MINI:
                ph_mini()
        elif kind == "ATT":
            if DO_ATTN:
                next(attg)

    E("t", lambda e: e.wait_ge(sems["do"], R.n("do")) if False else None) \
        if False else None
    R.st["t"].append((None, [(nm, 16 * dma_cnt[nm]) for nm in dma_pool["do"]],
                      None))

    with nc.Block() as block:
        def mk(eng_name):
            def body(eng):
                for fn, ws, inc in R.st[eng_name]:
                    for semn, val in ws:
                        eng.wait_ge(sems[semn], val)
                    if fn is None:
                        continue
                    inst = fn(eng)
                    if inc is not None:
                        semn, amt = inc
                        inst.then_inc(sems[semn], amt)
            return body

        block.tensor(mk("t"))
        block.vector(mk("v"))
        block.scalar(mk("s"))
        block.gpsimd(mk("g"))
        block.sync(mk("y"))

    es.close()
    return nc


_NC_CACHE = []


def kernel(**inputs):
    x = np.ascontiguousarray(np.asarray(inputs["L0_attn_x"], np.float32))
    w1 = np.ascontiguousarray(np.asarray(inputs["L0_ffn_w1"], np.float32))
    w2 = np.ascontiguousarray(np.asarray(inputs["L0_ffn_w2"], np.float32))
    cw = np.asarray(inputs["L0_ffn_conv_w"], np.float32)
    cb = np.asarray(inputs["L0_ffn_conv_b"], np.float32)
    ob = np.ascontiguousarray(np.asarray(inputs["L0_attn_ob"], np.float32))

    cwt = np.zeros((128, 4 * NC), np.float32)
    for k in range(3):
        cwt[:, k * NC:(k + 1) * NC] = cw[:, 0, k].reshape(NC, 128).T
    cwt[:, 3 * NC:] = cb.reshape(NC, 128).T
    qbv = np.zeros((128, 16), np.float32)
    qbv[:, 0:8] = np.asarray(inputs["L0_attn_qb"], np.float32).reshape(8, 128).T
    qbv[:, 8:16] = np.asarray(inputs["L0_attn_vb"], np.float32).reshape(8, 128).T
    lbv = np.zeros((128, 8), np.float32)
    for k, n in enumerate(("L0_attn_lqb", "L0_attn_lkb", "L0_attn_lvb",
                           "L0_attn_lnc_w", "L0_attn_lnd_w")):
        lbv[:, k] = np.tile(np.asarray(inputs[n], np.float32), 2)
    lab = np.zeros((128, 16), np.float32)
    lab[:, 0:8] = np.asarray(inputs["L0_attn_lna_w"], np.float32).reshape(8, 128).T
    lab[:, 8:16] = np.asarray(inputs["L0_attn_lnb_w"], np.float32).reshape(8, 128).T
    identf = np.eye(128, dtype=np.float32)
    identb = np.eye(128, dtype=np.float32).astype(ml_dtypes.bfloat16)
    onesf = np.ones((1, 128), np.float32)

    in_maps = []
    for core in range(8):
        b, q = core // 4, core % 4
        base = 2048 * q
        xs = np.zeros((S_TOK, D), np.float32)
        hi = min(base + S_TOK, 8192)
        xs[0:hi - base] = x[b, base:hi]
        xl = np.zeros((6, D), np.float32)
        lo2, hi2 = base - 4, base + 2
        s0, s1 = max(lo2, 0), min(hi2, 8192)
        xl[s0 - lo2:s1 - lo2] = x[b, s0:s1]
        xr = np.zeros((6, D), np.float32)
        lo2, hi2 = base + 2045, base + 2051
        s0, s1 = max(lo2, 0), min(hi2, 8192)
        xr[s0 - lo2:s1 - lo2] = x[b, s0:s1]
        xhh = np.zeros((40, D), np.float32)
        xhh[2:20] = x[b, 0:18]
        xhh[22:40] = x[1 - b, 0:18]
        maskv = np.zeros((1, 8), np.float32)
        maskv[0, 0] = 0.0 if base == 0 else 1.0
        maskv[0, 1] = 0.0 if base + 2048 >= 8192 else 1.0
        maskv[0, 2] = 1.0 if base == 0 else 0.0
        m = {
            "x_slice": xs, "x_left": xl, "x_right": xr, "x_head": xhh,
            "w1": w1, "w2": w2, "cwt": cwt, "ob": ob,
            "identf": identf, "identb": identb, "onesf": onesf,
            "onescol": np.ones((128, 1), np.float32),
            "maskv": maskv,
            "qw": np.ascontiguousarray(np.asarray(inputs["L0_attn_qw"], np.float32)),
            "kw": np.ascontiguousarray(np.asarray(inputs["L0_attn_kw"], np.float32)),
            "vw": np.ascontiguousarray(np.asarray(inputs["L0_attn_vw"], np.float32)),
            "ow": np.ascontiguousarray(np.asarray(inputs["L0_attn_ow"], np.float32)),
            "qbv": qbv, "lbv": lbv, "lab": lab,
            "lqw": np.ascontiguousarray(np.asarray(inputs["L0_attn_lqw"], np.float32)),
            "lkw": np.ascontiguousarray(np.asarray(inputs["L0_attn_lkw"], np.float32)),
            "lvw": np.ascontiguousarray(np.asarray(inputs["L0_attn_lvw"], np.float32)),
        }
        in_maps.append(m)

    nc = build_program()
    res = run_bass_kernel_spmd(nc, in_maps, core_ids=list(range(8)))
    out = np.zeros((2, 8192, D), np.float32)
    for core in range(8):
        b, q = core // 4, core % 4
        out[b, 2048 * q:2048 * (q + 1)] = res.results[core]["out"]
    return out


# version 16 (final)
# speedup vs baseline: 1.0689x; 1.0113x over previous
import contextlib
import numpy as np
import ml_dtypes
import concourse.bass as bass
import concourse.mybir as mybir
from concourse.bass_utils import run_bass_kernel_spmd

F32 = mybir.dt.float32
BF16 = mybir.dt.bfloat16
FP8 = mybir.dt.float8e4
ALU = mybir.AluOpType
AX = mybir.AxisListType
AF = mybir.ActivationFunctionType

D = 1024
HDIM = 4096
NT = 8
MMW = 258
FW = 256
NC = 32
NDC = 8
MAGIC = float(np.float32(1.5 * 2 ** 23))
EPS = 1e-5
S_TOK = 2049
DENOM = float(np.float32(2 * 16 * 8192 * 64))

DO_MINI = True
DO_ATTN = True

ENGS = ("t", "v", "s", "g", "y")


class Rec:
    def __init__(self):
        self.st = {e: [] for e in ENGS}
        self.cnt = {}
        self.lastw = {e: {} for e in ENGS}

    def do(self, eng, fn, waits, inc):
        flat = []
        for w in waits:
            if isinstance(w, list):
                flat.extend(w)
            else:
                flat.append(w)
        if eng in ("v", "s", "g"):
            flat.append((eng, self.cnt.get(eng, 0)))
            if inc is None:
                inc = (eng, 1)
        ws = []
        for sem, val in flat:
            if val > 0 and self.lastw[eng].get(sem, -1) < val:
                ws.append((sem, val))
                self.lastw[eng][sem] = val
        self.st[eng].append((fn, ws, inc))
        if inc is not None:
            sem, amt = inc
            self.cnt[sem] = self.cnt.get(sem, 0) + amt
            return self.cnt[sem]
        return None

    def n(self, sem):
        return self.cnt.get(sem, 0)


def build_program():
    nc = bass.Bass("TRN2", target_bir_lowering=False, debug=False)

    def din(name, shape, dt=F32):
        return nc.dram_tensor(name, shape, dt, kind="ExternalInput").ap()

    x_d = din("x_slice", [S_TOK, D])
    xl_d = din("x_left", [6, D])
    xr_d = din("x_right", [6, D])
    xh_d = din("x_head", [40, D])
    w1_d = din("w1", [HDIM, D])
    w2_d = din("w2", [D, HDIM])
    cwt_d = din("cwt", [128, 4 * NC])
    ob_d = din("ob", [D])
    idf_d = din("identf", [128, 128])
    idb_d = din("identb", [128, 128], BF16)
    ones_d = din("onesf", [1, 128])
    onesc_d = din("onescol", [128, 1])
    msk_d = din("maskv", [1, 8])
    aw_d = [din(n, [D, D]) for n in ("qw", "kw", "vw", "ow")]
    qbv_d = din("qbv", [128, 16])
    lbv_d = din("lbv", [128, 8])

    lab_d = din("lab", [128, 16])
    lw_d = [din(n, [64, 64]) for n in ("lqw", "lkw", "lvw")]

    out_d = nc.dram_tensor("out", [2048, D], F32, kind="ExternalOutput").ap()

    w1scr = nc.dram_tensor("w1scr", [HDIM, D], BF16, kind="Internal").ap()
    w2scr = nc.dram_tensor("w2scr", [D, HDIM], BF16, kind="Internal").ap()
    awscr = nc.dram_tensor("awscr", [4 * D, D], BF16, kind="Internal").ap()

    es = contextlib.ExitStack()
    sbt = lambda n, shp, dt=F32: es.enter_context(nc.sbuf_tensor(n, shp, dt))
    pst = lambda n, shp, dt=F32: es.enter_context(nc.psum_tensor(n, shp, dt))

    w1q = sbt("w1q", [128, NDC * HDIM], FP8)
    w2q = sbt("w2q", [128, NC * D], FP8)
    hbuf = sbt("hbuf", [128, NC * MMW])
    xqma = sbt("xqma", [128, NC * FW], BF16)
    xqmb = sbt("xqmb", [128, NC * FW], BF16)
    xA = [sbt(f"xA{r}", [128, D]) for r in range(3)]
    xB = [sbt(f"xB{r}", [128, D]) for r in range(3)]
    poolf = sbt("poolf", [128, D])
    dump = sbt("dump", [128, D])
    xq1 = sbt("xq1", [128, 2 * D], BF16)
    xq1e = sbt("xq1e", [128, D], BF16)
    xq1x = xq1e
    xq1T = sbt("xq1T", [128, NDC * MMW], BF16)
    tmpA = sbt("tmpA", [128, FW])
    tmpB = sbt("tmpB", [128, FW])
    absT = sbt("absT", [128, FW])
    Mt = sbt("Mt", [128, FW])
    bcast = sbt("bcast", [128, 352])
    obt = sbt("obt", [128, D])
    cwt = sbt("cwt_s", [128, 4 * NC])
    cols = sbt("cols", [128, 48])
    cols2 = sbt("cols2", [128, 24])
    colsh = sbt("colsh", [64, 48])
    idf = sbt("idf", [128, 128])
    idb = sbt("idb", [128, 128], BF16)
    onesf = sbt("ones_s", [1, 128])
    onescol = sbt("onescol_s", [128, 1])
    swc = sbt("swc", [128, 8])
    mskv = sbt("mskv", [1, 8])
    macc = sbt("macc", [128, 2])
    arow = sbt("arow", [1, 64])
    a2row = sbt("a2row", [1, MMW])
    wstg = sbt("wstg", [128, 2 * D], BF16)
    xh = sbt("xh", [64, D])
    scrA = sbt("scrA", [64, D])
    scrB = sbt("scrB", [64, D])
    xqh = sbt("xqh", [64, D], BF16)
    vib = xqmb[0:16, 3840:5888]
    at16 = xh
    xqhT = sbt("xqhT", [128, NDC * 40], BF16)
    xqmh_big = sbt("xqmh_big", [128, NC * 40 + 2 * 320], BF16)
    xqmh = xqmh_big[:, 0:NC * 40]
    fmb = xqmh_big[:, NC * 40:NC * 40 + 640]
    bdg = sbt("bdg", [128, 3 * 128], BF16)
    lbv = sbt("lbv_s", [128, 8])
    qbv = sbt("qbv_s", [128, 16])
    lab = sbt("lab_s", [128, 16])

    habc = sbt("habc", [64, 80])
    absh = sbt("absh", [128, 40])
    Mh = sbt("Mh", [128, 40])

    xqmTm = sbt("xqmTm", [128, NDC * 12], BF16)

    xqm_m = sbt("xqm_m", [128, NC * 2], BF16)
    PTb = None

    PTb = wstg[0:16, 0:512]
    xqmb_fv = xqmb[:, :].bitcast(F32)
    fmp = xqmb_fv[:, 0:1920]
    hh = xqmb_fv[:, 1920:3200]
    hm2 = xqmb_fv[:, 3200:3584]
    minq = xqh
    fmini = scrA
    minx = wstg[:, :].bitcast(F32)
    pm1 = [pst(f"pm1_{i}", [128, 512]) for i in range(3)]
    pu = pst("pu", [128, 1024], BF16)
    pm2 = [pst("pm2a", [128, 1024]), pst("pm2b", [128, 1024])]
    puf = pu[:, :].bitcast(F32)

    UF0 = 129
    SP0 = 385

    sems = {}
    DMA_K = {"dx": 4, "dw": 6, "da": 4, "do": 4, "dg": 4}
    dma_pool = {}
    dma_cnt = {}
    dma_rr = {}
    for cls, K in DMA_K.items():
        dma_pool[cls] = []
        for i in range(K):
            nm = f"{cls}{i}"
            sems[nm] = es.enter_context(nc.semaphore(nm))
            dma_pool[cls].append(nm)
            dma_cnt[nm] = 0
        dma_rr[cls] = 0
    for n in ("t", "v", "s", "g"):
        sems[n] = es.enter_context(nc.semaphore(n))

    R = Rec()
    ev = {}

    def E(eng, fn, w=(), inc=None, amt=1):
        return R.do(eng, fn, w, (inc, amt) if inc else None)

    def note(name, semname):
        ev[name] = (semname, R.n(semname))

    def wt(name):
        return ev[name]

    def gnote(name, parts):
        acc = []
        for p in parts:
            v = ev[p]
            if isinstance(v, list):
                acc.extend(v)
            else:
                acc.append(v)
        ev[name] = acc

    def dma(out, in_, w=(), sem_="dw", tp=False, eng="y", note_name=None):
        cls = sem_
        i = dma_rr[cls] % DMA_K[cls]
        dma_rr[cls] += 1
        nm = dma_pool[cls][i]
        ws = list(w)
        if dma_cnt[nm] > 0:
            ws.append((nm, 16 * dma_cnt[nm]))
        if tp:
            E(eng, lambda e, o=out, i2=in_: e.dma_start(out=o, in_=i2,
                                                        transpose=True), ws, nm, 16)
        else:
            E(eng, lambda e, o=out, i2=in_: e.dma_start(out=o, in_=i2), ws, nm, 16)
        dma_cnt[nm] += 1
        if note_name is not None:
            ev[note_name] = (nm, 16 * dma_cnt[nm])

    p64_user = [None]
    p0_user = [None]
    p32_user = [None]

    def emit_xload(i):
        r = i % 3
        w0 = 256 * i
        wl = [wt(f"xfree{i - 3}")] if i >= 3 else []
        if i >= 3:
            wl.append(wt(f"b{i - 2}_efill"))
        if i >= 4:
            wl.append(wt(f"b{i - 4}_xfill"))
        dma(xA[r][:, :], x_d[w0:w0 + 128, :], wl, "dx", note_name=f"xload{i}_a")
        dma(xB[r][:, :], x_d[w0 + 128:w0 + 256, :], wl, "dx", note_name=f"xload{i}_b")
        gnote(f"xload{i}", [f"xload{i}_a", f"xload{i}_b"])

    for i in range(3):
        emit_xload(i)
    dma(minx[0:6, :], xl_d[:, :], (), "dx", note_name="xs_l")
    dma(minx[32:38, :], xr_d[:, :], (), "dx", note_name="xs_r")
    dma(xh[0:40, :], xh_d[:, :], (), "dx", note_name="xs_h")
    gnote("xsmall", ["xs_l", "xs_r", "xs_h"])
    cparts = []
    for j, (t_, d_) in enumerate([(obt, None), (idf, idf_d), (idb, idb_d),
                                  (onesf, ones_d), (onescol, onesc_d),
                                  (mskv, msk_d), (cwt, cwt_d), (qbv, qbv_d),
                                  (lbv, lbv_d), (lab, lab_d)]):
        if d_ is None:
            dma(obt[:, :], ob_d.unsqueeze(0).partition_broadcast(128),
                note_name=f"cst{j}")
        else:
            dma(t_[:, :], d_[:, :], note_name=f"cst{j}")
        cparts.append(f"cst{j}")
    gnote("consts", cparts)
    E("v", lambda e: e.tensor_tensor(out=lbv[:, 5:6], in0=lbv[:, 3:4],
                                     in1=lbv[:, 4:5], op=ALU.mult),
      [wt("consts")], "v")
    note("gcol", "v")
    for kk, dst in ((2, 4), (0, 5), (1, 6)):
        E("t", lambda e, k=kk, d_=dst: e.matmul(
            puf[:, SP0 + d_:SP0 + d_ + 1], onesf[0:1, :], mskv[0:1, k:k + 1],
            start=True, stop=True), [wt("consts")], "t")
    note("mbc", "t")
    E("v", lambda e: e.tensor_copy(out=swc[:, 4:7], in_=puf[:, SP0 + 4:SP0 + 7]),
      [wt("mbc")], "v")
    note("mcols", "v")
    for t_ in (cols, cols2, colsh, habc, arow, a2row, Mh, Mt, absT, tmpA, tmpB):
        E("v", lambda e, t2=t_: e.memset(t2[:, :], 0.0), (), "v")
    note("zinit", "v")

    xqmb_f = xqmb[:, :].bitcast(F32)

    def emit_wquant_gen(which):
        wd = w1_d if which == 0 else w2_d
        scr = w1scr if which == 0 else w2scr
        R_, C_ = (HDIM, D) if which == 0 else (D, HDIM)
        CH = D
        nch, nsub = R_ // 128, C_ // CH
        NKC = nch * nsub
        NS = 6 if which == 0 else 4
        if which == 0:
            stg = [hbuf[:, s * CH:(s + 1) * CH] for s in range(NS)]
        else:
            stg = [xqmb_f[:, s * CH:(s + 1) * CH] for s in range(NS)]
        if which == 0:
            bst = [xqma[:, s * CH:(s + 1) * CH] for s in range(4)]
        else:
            bst = [xq1[:, 0:CH], xq1[:, CH:2 * CH]]
        NB_ = len(bst)
        E("v", lambda e, w_=which: e.memset(macc[:, w_:w_ + 1], 0.0), (), "v")
        note(f"wm0_{which}", "v")
        for k in range(NKC):
            ci, si = divmod(k, nsub)
            sl = stg[k % NS]
            wl = [wt(f"wred{which}_{k - NS}")] if k >= NS else []
            dma(sl, wd[ci * 128:(ci + 1) * 128, si * CH:(si + 1) * CH], wl,
                "dw" if which == 0 else "dg",
                eng="y" if which == 0 else "g", note_name=f"wld{which}_{k}")
            E("v", lambda e, s=sl: e.reduce_sum(
                out=tmpA[:, 0:1], in_=s, axis=AX.X, apply_absolute_value=True),
              [wt(f"wld{which}_{k}"), wt(f"wm0_{which}")] +
              ([wt(f"wred{which}_{k - 1}")] if k else []), "v")
            note(f"wabs{which}_{k}", "v")
            E("v", lambda e, w_=which: e.tensor_tensor(
                out=macc[:, w_:w_ + 1], in0=macc[:, w_:w_ + 1],
                in1=tmpA[:, 0:1], op=ALU.add), [wt(f"wabs{which}_{k}")], "v")
            note(f"wred{which}_{k}", "v")
            yield
        E("t", lambda e, w_=which: e.matmul(
            puf[0:1, SP0 + 8 + w_:SP0 + 9 + w_], macc[:, w_:w_ + 1],
            onescol[:, :], start=True, stop=True),
          [wt(f"wred{which}_{NKC - 1}")], "t")
        note(f"wsum{which}", "t")
        E("v", lambda e, w_=which, rc=float(R_) * float(C_): e.tensor_scalar(
            out=arow[0:1, w_:w_ + 1], in0=puf[0:1, SP0 + 8 + w_:SP0 + 9 + w_],
            scalar1=1.0 / rc, scalar2=1e-5, op0=ALU.mult, op1=ALU.max),
          [wt(f"wsum{which}")], "v")
        note(f"wmean{which}", "v")
        E("v", lambda e, w_=which: e.reciprocal(
            out=arow[0:1, 4 + w_:5 + w_], in_=arow[0:1, w_:w_ + 1]),
          [wt(f"wmean{which}")], "v")
        note(f"wrec{which}", "v")
        E("t", lambda e, w_=which: e.matmul(
            puf[:, SP0 + 10:SP0 + 11], onesf[0:1, :], arow[0:1, 4 + w_:5 + w_],
            start=True, stop=True), [wt(f"wrec{which}")])
        E("t", lambda e, w_=which: e.matmul(
            puf[:, SP0 + 11:SP0 + 12], onesf[0:1, :], arow[0:1, w_:w_ + 1],
            start=True, stop=True), (), "t")
        note(f"wbc{which}", "t")
        E("v", lambda e, w_=which: e.tensor_copy(
            out=swc[:, w_:w_ + 1], in_=puf[:, SP0 + 10:SP0 + 11]), [wt(f"wbc{which}")])
        if which == 1:
            E("v", lambda e: e.tensor_scalar(
                out=swc[:, 2:3], in0=puf[:, SP0 + 11:SP0 + 12],
                scalar1=0.5 / 127.0, scalar2=None, op0=ALU.mult), (), "v")
        else:
            E("v", lambda e: e.tensor_copy(
                out=swc[:, 3:4], in_=puf[:, SP0 + 11:SP0 + 12]), (), "v")
        note(f"wsw{which}", "v")
        for k in range(NKC):
            ci, si = divmod(k, nsub)
            sl, b_ = stg[k % NS], bst[k % NB_]
            wl = [wt(f"wq{which}_{k - NS}") if k >= NS else
                  wt(f"wred{which}_{NKC - 1}")]
            if k >= NB_:
                wl.append(wt(f"wbst{which}_{k - NB_}"))
            dma(sl, wd[ci * 128:(ci + 1) * 128, si * CH:(si + 1) * CH], wl,
                "dw" if which == 0 else "dg",
                eng="y" if which == 0 else "g", note_name=f"wld2{which}_{k}")
            E("v", lambda e, s=sl, w_=which: e.tensor_scalar(
                out=s, in0=s, scalar1=swc[:, w_:w_ + 1], scalar2=MAGIC,
                op0=ALU.mult, op1=ALU.add),
              [wt(f"wld2{which}_{k}"), wt(f"wsw{which}")], "v")
            note(f"wqa{which}_{k}", "v")
            E("v", lambda e, s=sl: e.tensor_scalar(
                out=s, in0=s, scalar1=MAGIC, scalar2=1.0,
                op0=ALU.subtract, op1=ALU.min), [wt(f"wqa{which}_{k}")], "v")
            note(f"wqb{which}_{k}", "v")
            E("v", lambda e, s=sl, b=b_: e.tensor_scalar(
                out=b, in0=s, scalar1=-1.0, scalar2=None, op0=ALU.max),
              [wt(f"wqb{which}_{k}")], "v")
            note(f"wq{which}_{k}", "v")
            dma(scr[ci * 128:(ci + 1) * 128, si * CH:(si + 1) * CH], b_,
                [wt(f"wq{which}_{k}")], "dw" if which == 0 else "dg",
                eng="y" if which == 0 else "g", note_name=f"wbst{which}_{k}")
            yield
        gnote(f"wscr{which}", [f"wbst{which}_{k}" for k in range(NKC)])
        tgt = w1q if which == 0 else w2q
        nslab = C_ // 128
        hb16 = hbuf[:, :].bitcast(BF16)
        if which == 0:
            ts_ = [hb16[:, s * R_:(s + 1) * R_] for s in range(4)]
        else:
            ts_ = [xqma[:, 4 * CH + s * R_:4 * CH + (s + 1) * R_]
                   for s in range(4)]
        for s_i in range(nslab):
            tstg = ts_[s_i % 4]
            wl = [wt(f"wscr{which}")]
            if which == 0:
                wl.append(wt("wld1_%d" % (4 * nsub)
                             if False else f"wred{which}_{NKC - 1}"))
            if s_i >= 4:
                wl.append(wt(f"wq8{which}_{s_i - 4}"))
            dma(tstg, scr[:, s_i * 128:(s_i + 1) * 128], wl,
                "dw" if s_i % 2 == 0 else "da", tp=True,
                note_name=f"wtp{which}_{s_i}")
            E("v", lambda e, t_=tstg, o=tgt[:, s_i * R_:(s_i + 1) * R_]:
              e.tensor_copy(out=o, in_=t_), [wt(f"wtp{which}_{s_i}")], "v")
            note(f"wq8{which}_{s_i}", "v")
            yield
        note(f"wdone{which}", "v")

    _wg = [emit_wquant_gen(0), emit_wquant_gen(1)]
    while _wg:
        for g_ in list(_wg):
            try:
                next(g_)
            except StopIteration:
                _wg.remove(g_)

    if DO_ATTN:
        xqma_f = xqma[:, :].bitcast(F32)
        poolf_b = poolf[:, :].bitcast(BF16)
        for wi in range(4):
            for ci in range(8):
                k = wi * 8 + ci
                stg = xqma_f[:, (k % 2) * D:(k % 2) * D + D]
                bst = poolf_b[:, (k % 2) * D:(k % 2) * D + D]
                wl = [wt("wscr0")]
                if k >= 2:
                    wl.append(wt(f"awcast{k - 2}"))
                dma(stg, aw_d[wi][ci * 128:(ci + 1) * 128, :], wl, "da", note_name=f"awld{k}")
                E("v", lambda e, s=stg, b=bst: e.tensor_copy(out=b, in_=s),
                  [wt(f"awld{k}")] +
                  ([wt(f"awst{k - 2}")] if k >= 2 else [wt("wscr0")]), "v")
                note(f"awcast{k}", "v")
                dma(awscr[wi * D + ci * 128:wi * D + (ci + 1) * 128, :], bst,
                    [wt(f"awcast{k}")], "da", note_name=f"awst{k}")
        note("awq_done", "v")
        gnote("awscr_done", [f"awst{k}" for k in range(32)])
        gnote("awpf", ["awst30", "awst31"])
        p0_user[0] = "awpf"
        p32_user[0] = "awpf"
        p64_user[0] = "awpf"

    CB = {"E": 0, "A": 12, "B": 24, "X": 36}

    def vch(fn, w=()):
        E("v", fn, list(w) + [("v", R.n("v"))], "v")

    def emit_qin(tag, sets, ct, srcdeps):
        srcdeps = list(srcdeps) + [wt("zinit")]
        prev_sq = None
        for (src, p0, n_, cb, dst, pstg, mcol) in sets:
            E("v", lambda e, s=src, c=ct[p0:p0 + n_, cb:cb + 1]:
              e.tensor_reduce(out=c, in_=s, axis=AX.X, op=ALU.add), srcdeps, "v")
            E("v", lambda e, s=src, c=ct[p0:p0 + n_, cb + 3:cb + 4]:
              e.tensor_reduce(out=c, in_=s, axis=AX.X, op=ALU.max), (), "v")
            E("v", lambda e, s=src, c=ct[p0:p0 + n_, cb + 4:cb + 5]:
              e.tensor_reduce(out=c, in_=s, axis=AX.X, op=ALU.min), (), "v")
            E("s", lambda e, s=src, c=ct[p0:p0 + n_, cb + 1:cb + 2], pg=pstg:
              e.activation(out=pg, in_=s, func=AF.Square, accum_out=c),
              list(srcdeps) + ([wt(prev_sq)] if prev_sq else []), "s")
            note(f"{tag}_sq_{cb}", "s")
            prev_sq = f"{tag}_sq_{cb}"
        note(f"{tag}_red", "v")
        note(f"{tag}_sq", "s")
        gdep = [wt(f"{tag}_red"), wt(f"{tag}_sq"), wt("zinit")]
        c = lambda k, _t=ct: _t[:, k::12] if _t.shape[0] == 128 else _t[0:_t.shape[0], k::12]
        cc = lambda k, _t=ct: _t[0:64, k::12] if _t is colsh else _t[:, k::12]
        c = cc
        vch(lambda e: e.tensor_scalar(out=c(0), in0=c(0), scalar1=1.0 / D,
                                      scalar2=None, op0=ALU.mult), gdep)
        vch(lambda e: e.tensor_scalar(out=c(1), in0=c(1), scalar1=1.0 / D,
                                      scalar2=None, op0=ALU.mult))
        vch(lambda e: e.tensor_tensor(out=c(11), in0=c(0), in1=c(0), op=ALU.mult))
        vch(lambda e: e.tensor_tensor(out=c(2), in0=c(1), in1=c(11), op=ALU.subtract))
        vch(lambda e: e.tensor_scalar(out=c(2), in0=c(2), scalar1=EPS,
                                      scalar2=None, op0=ALU.add))
        vch(lambda e: e.tensor_tensor(out=c(3), in0=c(3), in1=c(0), op=ALU.subtract))
        vch(lambda e: e.tensor_tensor(out=c(4), in0=c(0), in1=c(4), op=ALU.subtract))
        vch(lambda e: e.tensor_tensor(out=c(4), in0=c(3), in1=c(4), op=ALU.max))
        note(f"{tag}_pre", "v")
        E("s", lambda e: e.activation(out=c(5), in_=c(2), func=AF.Sqrt),
          [wt(f"{tag}_pre")], "s")
        note(f"{tag}_sqrt", "s")
        vch(lambda e: e.reciprocal(out=c(11), in_=c(5)), [wt(f"{tag}_sqrt")])
        for _ in range(2):
            vch(lambda e: e.tensor_tensor(out=c(5), in0=c(2), in1=c(11), op=ALU.mult))
            vch(lambda e: e.tensor_tensor(out=c(5), in0=c(5), in1=c(11), op=ALU.mult))
            vch(lambda e: e.tensor_scalar(out=c(5), in0=c(5), scalar1=-0.5,
                                          scalar2=1.5, op0=ALU.mult, op1=ALU.add))
            vch(lambda e: e.tensor_tensor(out=c(11), in0=c(11), in1=c(5), op=ALU.mult))
        vch(lambda e: e.tensor_tensor(out=c(6), in0=c(11), in1=c(4), op=ALU.mult))
        vch(lambda e: e.tensor_scalar(out=c(6), in0=c(6), scalar1=1e-5,
                                      scalar2=None, op0=ALU.max))
        vch(lambda e: e.tensor_scalar(out=c(9), in0=c(6), scalar1=1.0 / 127.0,
                                      scalar2=None, op0=ALU.mult))
        pn = 64 if ct is colsh else 128
        vch(lambda e, p_=pn: e.tensor_scalar(
            out=c(9), in0=c(9), scalar1=swc[0:p_, 3:4], scalar2=None,
            op0=ALU.mult))
        vch(lambda e: e.reciprocal(out=c(10), in_=c(6)))
        vch(lambda e: e.tensor_tensor(out=c(7), in0=c(11), in1=c(10), op=ALU.mult))
        vch(lambda e: e.tensor_scalar(out=c(7), in0=c(7), scalar1=127.0,
                                      scalar2=None, op0=ALU.mult))
        vch(lambda e: e.tensor_tensor(out=c(8), in0=c(0), in1=c(7), op=ALU.mult))
        vch(lambda e: e.tensor_scalar(out=c(8), in0=c(8), scalar1=-1.0,
                                      scalar2=None, op0=ALU.mult))
        note(f"{tag}_cols", "v")
        prev_rn = None
        for (src, p0, n_, cb, dst, pstg, mcol) in sets:
            E("s", lambda e, s=src, p=p0, n2=n_, b=cb, pg=pstg:
              e.activation(out=pg, in_=s, func=AF.Identity,
                           scale=ct[p:p + n2, b + 7:b + 8],
                           bias=ct[p:p + n2, b + 8:b + 9]),
              [wt(f"{tag}_cols")] + ([wt(prev_rn)] if prev_rn else []), "s")
            note(f"{tag}_p0_{cb}", "s")
            E("v", lambda e, d=dst, pg=pstg: e.tensor_scalar(
                out=d, in0=pg, scalar1=MAGIC, scalar2=MAGIC,
                op0=ALU.add, op1=ALU.subtract), [wt(f"{tag}_p0_{cb}")], "v")
            if mcol is not None:
                E("v", lambda e, d=dst, m=mcol: e.tensor_scalar(
                    out=d, in0=d, scalar1=m, scalar2=None, op0=ALU.mult), (), "v")
            note(f"{tag}_rn_{cb}", "v")
            prev_rn = f"{tag}_rn_{cb}"
        note(f"{tag}_done", "v")

    pu_read = [None]

    def pu_wait():
        return [wt(pu_read[0])] if pu_read[0] else []

    def pu_set(name):
        pu_read[0] = name

    def emit_bcast(tag, pieces, width, extradep=(), dst0=0):
        off = 0
        offs = []
        for (ct_, p0, n_, colk, dstc) in pieces:
            offs.append(off)
            off += n_
        groups = []
        for j, pc in enumerate(pieces):
            if groups and pieces[j - 1][1] == pc[1]:
                groups[-1].append(j)
            else:
                groups.append([j])
        prevc = None
        for gi, grp in enumerate(groups):
            wl = (list(extradep) + pu_wait()) if gi == 0 else [wt(prevc)]
            for jj, j in enumerate(grp):
                ct_, p0, n_, colk, dstc = pieces[j]
                E("t", lambda e, _t=ct_, p=p0, n2=n_, k=colk, o=offs[j]:
                  e.transpose(puf[0:1, UF0 + o:UF0 + o + n2],
                              _t[p:p + n2, k:k + 1], idf[p:p + n2, p:p + n2]),
                  wl if jj == 0 else (),
                  "t" if jj == len(grp) - 1 else None)
            note(f"{tag}_t_{gi}", "t")
            for jj, j in enumerate(grp):
                ct_, p0, n_, colk, dstc = pieces[j]
                E("v", lambda e, d=dstc, n2=n_, o=offs[j]: e.tensor_copy(
                    out=a2row[0:1, d:d + n2],
                    in_=puf[0:1, UF0 + o:UF0 + o + n2]),
                  [wt(f"{tag}_t_{gi}")] if jj == 0 else (),
                  "v" if jj == len(grp) - 1 else None)
            note(f"{tag}_c_{gi}", "v")
            prevc = f"{tag}_c_{gi}"
        note(f"{tag}_row", "v")
        E("t", lambda e, w_=width: e.matmul(
            puf[:, UF0:UF0 + w_], onesf[0:1, :], a2row[0:1, 0:w_],
            start=True, stop=True), [wt(f"{tag}_row")], "t")
        note(f"{tag}_mm", "t")
        E("v", lambda e, w_=width, d0=dst0: e.tensor_copy(
            out=bcast[:, d0:d0 + w_], in_=puf[:, UF0:UF0 + w_]),
          [wt(f"{tag}_mm")], "v")
        note(f"{tag}_bc", "v")
        pu_set(f"{tag}_bc")

    prev_round = {}
    prev_ma = {"a": None, "b": None}
    pm2_read = [None, None]

    def hsl(c):
        return hbuf[:, c * MMW:(c + 1) * MMW]

    def ph_stage(st, i, do_round=True):
        tag = f"{st}{i}"
        xq1a = xq1[:, 0:D]
        xq1b = xq1[:, D:2 * D]
        er = 0 if st == "a" else 32
        cb2 = (i % 2) * 8 if st == "a" else 16
        xqm = xqma if st == "a" else xqmb
        if st == "a":
            wl = [wt(p0_user[0])] if p0_user[0] else []
            if i == 0:
                dma(poolf[0:1, :], xl_d[3:4, :], wl, "dx",
                    note_name=f"{tag}_efill")
            else:
                dma(poolf[0:1, :], x_d[256 * i - 1:256 * i, :], wl, "dx", note_name=f"{tag}_efill")
            p0_user[0] = f"{tag}_qdone"
            wl = [wt(p64_user[0])] if p64_user[0] else []
            dma(poolf[64:65, :], x_d[256 * i + 256:256 * i + 257, :], wl, "dx", note_name=f"{tag}_xfill")
            p64_user[0] = f"{tag}_qdone"
        else:
            if i == 0:
                wl = [wt(p32_user[0])] if p32_user[0] else []
                wl.append(wt("mini_done") if DO_MINI else wt(f"a0_res"))
                dma(poolf[32:33, :], xh[62:63, :], wl, "dg", eng="g",
                    note_name=f"{tag}_efill")
            p32_user[0] = f"{tag}_qdone"
            wl = [wt(p64_user[0])] if p64_user[0] else []
            if i == NT - 1:
                src = xh[63:64, :]
                wl.append(wt("mini_done") if DO_MINI else wt(f"a{i}_res"))
            else:
                src = xA[(i + 1) % 3][0:1, :]
                wl.append(wt(f"a{i + 1}_res"))
            dma(poolf[64:65, :], src, wl, "dg", eng="g",
                note_name=f"{tag}_xfill")
            p64_user[0] = f"{tag}_qdone"
        srcdeps = [wt(f"{tag}_efill"), wt(f"{tag}_xfill")]
        if st == "a":
            srcdeps.append(wt(f"xload{i}"))
        else:
            srcdeps.append(wt(f"a{i}_res"))
            if i == 0 and DO_ATTN:
                srcdeps.append(wt("attn_add"))
        emcol = swc[er:er + 1, 5:6] if (st == "b" and i == 0) else None
        xmcol = swc[64:65, 6:7] if (st == "b" and i == NT - 1) else None
        sets = [
            (poolf[er:er + 1, :], er, 1, CB["E"], xq1e[er:er + 1, :],
             dump[er:er + 1, :], emcol),
            (xA[i % 3][:, :], 0, 128, CB["A"], xq1a, dump[:, :], None),
            (xB[i % 3][:, :], 0, 128, CB["B"], xq1b, dump[:, :], None),
            (poolf[64:65, :], 64, 1, CB["X"], xq1x[64:65, :],
             dump[64:65, :], xmcol),
        ]
        sets[1] = (xA[i % 3][:, :], 0, 128, CB["A"], xq1a, dump[:, :], None)
        emit_qin(tag, sets, cols, srcdeps)
        note(f"{tag}_qdone", "v")
        emit_bcast(f"{tag}_al", [
            (cols, er, 1, CB["E"] + 9, 0),
            (cols, 0, 128, CB["A"] + 9, 1),
            (cols, 0, 128, CB["B"] + 9, 129),
            (cols, 64, 1, CB["X"] + 9, 257),
        ], MMW, extradep=[wt(f"{tag}_cols")])
        for dc in range(NDC):
            tdep = pu_wait()
            if dc >= 1:
                tdep = [wt(f"{tag}_tc_{dc - 1}")]
            E("t", lambda e, s=xq1a[:, dc * 128:(dc + 1) * 128]:
              e.transpose(pu[:, 0:128], s, idb[:, :]),
              tdep + [wt(f"{tag}_done")])
            E("t", lambda e, s=xq1b[:, dc * 128:(dc + 1) * 128]:
              e.transpose(pu[:, 128:256], s, idb[:, :]))
            E("t", lambda e, s=xq1e[er:er + 1, dc * 128:(dc + 1) * 128]:
              e.transpose(pu[:, 256:257], s, idb[er:er + 1, er:er + 1]))
            E("t", lambda e, s=xq1x[64:65, dc * 128:(dc + 1) * 128]:
              e.transpose(pu[:, 258:259], s, idb[64:65, 64:65]), (), "t")
            note(f"{tag}_tp_{dc}", "t")
            E("s", lambda e, d=dc: e.copy(
                out=xq1T[:, d * MMW + 1:d * MMW + 257], in_=pu[:, 0:256]),
              [wt(f"{tag}_tp_{dc}")], "s")
            E("s", lambda e, d=dc: e.copy(
                out=xq1T[:, d * MMW:d * MMW + 1], in_=pu[:, 256:257]))
            E("s", lambda e, d=dc: e.copy(
                out=xq1T[:, d * MMW + 257:d * MMW + 258], in_=pu[:, 258:259]),
              (), "s")
            note(f"{tag}_tc_{dc}", "s")
        pu_set(f"{tag}_tc_{NDC - 1}")
        note(f"{tag}_T", "s")
        pm1b = [pm1[0][:, 0:MMW], pm1[1][:, 0:MMW], pm1[2][:, 0:MMW],
                pm2[0][:, 0:MMW], pm2[0][:, 512:512 + MMW], pm2[1][:, 0:MMW]]
        tA = [tmpA[:, 0:FW], absT[:, 0:FW]]
        tB = [tmpB[:, 0:FW], dump[:, 0:FW]]
        for k in range(NC + 3):
            if k < NC:
                cch = k
                wl = [wt(f"{tag}_T"), wt(f"{tag}_al_bc")]
                if cch >= 6:
                    wl.append(wt(f"{tag}_ev_{cch - 6}"))
                if cch in (3, 4) and pm2_read[0]:
                    wl.append(wt(pm2_read[0]))
                if cch == 5 and pm2_read[1]:
                    wl.append(wt(pm2_read[1]))
                for dc in range(NDC):
                    E("t", lambda e, c2=cch, d=dc: e.matmul(
                        pm1b[c2 % 6],
                        w1q[:, d * HDIM + c2 * 128:d * HDIM + (c2 + 1) * 128],
                        xq1T[:, d * MMW:(d + 1) * MMW],
                        start=(d == 0), stop=(d == NDC - 1)),
                      wl if dc == 0 else (), "t" if dc == NDC - 1 else None)
                note(f"{tag}_mm1_{cch}", "t")
                wl = [wt(f"{tag}_mm1_{cch}")]
                if prev_round.get(cch):
                    wl.append(wt(prev_round[cch]))
                E("s", lambda e, c2=cch: e.copy(out=hsl(c2),
                                                in_=pm1b[c2 % 6]),
                  wl, "s")
                note(f"{tag}_ev_{cch}", "s")
            if 0 <= k - 1 < NC:
                cch = k - 1
                E("g", lambda e, c2=cch: e.tensor_tensor(
                    out=hsl(c2), in0=hsl(c2), in1=bcast[:, 0:MMW], op=ALU.mult),
                  [wt(f"{tag}_ev_{cch}"), wt(f"{tag}_al_bc")], "g")
                note(f"{tag}_hs_{cch}", "g")
                wl = [wt(f"{tag}_hs_{cch}")]
                if cch >= 2:
                    wl.append(wt(f"{tag}_B_{cch - 2}"))
                E("s", lambda e, c2=cch: e.activation(
                    out=tA[c2 % 2],
                    in_=hsl(c2)[:, 1:257], func=AF.Identity,
                    scale=cwt[:, NC + c2:NC + c2 + 1],
                    bias=cwt[:, 3 * NC + c2:3 * NC + c2 + 1]), wl, "s")
                note(f"{tag}_A_{cch}", "s")
            if 0 <= k - 2 < NC:
                cch = k - 2
                E("v", lambda e, c2=cch: e.scalar_tensor_tensor(
                    out=tB[c2 % 2], in0=hsl(c2)[:, 0:256],
                    scalar=cwt[:, c2:c2 + 1],
                    in1=tA[c2 % 2],
                    op0=ALU.mult, op1=ALU.add), [wt(f"{tag}_A_{cch}")], "v")
                note(f"{tag}_B_{cch}", "v")
                E("v", lambda e, c2=cch: e.scalar_tensor_tensor(
                    out=hsl(c2)[:, 0:256], in0=hsl(c2)[:, 2:258],
                    scalar=cwt[:, 2 * NC + c2:2 * NC + c2 + 1],
                    in1=tB[c2 % 2], op0=ALU.mult, op1=ALU.add), (), "v")
                note(f"{tag}_C_{cch}", "v")
            if 0 <= k - 3 < NC:
                cch = k - 3
                E("s", lambda e, c2=cch: e.activation(
                    out=hsl(c2)[:, 0:256], in_=hsl(c2)[:, 0:256], func=AF.Silu),
                  [wt(f"{tag}_C_{cch}")], "s")
                note(f"{tag}_si_{cch}", "s")
        E("v", lambda e: e.tensor_reduce(
            out=Mt[:, 0:FW],
            in_=hbuf[:, :].rearrange("p (c w) -> p w c", w=MMW)[:, 0:FW, :],
            axis=AX.X, op=ALU.max, apply_absolute_value=True),
          [wt(f"{tag}_si_{NC - 1}")], "v")
        note(f"{tag}_Mt", "v")
        for g in range(2):
            E("t", lambda e, g2=g: e.transpose(
                puf[:, UF0:UF0 + 128], Mt[:, g2 * 128:(g2 + 1) * 128], idf[:, :]),
              [wt(f"{tag}_Mt")] + pu_wait() +
              ([wt(f"{tag}_mxr_{g - 1}")] if g else []), "t")
            note(f"{tag}_mxt_{g}", "t")
            E("v", lambda e, g2=g: e.reduce_max(
                out=cols2[:, cb2 + g2 * 4:cb2 + g2 * 4 + 1],
                in_=puf[:, UF0:UF0 + 128],
                axis=AX.X), [wt(f"{tag}_mxt_{g}")], "v")
            note(f"{tag}_mxr_{g}", "v")
        pu_set(f"{tag}_mxr_1")
        c2s = lambda k: cols2[:, cb2 + k:cb2 + 8:4]
        vch(lambda e: e.tensor_scalar(out=c2s(0), in0=c2s(0), scalar1=1e-5,
                                      scalar2=None, op0=ALU.max),
            [wt(f"{tag}_mxr_1")])
        vch(lambda e: e.tensor_tensor(out=c2s(2), in0=c2s(0), in1=swc[:, 2:3]
                                      .broadcast_to([128, 2]), op=ALU.mult))
        vch(lambda e: e.reciprocal(out=c2s(1), in0=c2s(0)) if False else
            e.reciprocal(out=c2s(1), in_=c2s(0)))
        vch(lambda e: e.tensor_scalar(out=c2s(1), in0=c2s(1), scalar1=127.0,
                                      scalar2=None, op0=ALU.mult))
        note(f"{tag}_s2", "v")
        emit_bcast(f"{tag}_s2", [
            (cols2, 0, 128, cb2 + 1, 0),
            (cols2, 0, 128, cb2 + 5, 128),
        ], FW, extradep=[wt(f"{tag}_s2"), wt(f"{tag}_hs_{NC - 1}")])
        if do_round:
            ph_round(st, i)

    def ph_round(st, i):
        tag = f"{st}{i}"
        xqm = xqma if st == "a" else xqmb
        xqm_dep = [wt(prev_ma[st])] if prev_ma[st] else []
        h3 = hbuf[:, :].rearrange("p (c w) -> p c w", w=MMW)
        for cch in range(NC):
            E("g", lambda e, c2=cch: e.tensor_tensor(
                out=hsl(c2)[:, 0:256], in0=hsl(c2)[:, 0:256],
                in1=bcast[:, 0:FW], op=ALU.mult),
              [wt(f"{tag}_s2_bc")] +
              ([wt(f"{tag}_Mt")] if cch == 0 else []), "g")
            note(f"{tag}_sc_{cch}", "g")
        for j in range(4):
            E("v", lambda e, j2=j, xq=xqm: e.tensor_scalar(
                out=xq[:, j2 * 8 * FW:(j2 + 1) * 8 * FW]
                .rearrange("p (c w) -> p c w", w=FW),
                in0=h3[:, j2 * 8:(j2 + 1) * 8, 0:256],
                scalar1=MAGIC, scalar2=MAGIC, op0=ALU.add, op1=ALU.subtract),
              [wt(f"{tag}_sc_{j * 8 + 7}")] + xqm_dep, "v")
            note(f"{tag}_rn_{j}", "v")
            for c in range(j * 8, j * 8 + 8):
                prev_round[c] = f"{tag}_rn_{j}"
        note(f"{tag}_round", "v")

    def ph_mm2(st, i):
        tag = f"{st}{i}"
        cb2 = (i % 2) * 8 if st == "a" else 16
        xqm = xqma if st == "a" else xqmb
        for g in range(2):
            wl = [wt(f"{tag}_round")]
            if pm2_read[g]:
                wl.append(wt(pm2_read[g]))
            for jc in range(NC):
                for nh in range(2):
                    E("t", lambda e, g2=g, j=jc, n_=nh, xq=xqm: e.matmul(
                        pm2[g2][:, n_ * 512:(n_ + 1) * 512],
                        xq[:, j * FW + g2 * 128:j * FW + g2 * 128 + 128],
                        w2q[:, j * D + n_ * 512:j * D + (n_ + 1) * 512],
                        start=(j == 0), stop=(j == NC - 1)),
                      wl if (jc == 0 and nh == 0) else (),
                      "t" if (jc == NC - 1 and nh == 1) else None)
            note(f"{tag}_mm2_{g}", "t")
        prev_ma[st] = f"{tag}_mm2_1"
        if st == "b" and i + 1 < NT:
            dma(poolf[32:33, :], xB[i % 3][127:128, :],
                [wt(p32_user[0]), wt(f"a{i}_res")], "dg", eng="g",
                note_name=f"b{i + 1}_efill")
        for g in range(2):
            xg = xA[i % 3] if g == 0 else xB[i % 3]
            rwl = [wt(f"{tag}_mm2_{g}"), wt(f"{tag}_s2")]
            if st == "b" and g == 1 and i + 1 < NT:
                rwl.append(wt(f"b{i + 1}_efill"))
            E("v", lambda e, g2=g, x=xg: e.scalar_tensor_tensor(
                out=x[:, :], in0=pm2[g2][:, 0:D],
                scalar=cols2[:, cb2 + g2 * 4 + 2:cb2 + g2 * 4 + 3], in1=x[:, :],
                op0=ALU.mult, op1=ALU.add),
              rwl, "v")
            if st == "a":
                E("v", lambda e, x=xg: e.tensor_tensor(
                    out=x[:, :], in0=x[:, :], in1=obt[:, :], op=ALU.add), (), "v")
            note(f"{tag}_res_{g}", "v")
            pm2_read[g] = f"{tag}_res_{g}"
        note(f"{tag}_res", "v")
        if st == "b":
            dma(out_d[256 * i:256 * i + 128, :], xA[i % 3][:, :],
                [wt(f"{tag}_res_0")], "do", note_name=f"out{i}_a")
            dma(out_d[256 * i + 128:256 * i + 256, :], xB[i % 3][:, :],
                [wt(f"{tag}_res_1")], "do", note_name=f"out{i}_b")
            gnote(f"xfree{i}", [f"out{i}_a", f"out{i}_b"])

    def ph_mini(upto=99):
        tag = "mi"
        sets = [
            (minx[0:6, :], 0, 6, CB["A"], minq[0:6, :], scrB[0:6, :], None),
            (minx[32:38, :], 32, 6, CB["B"], minq[32:38, :], scrB[32:38, :], None),
        ]
        emit_qin(tag, sets, colsh, [wt("xsmall")])
        for dc in range(NDC):
            tdep = pu_wait() if dc == 0 else [wt(f"{tag}_tc_{dc - 1}")]
            E("t", lambda e, d=dc: e.transpose(
                pu[:, 0:6], minq[0:6, d * 128:(d + 1) * 128], idb[0:6, 0:6]),
              tdep + [wt(f"{tag}_done")])
            E("t", lambda e, d=dc: e.transpose(
                pu[:, 6:12], minq[32:38, d * 128:(d + 1) * 128],
                idb[32:38, 32:38]), (), "t")
            note(f"{tag}_tp_{dc}", "t")
            E("s", lambda e, d=dc: e.copy(
                out=xqmTm[:, d * 12:(d + 1) * 12], in_=pu[:, 0:12]),
              [wt(f"{tag}_tp_{dc}")], "s")
            note(f"{tag}_tc_{dc}", "s")
        pu_set(f"{tag}_tc_{NDC - 1}")
        for cch in range(NC):
            wl = [wt(f"{tag}_tc_{NDC - 1}")] + (pu_wait() if cch == 0 else
                                                [wt(f"{tag}_ev_{cch - 1}")])
            for dc in range(NDC):
                E("t", lambda e, c2=cch, d=dc: e.matmul(
                    puf[:, UF0:UF0 + 12],
                    w1q[:, d * HDIM + c2 * 128:d * HDIM + (c2 + 1) * 128],
                    xqmTm[:, d * 12:(d + 1) * 12],
                    start=(d == 0), stop=(d == NDC - 1)),
                  wl if dc == 0 else (), "t" if dc == NDC - 1 else None)
            note(f"{tag}_mm1_{cch}", "t")
            E("s", lambda e, c2=cch: e.copy(
                out=hm2[:, c2 * 12:(c2 + 1) * 12], in_=puf[:, UF0:UF0 + 12]),
              [wt(f"{tag}_mm1_{cch}")], "s")
            note(f"{tag}_ev_{cch}", "s")
        pu_set(f"{tag}_ev_{NC - 1}")
        emit_bcast(f"{tag}_al", [
            (colsh, 0, 6, CB["A"] + 9, 0),
            (colsh, 32, 6, CB["B"] + 9, 6),
        ], 12, extradep=[wt(f"{tag}_cols")], dst0=258)
        hm3 = hm2[:, :].rearrange("p (c w) -> p c w", w=12)
        E("g", lambda e: e.tensor_tensor(
            out=hm3[:, :, :], in0=hm3[:, :, :],
            in1=bcast[:, 0:12].unsqueeze(1).broadcast_to([128, NC, 12])
            if False else bcast[:, 0:12].broadcast_to([128, 12]),
            op=ALU.mult), ()) if False else None
        for cch in range(NC):
            E("g", lambda e, c2=cch: e.tensor_tensor(
                out=hm2[:, c2 * 12:(c2 + 1) * 12], in0=hm2[:, c2 * 12:(c2 + 1) * 12],
                in1=bcast[:, 258:270], op=ALU.mult),
              [wt(f"{tag}_al_bc"), wt(f"{tag}_ev_{cch}")], "g")
        note(f"{tag}_hs", "g")
        h3 = hm2[:, :].rearrange("p (c w) -> p c w", w=12)
        E("s", lambda e: e.activation(
            out=scrB[:, :].rearrange("p (c w) -> p c w", w=16)[:, 0:NC, 0:2],
            in_=h3[:, :, 3:9:5] if False else h3[:, :, 3:10:6],
            func=AF.Identity, scale=cwt[:, NC:2 * NC].transpose([0, 1])
            if False else 1.0, bias=0.0),
          [wt(f"{tag}_hs")], "s") if False else None
        for cch in range(NC):
            E("s", lambda e, c2=cch: e.activation(
                out=tmpB[:, 0:2], in_=hm2[:, c2 * 12 + 3:c2 * 12 + 10:6],
                func=AF.Identity, scale=cwt[:, NC + c2:NC + c2 + 1],
                bias=cwt[:, 3 * NC + c2:3 * NC + c2 + 1]),
              [wt(f"{tag}_hs")]
              + ([wt(f"{tag}_cC_{cch - 1}")] if cch else []), "s")
            note(f"{tag}_cA_{cch}", "s")
            E("v", lambda e, c2=cch: e.scalar_tensor_tensor(
                out=tmpB[:, 2:4], in0=hm2[:, c2 * 12 + 2:c2 * 12 + 9:6],
                scalar=cwt[:, c2:c2 + 1], in1=tmpB[:, 0:2],
                op0=ALU.mult, op1=ALU.add), [wt(f"{tag}_cA_{cch}")], "v")
            E("v", lambda e, c2=cch: e.scalar_tensor_tensor(
                out=hm2[:, c2 * 12 + 3:c2 * 12 + 10:6],
                in0=hm2[:, c2 * 12 + 4:c2 * 12 + 11:6],
                scalar=cwt[:, 2 * NC + c2:2 * NC + c2 + 1], in1=tmpB[:, 2:4],
                op0=ALU.mult, op1=ALU.add), (), "v")
            note(f"{tag}_cC_{cch}", "v")
            E("s", lambda e, c2=cch: e.activation(
                out=hm2[:, c2 * 12 + 3:c2 * 12 + 10:6],
                in_=hm2[:, c2 * 12 + 3:c2 * 12 + 10:6], func=AF.Silu),
              [wt(f"{tag}_cC_{cch}")], "s")
            note(f"{tag}_cs_{cch}", "s")
            E("s", lambda e, c2=cch: e.activation(
                out=absT[:, 0:2],
                in_=hm2[:, c2 * 12 + 3:c2 * 12 + 10:6], func=AF.Abs),
              [wt(f"{tag}_cs_{cch}")] +
              ([wt(f"{tag}_Mm_{cch - 1}")] if cch >= 1 else []), "s")
            note(f"{tag}_cb_{cch}", "s")
            if cch == 0:
                E("v", lambda e, c2=cch: e.tensor_copy(
                    out=Mh[:, 0:2], in_=absT[:, 0:2]),
                  [wt(f"{tag}_cb_{cch}")], "v")
            else:
                E("v", lambda e, c2=cch: e.tensor_tensor(
                    out=Mh[:, 0:2], in0=Mh[:, 0:2],
                    in1=absT[:, 0:2], op=ALU.max),
                  [wt(f"{tag}_cb_{cch}")], "v")
            note(f"{tag}_Mm_{cch}", "v")
        E("t", lambda e: e.transpose(puf[0:2, UF0:UF0 + 128], Mh[:, 0:2]
                                     .transpose([0, 1]) if False else Mh[:, 0:2],
                                     idf[:, :]),
          [wt(f"{tag}_Mm_{NC - 1}")] + pu_wait(), "t")
        note(f"{tag}_mxt", "t")
        E("v", lambda e: e.reduce_max(out=cols2[0:2, 18:19],
                                      in_=puf[0:2, UF0:UF0 + 128], axis=AX.X),
          [wt(f"{tag}_mxt")], "v")
        note(f"{tag}_mxr", "v")
        pu_set(f"{tag}_mxr")
        vch(lambda e: e.tensor_scalar(out=cols2[0:2, 18:19], in0=cols2[0:2, 18:19],
                                      scalar1=1e-5, scalar2=None, op0=ALU.max),
            [wt(f"{tag}_mxr")])
        vch(lambda e: e.tensor_tensor(out=cols2[0:2, 19:20], in0=cols2[0:2, 18:19],
                                      in1=swc[0:2, 2:3], op=ALU.mult))
        vch(lambda e: e.reciprocal(out=cols2[0:2, 17:18], in_=cols2[0:2, 18:19]))
        vch(lambda e: e.tensor_scalar(out=cols2[0:2, 17:18], in0=cols2[0:2, 17:18],
                                      scalar1=127.0, scalar2=None, op0=ALU.mult))
        note(f"{tag}_s2c", "v")
        emit_bcast(f"{tag}_s2", [(cols2, 0, 2, 17, 0)], 2,
                   extradep=[wt(f"{tag}_s2c")], dst0=270)
        for cch in range(NC):
            E("g", lambda e, c2=cch: e.tensor_tensor(
                out=hm2[:, c2 * 12 + 3:c2 * 12 + 10:6],
                in0=hm2[:, c2 * 12 + 3:c2 * 12 + 10:6],
                in1=bcast[:, 270:272], op=ALU.mult),
              [wt(f"{tag}_s2_bc"), wt(f"{tag}_Mm_{cch}")], "g")
            note(f"{tag}_ms_{cch}", "g")
            E("v", lambda e, c2=cch: e.tensor_scalar(
                out=xqm_m[:, c2 * 2:(c2 + 1) * 2],
                in0=hm2[:, c2 * 12 + 3:c2 * 12 + 10:6],
                scalar1=MAGIC, scalar2=MAGIC, op0=ALU.add, op1=ALU.subtract),
              [wt(f"{tag}_ms_{cch}")], "v")
            note(f"{tag}_rn_{cch}", "v")
        note(f"{tag}_round", "v")
        for dmc in range(NDC):
            wl = [wt(f"{tag}_round")] + (pu_wait() if dmc == 0 else
                                         [wt(f"{tag}_fc_{dmc - 1}"
                                             if upto >= 63 else
                                             f"{tag}_fv_{dmc - 1}")])
            for jc in range(NC):
                E("t", lambda e, d=dmc, j=jc: e.matmul(
                    puf[:, UF0:UF0 + 2],
                    w2q[:, j * D + d * 128:j * D + (d + 1) * 128],
                    xqm_m[:, j * 2:(j + 1) * 2],
                    start=(j == 0), stop=(j == NC - 1)),
                  wl if jc == 0 else (), "t" if jc == NC - 1 else None)
            note(f"{tag}_mm2_{dmc}", "t")
            E("t", lambda e: e.transpose(puf[0:2, UF0 + 4:UF0 + 132],
                                         puf[:, UF0:UF0 + 2], idf[:, :]),
              (), "t") if False else None
            if upto < 62:
                E("v", lambda e, d=dmc: e.tensor_copy(
                    out=tmpB[:, 0:2], in_=puf[:, UF0:UF0 + 2]),
                  [wt(f"{tag}_mm2_{dmc}")], "v")
                note(f"{tag}_fv_{dmc}", "v")
                continue
            E("v", lambda e, d=dmc: e.tensor_copy(
                out=tmpB[:, 0:2], in_=puf[:, UF0:UF0 + 2]),
              [wt(f"{tag}_mm2_{dmc}")], "v")
            note(f"{tag}_fv_{dmc}", "v")
            E("t", lambda e, d=dmc: e.transpose(
                puf[0:2, UF0 + 4:UF0 + 132], tmpB[:, 0:2], idf[:, :]),
              [wt(f"{tag}_fv_{dmc}")]
              + ([wt(f"{tag}_fc_{dmc - 1}")] if dmc > 0 else []), "t")
            note(f"{tag}_ft_{dmc}", "t")
            if upto < 63:
                note(f"{tag}_fc_{dmc}", "t")
                continue
            E("s", lambda e, d=dmc: e.copy(
                out=fmini[0:2, d * 128:(d + 1) * 128],
                in_=puf[0:2, UF0 + 4:UF0 + 132]),
              [wt(f"{tag}_ft_{dmc}")], "s")
            note(f"{tag}_fc_{dmc}", "s")
        if upto < 99:
            return
        pu_set(f"{tag}_fc_{NDC - 1}")
        note(f"{tag}_f", "s")
        dma(scrB[0:1, :], minx[3:4, :], [wt(f"{tag}_done")], "dg", eng="g",
            note_name=f"{tag}_xsrc0")
        dma(scrB[1:2, :], minx[36:37, :], [wt(f"{tag}_done")], "dg", eng="g",
            note_name=f"{tag}_xsrc")
        E("v", lambda e: e.scalar_tensor_tensor(
            out=scrB[0:2, :], in0=fmini[0:2, :], scalar=cols2[0:2, 19:20],
            in1=scrB[0:2, :], op0=ALU.mult, op1=ALU.add),
          [wt(f"{tag}_f"), wt(f"{tag}_xsrc"), wt(f"{tag}_xsrc0"),
           wt(f"{tag}_s2c")], "v")
        E("v", lambda e: e.tensor_tensor(
            out=scrB[0:2, :], in0=scrB[0:2, :], in1=obt[0:2, :], op=ALU.add),
          (), "v")
        dma(xh[62:64, :], scrB[0:2, :], [("v", R.n("v"))], "dg", eng="g",
            note_name="mini_done")

    def hqin(tag, src, p0, n_, dst, pstg, deps):
        emit_qin(tag, [(src, p0, n_, CB["A"], dst, pstg, None)], colsh, deps)

    qzb = xq1T[:, 0:640]

    def ph_attn_gen():
        tag = "at"
        hqin("ah", xh[0:40, :], 0, 40, xqh[0:40, :], scrB[0:40, :],
             [wt("xsmall"), wt("mini_done")])
        for dc in range(NDC):
            tdep = (pu_wait() if dc == 0 else [wt(f"ah_tc_{dc - 1}")])
            E("t", lambda e, d=dc: e.transpose(
                pu[:, 0:40], xqh[0:40, d * 128:(d + 1) * 128], idb[0:40, 0:40]),
              tdep + [wt("ah_done")], "t")
            note(f"ah_tp_{dc}", "t")
            E("s", lambda e, d=dc: e.copy(
                out=xqhT[:, d * 40:(d + 1) * 40], in_=pu[:, 0:40]),
              [wt(f"ah_tp_{dc}")], "s")
            note(f"ah_tc_{dc}", "s")
        pu_set(f"ah_tc_{NDC - 1}")
        yield
        emit_bcast("ah_al", [(colsh, 0, 40, CB["A"] + 9, 0)], 40,
                   extradep=[wt("ah_cols")], dst0=272)
        for grp in range(NC // 4):
            wl = [wt(f"ah_tc_{NDC - 1}")] + (pu_wait() if grp == 0 else
                                             [wt(f"ah_evg_{grp - 1}")])
            for ci in range(4):
                cch = grp * 4 + ci
                for dc in range(NDC):
                    E("t", lambda e, c2=cch, ci2=ci, d=dc: e.matmul(
                        puf[:, UF0 + ci2 * 40:UF0 + ci2 * 40 + 40],
                        w1q[:, d * HDIM + c2 * 128:d * HDIM + (c2 + 1) * 128],
                        xqhT[:, d * 40:(d + 1) * 40],
                        start=(d == 0), stop=(d == NDC - 1)),
                      wl if (ci == 0 and dc == 0) else (),
                      "t" if (ci == 3 and dc == NDC - 1) else None)
            note(f"ah_mm1g_{grp}", "t")
            E("v", lambda e, g2=grp: e.tensor_copy(
                out=hh[:, g2 * 160:(g2 + 1) * 160],
                in_=puf[:, UF0:UF0 + 160]),
              [wt(f"ah_mm1g_{grp}")], "v")
            note(f"ah_evg_{grp}", "v")
        for cch in range(NC):
            ev[f"ah_ev_{cch}"] = ev[f"ah_evg_{cch // 4}"]
            E("g", lambda e, c2=cch: e.tensor_tensor(
                out=hh[:, c2 * 40:(c2 + 1) * 40], in0=hh[:, c2 * 40:(c2 + 1) * 40],
                in1=bcast[:, 272:312], op=ALU.mult),
              [wt("ah_al_bc"), wt(f"ah_ev_{cch}")], "g")
            note(f"ah_hs_{cch}", "g")
            E("s", lambda e, c2=cch: e.activation(
                out=absh[:, 1:39], in_=hh[:, c2 * 40 + 1:c2 * 40 + 39],
                func=AF.Identity, scale=cwt[:, NC + c2:NC + c2 + 1],
                bias=cwt[:, 3 * NC + c2:3 * NC + c2 + 1]),
              [wt(f"ah_hs_{cch}")] + ([wt(f"ah_Mm_{cch - 1}")] if cch else []),
              "s")
            note(f"ah_cA_{cch}", "s")
            E("v", lambda e, c2=cch: e.scalar_tensor_tensor(
                out=absh[:, 1:39], in0=hh[:, c2 * 40 + 0:c2 * 40 + 38],
                scalar=cwt[:, c2:c2 + 1], in1=absh[:, 1:39],
                op0=ALU.mult, op1=ALU.add), [wt(f"ah_cA_{cch}")], "v")
            note(f"ah_cB_{cch}", "v")
            E("v", lambda e, c2=cch: e.scalar_tensor_tensor(
                out=hh[:, c2 * 40 + 1:c2 * 40 + 39],
                in0=hh[:, c2 * 40 + 2:c2 * 40 + 40],
                scalar=cwt[:, 2 * NC + c2:2 * NC + c2 + 1], in1=absh[:, 1:39],
                op0=ALU.mult, op1=ALU.add), (), "v")
            note(f"ah_cC_{cch}", "v")
            E("s", lambda e, c2=cch: e.activation(
                out=hh[:, c2 * 40 + 1:c2 * 40 + 39],
                in_=hh[:, c2 * 40 + 1:c2 * 40 + 39], func=AF.Silu),
              [wt(f"ah_cC_{cch}")], "s")
            note(f"ah_si_{cch}", "s")
            E("s", lambda e, c2=cch: e.activation(
                out=absh[:, 1:39], in_=hh[:, c2 * 40 + 1:c2 * 40 + 39],
                func=AF.Abs), [wt(f"ah_si_{cch}")] +
              ([wt(f"ah_Mm_{cch - 1}")] if cch else []), "s")
            note(f"ah_ab_{cch}", "s")
            if cch == 0:
                E("v", lambda e: e.tensor_copy(out=Mh[:, 1:39], in_=absh[:, 1:39]),
                  [wt(f"ah_ab_{cch}")], "v")
            else:
                E("v", lambda e: e.tensor_tensor(
                    out=Mh[:, 1:39], in0=Mh[:, 1:39], in1=absh[:, 1:39],
                    op=ALU.max), [wt(f"ah_ab_{cch}")], "v")
            note(f"ah_Mm_{cch}", "v")
        yield
        E("t", lambda e: e.transpose(puf[0:40, UF0:UF0 + 128], Mh[:, 0:40],
                                     idf[:, :]),
          [wt(f"ah_Mm_{NC - 1}")] + pu_wait(), "t")
        note("ah_mxt", "t")
        E("v", lambda e: e.reduce_max(out=colsh[0:40, 6:7],
                                      in_=puf[0:40, UF0:UF0 + 128], axis=AX.X),
          [wt("ah_mxt")], "v")
        note("ah_mxr", "v")
        pu_set("ah_mxr")
        vch(lambda e: e.tensor_scalar(out=colsh[0:40, 6:7], in0=colsh[0:40, 6:7],
                                      scalar1=1e-5, scalar2=None, op0=ALU.max),
            [wt("ah_mxr")])
        vch(lambda e: e.tensor_tensor(out=colsh[0:40, 10:11],
                                      in0=colsh[0:40, 6:7], in1=swc[0:40, 2:3],
                                      op=ALU.mult))
        vch(lambda e: e.reciprocal(out=colsh[0:40, 11:12], in_=colsh[0:40, 6:7]))
        vch(lambda e: e.tensor_scalar(out=colsh[0:40, 11:12],
                                      in0=colsh[0:40, 11:12], scalar1=127.0,
                                      scalar2=None, op0=ALU.mult))
        note("ah_s2c", "v")
        emit_bcast("ah_s2", [(colsh, 0, 40, 11, 0)], 40,
                   extradep=[wt("ah_s2c")], dst0=312)
        for cch in range(NC):
            E("g", lambda e, c2=cch: e.tensor_tensor(
                out=hh[:, c2 * 40 + 1:c2 * 40 + 39],
                in0=hh[:, c2 * 40 + 1:c2 * 40 + 39], in1=bcast[:, 313:351],
                op=ALU.mult), [wt("ah_s2_bc"), wt(f"ah_Mm_{cch}")], "g")
            note(f"ah_ms_{cch}", "g")
            E("v", lambda e, c2=cch: e.tensor_scalar(
                out=xqmh[:, c2 * 40:(c2 + 1) * 40], in0=hh[:, c2 * 40:(c2 + 1) * 40],
                scalar1=MAGIC, scalar2=MAGIC, op0=ALU.add, op1=ALU.subtract),
              [wt(f"ah_ms_{cch}")], "v")
            note(f"ah_rn_{cch}", "v")
        note("ah_round", "v")
        yield
        for nh in range(2):
            wl = [wt("ah_round")] + pu_wait()
            for jc in range(NC):
                E("t", lambda e, j=jc, n_=nh: e.matmul(
                    puf[0:40, 0:512],
                    xqmh[:, j * 40:(j + 1) * 40],
                    w2q[:, j * D + n_ * 512:j * D + (n_ + 1) * 512],
                    start=(j == 0), stop=(j == NC - 1)),
                  wl if jc == 0 else (), "t" if jc == NC - 1 else None)
            note(f"ah_mm2_{nh}", "t")
            E("s", lambda e, n_=nh: e.activation(
                out=scrA[0:40, n_ * 512:(n_ + 1) * 512], in_=puf[0:40, 0:512],
                func=AF.Identity, scale=colsh[0:40, 10:11]),
              [wt(f"ah_mm2_{nh}")], "s")
            note(f"ah_f_{nh}", "s")
            pu_set(f"ah_f_{nh}")
        E("v", lambda e: e.tensor_tensor(out=xh[0:40, :], in0=xh[0:40, :],
                                         in1=scrA[0:40, :], op=ALU.add),
          [wt("ah_f_0"), wt("ah_f_1")], "v")
        note("ah_x1h", "v")
        yield
        E("v", lambda e: e.tensor_reduce(out=colsh[0:40, 0:1], in_=xh[0:40, :],
                                         axis=AX.X, op=ALU.add),
          [wt("ah_x1h")], "v")
        note("zr", "v")
        E("s", lambda e: e.activation(out=scrB[0:40, :], in_=xh[0:40, :],
                                      func=AF.Square, accum_out=colsh[0:40, 1:2]),
          [wt("ah_x1h")], "s")
        note("zs", "s")
        zc = lambda k: colsh[0:40, k:k + 1]
        vch(lambda e: e.tensor_scalar(out=zc(0), in0=zc(0), scalar1=1.0 / D,
                                      scalar2=None, op0=ALU.mult),
            [wt("zr"), wt("zs")])
        vch(lambda e: e.tensor_scalar(out=zc(1), in0=zc(1), scalar1=1.0 / D,
                                      scalar2=None, op0=ALU.mult))
        vch(lambda e: e.tensor_tensor(out=zc(3), in0=zc(0), in1=zc(0),
                                      op=ALU.mult))
        vch(lambda e: e.tensor_tensor(out=zc(2), in0=zc(1), in1=zc(3),
                                      op=ALU.subtract))
        vch(lambda e: e.tensor_scalar(out=zc(2), in0=zc(2), scalar1=EPS,
                                      scalar2=None, op0=ALU.add))
        note("zpre", "v")
        E("s", lambda e: e.activation(out=zc(5), in_=zc(2), func=AF.Sqrt),
          [wt("zpre")], "s")
        note("zsqrt", "s")
        vch(lambda e: e.reciprocal(out=zc(11), in_=zc(5)), [wt("zsqrt")])
        for _ in range(2):
            vch(lambda e: e.tensor_tensor(out=zc(5), in0=zc(2), in1=zc(11),
                                          op=ALU.mult))
            vch(lambda e: e.tensor_tensor(out=zc(5), in0=zc(5), in1=zc(11),
                                          op=ALU.mult))
            vch(lambda e: e.tensor_scalar(out=zc(5), in0=zc(5), scalar1=-0.5,
                                          scalar2=1.5, op0=ALU.mult, op1=ALU.add))
            vch(lambda e: e.tensor_tensor(out=zc(11), in0=zc(11), in1=zc(5),
                                          op=ALU.mult))
        vch(lambda e: e.tensor_tensor(out=zc(4), in0=zc(0), in1=zc(11),
                                      op=ALU.mult))
        vch(lambda e: e.tensor_scalar(out=zc(4), in0=zc(4), scalar1=-1.0,
                                      scalar2=None, op0=ALU.mult))
        note("zcols", "v")
        E("s", lambda e: e.activation(out=scrA[0:40, :], in_=xh[0:40, :],
                                      func=AF.Identity, scale=zc(11),
                                      bias=zc(4)), [wt("zcols")], "s")
        note("z16", "s")
        zaT = fmb[:, 0:320]
        zbT = fmb[:, 320:640]
        for dc in range(NDC):
            E("t", lambda e, d=dc: e.transpose(
                puf[:, UF0:UF0 + 40], scrA[0:40, d * 128:(d + 1) * 128],
                idf[0:40, 0:40]),
              [wt("z16")] + (pu_wait() if dc == 0 else [wt(f"zb_{dc - 1}")]),
              "t")
            note(f"zt_{dc}", "t")
            E("s", lambda e, d=dc: e.activation(
                out=zaT[:, d * 40:(d + 1) * 40], in_=puf[:, UF0:UF0 + 40],
                func=AF.Identity, scale=lab[:, d:d + 1]), [wt(f"zt_{dc}")], "s")
            note(f"za_{dc}", "s")
            E("s", lambda e, d=dc: e.activation(
                out=zbT[:, d * 40:(d + 1) * 40], in_=puf[:, UF0:UF0 + 40],
                func=AF.Identity, scale=lab[:, 8 + d:9 + d]), (), "s")
            note(f"zb_{dc}", "s")
        pu_set(f"zb_{NDC - 1}")
        note("zT", "s")
        yield
        def pj_dst(i2):
            if i2 < 3:
                return pm1[i2][:, 0:40]
            if i2 == 3:
                return puf[:, UF0:UF0 + 40]
            g, hb = divmod(i2 - 4, 2)
            return pm2[g][:, hb * 512:hb * 512 + 40]

        for wi in range(3):
            rhs = zaT if wi == 0 else zbT
            for dc in range(NDC):
                wl = [wt("awscr_done"), wt("zT")]
                if wi or dc:
                    wl.append(wt(f"pj_ev_{wi - 1}_{NDC - 1}") if (dc == 0 and wi)
                              else wt(f"pj_mm_{wi}_{dc - 2}") if dc >= 2 else
                              wt("zT"))
                dma(wstg[:, (dc % 2) * D:(dc % 2) * D + D],
                    awscr[wi * D:(wi + 1) * D, dc * 128:(dc + 1) * 128],
                    wl, "da", tp=True, note_name=f"pj_sl_{wi}_{dc}")
                for ic in range(8):
                    E("t", lambda e, w_=wi, d=dc, i2=ic, r=rhs: e.matmul(
                        pj_dst(i2),
                        wstg[:, (d % 2) * D + i2 * 128:(d % 2) * D + (i2 + 1) * 128],
                        r[:, d * 40:(d + 1) * 40],
                        start=(d == 0), stop=(d == NDC - 1)),
                      ([wt(f"pj_sl_{wi}_{dc}")] +
                       ((pu_wait() + [wt(pm2_read[0]), wt(pm2_read[1])])
                        if (wi == 0 and dc == 0 and ic == 0) else []) +
                       ([wt(f"pj_ev_{wi - 1}_7")] if (dc == 0 and ic == 0 and wi)
                        else [])) if ic == 0 else (),
                      "t" if ic == 7 else None)
                note(f"pj_mm_{wi}_{dc}", "t")
            for ic in range(8):
                if wi == 0:
                    E("s", lambda e, w_=wi, i2=ic: e.activation(
                        out=fmp[:, w_ * 320 + i2 * 40:w_ * 320 + (i2 + 1) * 40],
                        in_=pj_dst(i2),
                        func=AF.Identity, bias=qbv[:, i2:i2 + 1]),
                      [wt(f"pj_mm_{wi}_{NDC - 1}")], "s")
                elif wi == 2:
                    E("s", lambda e, w_=wi, i2=ic: e.activation(
                        out=fmp[:, w_ * 320 + i2 * 40:w_ * 320 + (i2 + 1) * 40],
                        in_=pj_dst(i2),
                        func=AF.Identity, bias=qbv[:, 8 + i2:9 + i2]),
                      [wt(f"pj_mm_{wi}_{NDC - 1}")], "s")
                else:
                    E("s", lambda e, w_=wi, i2=ic: e.copy(
                        out=fmp[:, w_ * 320 + i2 * 40:w_ * 320 + (i2 + 1) * 40],
                        in_=pj_dst(i2)),
                      [wt(f"pj_mm_{wi}_{NDC - 1}")], "s")
                note(f"pj_ev_{wi}_{ic}", "s")
            note(f"pj_{wi}", "s")
        pu_set("pj_ev_2_7")
        pm2_read[0] = "pj_2"
        pm2_read[1] = "pj_2"
        yield
        E("v", lambda e: e.memset(bdg[:, :], 0.0), (), "v")
        note("bdg0", "v")
        for wi in range(3):
            dma(scrB[0:64, 0:64], lw_d[wi][:, :],
                [wt("bdg0")] + ([wt(f"lwt_{wi - 1}")] if wi else []),
                "da", note_name=f"lw_{wi}")
            E("t", lambda e, w_=wi: e.transpose(
                puf[0:64, UF0:UF0 + 64], scrB[0:64, 0:64], idf[0:64, 0:64]),
              [wt(f"lw_{wi}")] + pu_wait()
              + ([wt(f"lwc1_{wi - 1}")] if wi else []), "t")
            note(f"lwt_{wi}", "t")
            E("s", lambda e, w_=wi: e.copy(
                out=bdg[0:64, w_ * 128:w_ * 128 + 64],
                in_=puf[0:64, UF0:UF0 + 64]), [wt(f"lwt_{wi}"), wt("bdg0")], "s")
            note(f"lwc1_{wi}", "s")
            dma(bdg[64:128, wi * 128 + 64:wi * 128 + 128],
                bdg[0:64, wi * 128:wi * 128 + 64], [wt(f"lwc1_{wi}")], "dg",
                eng="g", note_name=f"lwc_{wi}")
        pu_set("lwc_2")
        gnote("bdg_done", ["lwc_0", "lwc_1", "lwc_2"])
        def transform(tg, src_slot, wi, dst_slot, lncol):
            E("v", lambda e, s=src_slot: e.tensor_copy(
                out=fmb[:, 320:640], in_=fmp[:, s * 320:(s + 1) * 320]),
              [wt(f"{tg}_srcdep")], "v")
            note(f"{tg}_cast", "v")
            for ic in range(8):
                E("t", lambda e, w_=wi, i2=ic: e.matmul(
                    puf[:, UF0 + i2 * 40:UF0 + (i2 + 1) * 40],
                    bdg[:, w_ * 128:(w_ + 1) * 128],
                    fmb[:, 320 + i2 * 40:320 + (i2 + 1) * 40],
                    start=True, stop=True),
                  [wt(f"{tg}_cast"), wt("bdg_done")] +
                  (pu_wait() if ic == 0 else []), "t" if ic == 7 else None)
            note(f"{tg}_mm", "t")
            for ic in range(8):
                E("s", lambda e, w_=wi, i2=ic: e.activation(
                    out=scrB[0:40, 0:0 + 128][0:40, :] if False else
                    scrB[0:40, i2 * 128:(i2 + 1) * 128],
                    in_=puf[:, UF0 + i2 * 40:UF0 + (i2 + 1) * 40],
                    func=AF.Identity, bias=lbv[:, w_:w_ + 1]),
                  [wt(f"{tg}_mm")], "s") if False else None
            for ic in range(8):
                E("s", lambda e, w_=wi, i2=ic: e.activation(
                    out=fmp[:, 3 * 320 + i2 * 40:3 * 320 + (i2 + 1) * 40],
                    in_=puf[:, UF0 + i2 * 40:UF0 + (i2 + 1) * 40],
                    func=AF.Identity, bias=lbv[:, w_:w_ + 1]),
                  [wt(f"{tg}_mm")], "s")
            note(f"{tg}_ev", "s")
            pu_set(f"{tg}_ev")
            for dc in range(NDC):
                E("t", lambda e, d=dc: e.transpose(
                    puf[0:40, UF0:UF0 + 128],
                    fmp[:, 3 * 320 + d * 40:3 * 320 + (d + 1) * 40],
                    idf[:, :]),
                  [wt(f"{tg}_ev")] + (pu_wait() if dc == 0 else
                                      [wt(f"{tg}_tmc_{dc - 1}")]), "t")
                note(f"{tg}_tm_{dc}", "t")
                E("s", lambda e, d=dc: e.copy(
                    out=scrB[0:40, d * 128:(d + 1) * 128],
                    in_=puf[0:40, UF0:UF0 + 128]), [wt(f"{tg}_tm_{dc}")], "s")
                note(f"{tg}_tmc_{dc}", "s")
            pu_set(f"{tg}_tmc_{NDC - 1}")
            note(f"{tg}_tm", "s")
            s3 = scrB[0:40, :].rearrange("p (h w) -> p h w", w=64)
            E("v", lambda e: e.tensor_reduce(
                out=habc[0:40, 0:16], in_=s3, axis=AX.X, op=ALU.add),
              [wt(f"{tg}_tm")], "v")
            note(f"{tg}_hr", "v")
            E("s", lambda e: e.activation(
                out=at16[0:40, :], in_=scrB[0:40, :], func=AF.Square),
              [wt(f"{tg}_tm")], "s")
            note(f"{tg}_hsq", "s")
            E("v", lambda e: e.tensor_reduce(
                out=habc[0:40, 16:32],
                in_=at16[0:40, :].rearrange("p (h w) -> p h w", w=64),
                axis=AX.X, op=ALU.add), [wt(f"{tg}_hsq")], "v")
            note(f"{tg}_hr2", "v")
            hc = lambda k: habc[0:40, k * 16:(k + 1) * 16]
            vch(lambda e: e.tensor_scalar(out=hc(0), in0=hc(0), scalar1=1.0 / 64,
                                          scalar2=None, op0=ALU.mult),
                [wt(f"{tg}_hr"), wt(f"{tg}_hr2")])
            vch(lambda e: e.tensor_scalar(out=hc(1), in0=hc(1), scalar1=1.0 / 64,
                                          scalar2=None, op0=ALU.mult))
            vch(lambda e: e.tensor_tensor(out=hc(4), in0=hc(0), in1=hc(0),
                                          op=ALU.mult))
            vch(lambda e: e.tensor_tensor(out=hc(2), in0=hc(1), in1=hc(4),
                                          op=ALU.subtract))
            vch(lambda e: e.tensor_scalar(out=hc(2), in0=hc(2), scalar1=EPS,
                                          scalar2=None, op0=ALU.add))
            note(f"{tg}_hpre", "v")
            E("s", lambda e: e.activation(out=hc(4), in_=hc(2), func=AF.Sqrt),
              [wt(f"{tg}_hpre")], "s")
            note(f"{tg}_hsqrt", "s")
            vch(lambda e: e.reciprocal(out=hc(3), in_=hc(4)), [wt(f"{tg}_hsqrt")])
            for _ in range(2):
                vch(lambda e: e.tensor_tensor(out=hc(4), in0=hc(2), in1=hc(3),
                                              op=ALU.mult))
                vch(lambda e: e.tensor_tensor(out=hc(4), in0=hc(4), in1=hc(3),
                                              op=ALU.mult))
                vch(lambda e: e.tensor_scalar(out=hc(4), in0=hc(4), scalar1=-0.5,
                                              scalar2=1.5, op0=ALU.mult,
                                              op1=ALU.add))
                vch(lambda e: e.tensor_tensor(out=hc(3), in0=hc(3), in1=hc(4),
                                              op=ALU.mult))
            note(f"{tg}_hrs", "v")
            vch(lambda e: e.tensor_tensor(out=habc[0:40, 64:80],
                                          in0=hc(0), in1=hc(3), op=ALU.mult),
                [wt(f"{tg}_hrs")])
            vch(lambda e: e.tensor_scalar(out=habc[0:40, 64:80],
                                          in0=habc[0:40, 64:80], scalar1=-1.0,
                                          scalar2=None, op0=ALU.mult))
            note(f"{tg}_mrs", "v")
            for h in range(16):
                E("s", lambda e, h2=h: e.activation(
                    out=scrB[0:40, h2 * 64:(h2 + 1) * 64],
                    in_=scrB[0:40, h2 * 64:(h2 + 1) * 64], func=AF.Identity,
                    scale=habc[0:40, 48 + h2:49 + h2],
                    bias=habc[0:40, 64 + h2:65 + h2]),
                  [wt(f"{tg}_mrs")] if h == 0 else (),
                  "s" if h == 15 else None)
            note(f"{tg}_ln", "s")
            for dc in range(NDC):
                E("t", lambda e, d=dc: e.transpose(
                    puf[:, UF0:UF0 + 40], scrB[0:40, d * 128:(d + 1) * 128],
                    idf[0:40, 0:40]),
                  [wt(f"{tg}_ln")] + (pu_wait() if dc == 0 else
                                      [wt(f"{tg}_fbc_{dc - 1}")]), "t")
                note(f"{tg}_fb_{dc}", "t")
                if lncol == 1:
                    E("s", lambda e, d=dc, ds=dst_slot: e.activation(
                        out=fmb[:, ds * 320 + d * 40:ds * 320 + (d + 1) * 40],
                        in_=puf[:, UF0:UF0 + 40], func=AF.Identity,
                        scale=lbv[:, 5:6]), [wt(f"{tg}_fb_{dc}"), wt("gcol")], "s")
                else:
                    E("s", lambda e, d=dc, ds=dst_slot: e.copy(
                        out=fmb[:, ds * 320 + d * 40:ds * 320 + (d + 1) * 40],
                        in_=puf[:, UF0:UF0 + 40]), [wt(f"{tg}_fb_{dc}")], "s")
                note(f"{tg}_fbc_{dc}", "s")
            pu_set(f"{tg}_fbc_{NDC - 1}")
            note(f"{tg}_out", "s")

        yield
        note("kn_srcdep", "s")
        ev["kn_srcdep"] = ev["pj_2"]
        transform("kn", 1, 1, 0, 1)
        yield
        E("v", lambda e: e.tensor_copy(out=fmb[:, 320:640],
                                       in_=fmp[:, 2 * 320:3 * 320]),
          [wt("pj_2"), wt("kn_mm")], "v")
        note("vi_cast", "v")
        for ic in range(8):
            E("t", lambda e, i2=ic: e.matmul(
                puf[:, UF0 + i2 * 40:UF0 + (i2 + 1) * 40],
                bdg[:, 2 * 128:3 * 128],
                fmb[:, 320 + i2 * 40:320 + (i2 + 1) * 40],
                start=True, stop=True),
              [wt("vi_cast"), wt("bdg_done")] + (pu_wait() if ic == 0 else []),
              "t" if ic == 7 else None)
        note("vi_mm", "t")
        for ic in range(8):
            E("s", lambda e, i2=ic: e.activation(
                out=fmp[:, 3 * 320 + i2 * 40:3 * 320 + (i2 + 1) * 40],
                in_=puf[:, UF0 + i2 * 40:UF0 + (i2 + 1) * 40],
                func=AF.Identity, bias=lbv[:, 2:3]), [wt("vi_mm")], "s")
        note("vi_ev", "s")
        pu_set("vi_ev")
        for dc in range(NDC):
            for b in range(2):
                E("t", lambda e, d=dc, b2=b: e.transpose(
                    puf[0:16, UF0:UF0 + 128],
                    fmp[:, 3 * 320 + d * 40 + b2 * 20 + 2:
                        3 * 320 + d * 40 + b2 * 20 + 18], idf[:, :]),
                  [wt("vi_ev")] + (pu_wait() if (dc == 0 and b == 0) else
                                   [wt(f"vi_tc_{dc}_{b - 1}") if b else
                                    wt(f"vi_tc_{dc - 1}_1")]), "t")
                note(f"vi_tm_{dc}_{b}", "t")
                E("s", lambda e, d=dc, b2=b: e.copy(
                    out=vib[0:16, b2 * 1024 + d * 128:b2 * 1024 + (d + 1) * 128],
                    in_=puf[0:16, UF0:UF0 + 128]),
                  [wt(f"vi_tm_{dc}_{b}"), wt("ah_round")], "s")
                note(f"vi_tc_{dc}_{b}", "s")
        pu_set(f"vi_tc_{NDC - 1}_1")
        note("vi_tm", "s")
        yield
        E("v", lambda e: e.memset(qzb[:, :], 0.0),
          [wt(f"a2_mm1_{NC - 1}")], "v")
        note("qzb0", "v")
        E("v", lambda e: e.tensor_copy(out=fmp[:, 5 * 320:6 * 320],
                                       in_=fmp[:, 0:320]), [wt("pj_0")], "v")
        note("q0c", "v")

        def iteration(n):
            tg = f"it{n}"
            ev[f"{tg}_srcdep"] = [
                ev["q0c" if n == 0 else f"qup{n}"],
                ev["vi_mm" if n == 0 else f"it{n - 1}_lg"]]
            transform(tg, 5, 0, 1, 0)
            E("v", lambda e: e.tensor_copy(
                out=qzb[0:64, 0:320], in_=fmb[0:64, 320:640]),
              [wt(f"{tg}_out"), wt("qzb0")], "v")
            E("v", lambda e: e.tensor_copy(
                out=qzb[64:128, 320:640], in_=fmb[64:128, 320:640]), (), "v")
            note(f"{tg}_qz", "v")
            for b in range(2):
                for h in range(16):
                    bh = b * 16 + h
                    E("t", lambda e, b2=b, h2=h, bh2=bh: e.matmul(
                        puf[0:16, bh2 * 16:(bh2 + 1) * 16],
                        qzb[:, (h2 % 2) * 320 + (h2 // 2) * 40 + b2 * 20 + 2:
                            (h2 % 2) * 320 + (h2 // 2) * 40 + b2 * 20 + 18],
                        fmb[:, (h2 // 2) * 40 + b2 * 20 + 2:
                            (h2 // 2) * 40 + b2 * 20 + 18],
                        start=True, stop=True),
                      ([wt(f"{tg}_out"), wt(f"{tg}_qz"), wt("kn_out")]
                       + pu_wait())
                      if bh == 0 else (), "t" if bh == 31 else None)
            note(f"{tg}_lg", "t")
            E("v", lambda e: e.tensor_reduce(
                out=habc[0:16, 0:32],
                in_=puf[0:16, 0:512].rearrange("p (t k) -> p t k", k=16),
                axis=AX.X, op=ALU.max), [wt(f"{tg}_lg")], "v")
            note(f"{tg}_sm1", "v")
            vch(lambda e: e.tensor_scalar(
                out=habc[0:16, 0:32], in0=habc[0:16, 0:32], scalar1=-0.125,
                scalar2=None, op0=ALU.mult), [wt(f"{tg}_sm1")])
            note(f"{tg}_nmx", "v")
            for t in range(32):
                E("s", lambda e, t2=t: e.activation(
                    out=scrA[0:16, t2 * 16:(t2 + 1) * 16],
                    in_=puf[0:16, t2 * 16:(t2 + 1) * 16], func=AF.Exp,
                    scale=0.125, bias=habc[0:16, t2:t2 + 1]),
                  [wt(f"{tg}_nmx")] if t == 0 else (),
                  "s" if t == 31 else None)
            note(f"{tg}_exp", "s")
            pu_set(f"{tg}_exp")
            E("v", lambda e: e.tensor_reduce(
                out=habc[0:16, 32:64],
                in_=scrA[0:16, 0:512].rearrange("p (t k) -> p t k", k=16),
                axis=AX.X, op=ALU.add), [wt(f"{tg}_exp")], "v")
            note(f"{tg}_sm2", "v")
            vch(lambda e: e.reciprocal(out=habc[0:16, 0:32],
                                       in_=habc[0:16, 32:64]),
                [wt(f"{tg}_sm2")])
            note(f"{tg}_rcp", "v")
            for t in range(32):
                E("s", lambda e, t2=t: e.activation(
                    out=scrA[0:16, t2 * 16:(t2 + 1) * 16],
                    in_=scrA[0:16, t2 * 16:(t2 + 1) * 16], func=AF.Identity,
                    scale=habc[0:16, t2:t2 + 1]),
                  [wt(f"{tg}_rcp")] if t == 0 else (),
                  "s" if t == 31 else None)
            note(f"{tg}_P", "s")
            def pt_tgt(bh2):
                if bh2 % 2 == 0:
                    return puf[0:16, 256 + (bh2 % 16) * 16:
                               256 + (bh2 % 16) * 16 + 16]
                return pm1[0][0:16, (bh2 % 16) * 16:(bh2 % 16) * 16 + 16]
            pt_order = [b for b in range(32) if b % 2 == 0] + \
                [b for b in range(32) if b % 2 == 1]
            for idx, bh in enumerate(pt_order):
                E("t", lambda e, bh2=bh: e.transpose(
                    pt_tgt(bh2),
                    scrA[0:16, bh2 * 16:(bh2 + 1) * 16], idf[0:16, 0:16]),
                  [wt(f"{tg}_P")] if idx == 0 else
                  ([wt(f"{tg}_PTc_{bh - 16}")] if bh >= 16 else ()), "t")
                note(f"{tg}_PT_{bh}", "t")
                E("v", lambda e, bh2=bh: e.tensor_copy(
                    out=PTb[0:16, bh2 * 16:(bh2 + 1) * 16],
                    in_=pt_tgt(bh2)),
                  [wt(f"{tg}_PT_{bh}")], "v")
                note(f"{tg}_PTc_{bh}", "v")
            note(f"{tg}_PTb", "s")
            pu_set(f"{tg}_PTb")
            E("v", lambda e: e.memset(puf[:, UF0:UF0 + 320], 0.0),
              pu_wait(), "v")
            note(f"{tg}_az", "v")
            for b in range(2):
                for h in range(16):
                    E("t", lambda e, b2=b, h2=h: e.matmul(
                        puf[(h2 % 2) * 64:(h2 % 2) * 64 + 64,
                            UF0 + (h2 // 2) * 40 + b2 * 20 + 2:
                            UF0 + (h2 // 2) * 40 + b2 * 20 + 18],
                        vib[0:16, b2 * 1024 + h2 * 64:b2 * 1024 + (h2 + 1) * 64],
                        PTb[0:16, (b2 * 16 + h2) * 16:(b2 * 16 + h2 + 1) * 16],
                        start=True, stop=True,
                        tile_position=(0, 64 * (h2 % 2))),
                      [wt(f"{tg}_PTb"), wt("vi_tm"), wt(f"{tg}_az")]
                      if (b == 0 and h == 0) else (),
                      "t" if (b == 1 and h == 15) else None)
            note(f"{tg}_av", "t")
            its = {0: 1, 1: 2, 2: 4}[n]
            E("v", lambda e, s2=its: e.tensor_copy(
                out=fmp[:, s2 * 320:(s2 + 1) * 320],
                in_=puf[:, UF0:UF0 + 320]), [wt(f"{tg}_av")], "v")
            note(f"{tg}_it", "v")
            pu_set(f"{tg}_it")

        iteration(0)
        E("v", lambda e: e.tensor_tensor(
            out=fmp[:, 5 * 320:6 * 320], in0=fmp[:, 5 * 320:6 * 320],
            in1=fmp[:, 1 * 320:2 * 320], op=ALU.add), [wt("it0_it")], "v")
        note("qup1", "v")
        yield
        iteration(1)
        E("v", lambda e: e.tensor_tensor(
            out=fmp[:, 3 * 320:4 * 320], in0=fmp[:, 2 * 320:3 * 320],
            in1=fmp[:, 1 * 320:2 * 320], op=ALU.subtract), [wt("it1_it")], "v")
        note("df_sub", "v")
        d3 = fmp[:, 3 * 320:4 * 320].rearrange("p (c w) -> p c w", w=40)
        for b in range(2):
            E("v", lambda e, b2=b: e.reduce_sum(
                out=habc[0:128, 64 + b2 * 8:64 + b2 * 8 + 8]
                if False else arow[0:1, 40:48],
                in_=d3[:, :, b2 * 20 + 2:b2 * 20 + 18], axis=AX.X,
                apply_absolute_value=True), [wt("df_sub")], "v") if False else None
        E("v", lambda e: e.reduce_sum(
            out=habc[0:128, 64:72] if False else Mh[:, 0:8],
            in_=d3[:, :, 2:18], axis=AX.X, apply_absolute_value=True),
          [wt("df_sub")], "v")
        note("df_r1a", "v")
        E("v", lambda e: e.reduce_sum(
            out=Mh[:, 8:16], in_=d3[:, :, 22:38], axis=AX.X,
            apply_absolute_value=True), (), "v")
        note("df_r1", "v")
        E("v", lambda e: e.reduce_sum(out=Mh[:, 16:17], in_=Mh[:, 0:16],
                                      axis=AX.X), [wt("df_r1")], "v")
        note("df_r2", "v")
        E("t", lambda e: e.matmul(
            puf[0:1, SP0:SP0 + 1], Mh[:, 16:17],
            onescol[:, :], start=True, stop=True),
          [wt("df_r2")], "t")
        note("df_mm", "t")
        vch(lambda e: e.tensor_scalar(
            out=arow[0:1, 16:17], in0=puf[0:1, SP0:SP0 + 1],
            scalar1=1.0 / DENOM, scalar2=None, op0=ALU.mult), [wt("df_mm")])
        vch(lambda e: e.tensor_scalar(
            out=arow[0:1, 17:18], in0=arow[0:1, 16:17], scalar1=0.1,
            scalar2=0.01, op0=ALU.mult, op1=ALU.add))
        vch(lambda e: e.tensor_tensor(
            out=arow[0:1, 18:19], in0=arow[0:1, 16:17], in1=arow[0:1, 17:18],
            op=ALU.is_lt))
        vch(lambda e: e.tensor_scalar(
            out=arow[0:1, 19:20], in0=arow[0:1, 18:19], scalar1=-1.0,
            scalar2=1.0, op0=ALU.mult, op1=ALU.add))
        note("df_s", "v")
        E("t", lambda e: e.matmul(
            puf[:, SP0 + 1:SP0 + 2], onesf[0:1, :], arow[0:1, 18:19],
            start=True, stop=True), [wt("df_s")])
        E("t", lambda e: e.matmul(
            puf[:, SP0 + 2:SP0 + 3], onesf[0:1, :], arow[0:1, 19:20],
            start=True, stop=True), (), "t")
        note("df_bc", "t")
        E("v", lambda e: e.tensor_copy(out=Mh[:, 20:22],
                                       in_=puf[:, SP0 + 1:SP0 + 3]),
          [wt("df_bc")], "v")
        note("df_cols", "v")
        E("v", lambda e: e.tensor_scalar(
            out=fmp[:, 3 * 320:4 * 320], in0=fmp[:, 2 * 320:3 * 320],
            scalar1=Mh[:, 21:22], scalar2=None, op0=ALU.mult),
          [wt("df_cols")], "v")
        E("v", lambda e: e.tensor_tensor(
            out=fmp[:, 5 * 320:6 * 320], in0=fmp[:, 5 * 320:6 * 320],
            in1=fmp[:, 3 * 320:4 * 320], op=ALU.add), (), "v")
        note("qup2", "v")
        yield
        iteration(2)
        yield
        E("v", lambda e: e.tensor_scalar(
            out=fmp[:, 3 * 320:4 * 320], in0=fmp[:, 2 * 320:3 * 320],
            scalar1=Mh[:, 20:21], scalar2=None, op0=ALU.mult),
          [wt("it2_it"), wt("df_cols")], "v")
        E("v", lambda e: e.scalar_tensor_tensor(
            out=fmp[:, 3 * 320:4 * 320], in0=fmp[:, 4 * 320:5 * 320],
            scalar=Mh[:, 21:22], in1=fmp[:, 3 * 320:4 * 320],
            op0=ALU.mult, op1=ALU.add), (), "v")
        note("at_fin", "v")
        E("v", lambda e: e.tensor_copy(out=fmb[:, 320:640],
                                       in_=fmp[:, 3 * 320:4 * 320]),
          [wt("at_fin"), wt("it2_lg")], "v")
        note("at_cast", "v")
        for nh in range(2):
            for dc in range(NDC):
                wl = [wt("awscr_done"), wt("at_cast")]
                if nh or dc:
                    wl.append(wt(f"oj_mm_{nh}_{dc - 2}") if dc >= 2 else
                              wt(f"oj_ev_{nh - 1}") if nh else wt("at_cast"))
                dma(wstg[:, (dc % 2) * D:(dc % 2) * D + D],
                    awscr[3 * D:4 * D, dc * 128:(dc + 1) * 128], wl, "da",
                    tp=True, note_name=f"oj_sl_{nh}_{dc}")
                E("t", lambda e, n_=nh, d=dc: e.matmul(
                    puf[0:16, 0:512],
                    fmb[:, 320 + d * 40 + 2:320 + d * 40 + 18],
                    wstg[:, (d % 2) * D + n_ * 512:(d % 2) * D + (n_ + 1) * 512],
                    start=(d == 0), stop=(d == NDC - 1)),
                  [wt(f"oj_sl_{nh}_{dc}")] + (pu_wait() if (nh == 0 and dc == 0)
                                              else []), "t")
                note(f"oj_mm_{nh}_{dc}", "t")
            E("s", lambda e, n_=nh: e.copy(
                out=at16[0:16, n_ * 512:(n_ + 1) * 512], in_=puf[0:16, 0:512]),
              [wt(f"oj_mm_{nh}_{NDC - 1}")], "s")
            note(f"oj_ev_{nh}", "s")
        pu_set("oj_ev_1")
        note("attn_o", "s")

        E("v", lambda e: e.tensor_scalar(
            out=at16[0:16, :], in0=at16[0:16, :], scalar1=swc[0:16, 4:5],
            scalar2=None, op0=ALU.mult),
          [wt("attn_o"), wt("a0_res"), wt("mcols")], "v")
        E("v", lambda e: e.tensor_tensor(
            out=xA[0][0:16, :], in0=xA[0][0:16, :], in1=at16[0:16, :],
            op=ALU.add), (), "v")
        note("attn_add", "v")
        yield

    attg = ph_attn_gen() if DO_ATTN else None
    order = []
    order += [("A", 0), ("MINI", 0), ("ATT", 1), ("Af", 1), ("ATT", 2),
              ("MA", 0), ("ATT", 3), ("Ar", 1), ("ATT", 4), ("Af", 2),
              ("ATT", 5), ("ATT", 6), ("MA", 1), ("ATT", 7), ("Ar", 2),
              ("MA", 2), ("ATT", 8), ("ATT", 9), ("ATT", 10), ("ATT", 11),
              ("ATT", 12), ("ATT", 13),
              ("B", 0), ("MB", 0), ("B", 1), ("A", 3),
              ("MA", 3), ("MB", 1), ("B", 2), ("A", 4), ("MA", 4), ("MB", 2),
              ("B", 3), ("A", 5), ("MA", 5), ("MB", 3), ("B", 4), ("A", 6),
              ("MA", 6), ("MB", 4), ("B", 5), ("A", 7), ("MA", 7), ("MB", 5),
              ("B", 6), ("MB", 6), ("B", 7), ("MB", 7)]
    for kind, i in order:
        if kind == "A":
            if i >= 3:
                emit_xload(i)
            ph_stage("a", i)
        elif kind == "Af":
            ph_stage("a", i, do_round=False)
        elif kind == "Ar":
            ph_round("a", i)
        elif kind == "B":
            ph_stage("b", i)
        elif kind == "MA":
            ph_mm2("a", i)
        elif kind == "MB":
            ph_mm2("b", i)
        elif kind == "MINI":
            if DO_MINI:
                ph_mini()
        elif kind == "ATT":
            if DO_ATTN:
                next(attg)

    E("t", lambda e: e.wait_ge(sems["do"], R.n("do")) if False else None) \
        if False else None
    R.st["t"].append((None, [(nm, 16 * dma_cnt[nm]) for nm in dma_pool["do"]],
                      None))

    with nc.Block() as block:
        def mk(eng_name):
            def body(eng):
                for fn, ws, inc in R.st[eng_name]:
                    for semn, val in ws:
                        eng.wait_ge(sems[semn], val)
                    if fn is None:
                        continue
                    inst = fn(eng)
                    if inc is not None:
                        semn, amt = inc
                        inst.then_inc(sems[semn], amt)
            return body

        block.tensor(mk("t"))
        block.vector(mk("v"))
        block.scalar(mk("s"))
        block.gpsimd(mk("g"))
        block.sync(mk("y"))

    es.close()
    return nc


_NC_CACHE = []


def kernel(**inputs):
    x = np.ascontiguousarray(np.asarray(inputs["L0_attn_x"], np.float32))
    w1 = np.ascontiguousarray(np.asarray(inputs["L0_ffn_w1"], np.float32))
    w2 = np.ascontiguousarray(np.asarray(inputs["L0_ffn_w2"], np.float32))
    cw = np.asarray(inputs["L0_ffn_conv_w"], np.float32)
    cb = np.asarray(inputs["L0_ffn_conv_b"], np.float32)
    ob = np.ascontiguousarray(np.asarray(inputs["L0_attn_ob"], np.float32))

    cwt = np.zeros((128, 4 * NC), np.float32)
    for k in range(3):
        cwt[:, k * NC:(k + 1) * NC] = cw[:, 0, k].reshape(NC, 128).T
    cwt[:, 3 * NC:] = cb.reshape(NC, 128).T
    qbv = np.zeros((128, 16), np.float32)
    qbv[:, 0:8] = np.asarray(inputs["L0_attn_qb"], np.float32).reshape(8, 128).T
    qbv[:, 8:16] = np.asarray(inputs["L0_attn_vb"], np.float32).reshape(8, 128).T
    lbv = np.zeros((128, 8), np.float32)
    for k, n in enumerate(("L0_attn_lqb", "L0_attn_lkb", "L0_attn_lvb",
                           "L0_attn_lnc_w", "L0_attn_lnd_w")):
        lbv[:, k] = np.tile(np.asarray(inputs[n], np.float32), 2)
    lab = np.zeros((128, 16), np.float32)
    lab[:, 0:8] = np.asarray(inputs["L0_attn_lna_w"], np.float32).reshape(8, 128).T
    lab[:, 8:16] = np.asarray(inputs["L0_attn_lnb_w"], np.float32).reshape(8, 128).T
    identf = np.eye(128, dtype=np.float32)
    identb = np.eye(128, dtype=np.float32).astype(ml_dtypes.bfloat16)
    onesf = np.ones((1, 128), np.float32)

    in_maps = []
    for core in range(8):
        b, q = core // 4, core % 4
        base = 2048 * q
        xs = np.zeros((S_TOK, D), np.float32)
        hi = min(base + S_TOK, 8192)
        xs[0:hi - base] = x[b, base:hi]
        xl = np.zeros((6, D), np.float32)
        lo2, hi2 = base - 4, base + 2
        s0, s1 = max(lo2, 0), min(hi2, 8192)
        xl[s0 - lo2:s1 - lo2] = x[b, s0:s1]
        xr = np.zeros((6, D), np.float32)
        lo2, hi2 = base + 2045, base + 2051
        s0, s1 = max(lo2, 0), min(hi2, 8192)
        xr[s0 - lo2:s1 - lo2] = x[b, s0:s1]
        xhh = np.zeros((40, D), np.float32)
        xhh[2:20] = x[b, 0:18]
        xhh[22:40] = x[1 - b, 0:18]
        maskv = np.zeros((1, 8), np.float32)
        maskv[0, 0] = 0.0 if base == 0 else 1.0
        maskv[0, 1] = 0.0 if base + 2048 >= 8192 else 1.0
        maskv[0, 2] = 1.0 if base == 0 else 0.0
        m = {
            "x_slice": xs, "x_left": xl, "x_right": xr, "x_head": xhh,
            "w1": w1, "w2": w2, "cwt": cwt, "ob": ob,
            "identf": identf, "identb": identb, "onesf": onesf,
            "onescol": np.ones((128, 1), np.float32),
            "maskv": maskv,
            "qw": np.ascontiguousarray(np.asarray(inputs["L0_attn_qw"], np.float32)),
            "kw": np.ascontiguousarray(np.asarray(inputs["L0_attn_kw"], np.float32)),
            "vw": np.ascontiguousarray(np.asarray(inputs["L0_attn_vw"], np.float32)),
            "ow": np.ascontiguousarray(np.asarray(inputs["L0_attn_ow"], np.float32)),
            "qbv": qbv, "lbv": lbv, "lab": lab,
            "lqw": np.ascontiguousarray(np.asarray(inputs["L0_attn_lqw"], np.float32)),
            "lkw": np.ascontiguousarray(np.asarray(inputs["L0_attn_lkw"], np.float32)),
            "lvw": np.ascontiguousarray(np.asarray(inputs["L0_attn_lvw"], np.float32)),
        }
        in_maps.append(m)

    nc = build_program()
    res = run_bass_kernel_spmd(nc, in_maps, core_ids=list(range(8)))
    out = np.zeros((2, 8192, D), np.float32)
    for core in range(8):
        b, q = core // 4, core % 4
        out[b, 2048 * q:2048 * (q + 1)] = res.results[core]["out"]
    return out
